# version 11
# baseline (speedup 1.0000x reference)
import numpy as np

import concourse.bass as bass
import concourse.mybir as mybir
from concourse.bass_utils import run_bass_kernel_spmd

F32 = mybir.dt.float32
F16 = mybir.dt.float16
ALU = mybir.AluOpType
ACTF = mybir.ActivationFunctionType
AXX = mybir.AxisListType.X

NCORES = 8
E_, DK_, R_ = 8, 32, 16
EDK = E_ * DK_
ER = E_ * R_
NIT = 28


class Cfg:
    def __init__(self, D=4096, I=11008, tokc=1024, nt=4):
        self.D = D
        self.I = I
        self.tokc = tokc
        self.nt = nt
        self.hb = nt * 128
        self.nh = tokc // self.hb
        assert self.nh * self.hb == tokc
        self.hc = D // 128
        self.ich = I // 128
        self.gblk = []
        i0 = 0
        while i0 < I:
            w = min(512, I - i0)
            self.gblk.append((i0, w))
            i0 += w
        self.ksel = float(I // 2)
        self.ob = D // 512
        self.qsel = I // 4
        assert self.qsel * 4 == I and self.qsel <= 2752


CFG = Cfg()


def _gu_layout(w, cfg):
    hc = cfg.hc
    O = w.shape[0]
    t = np.ascontiguousarray(w.T).reshape(hc, 128, O).transpose(1, 0, 2)
    blocks = []
    i0 = 0
    while i0 < O:
        wd = min(512, O - i0)
        blocks.append(np.ascontiguousarray(t[:, :, i0:i0 + wd]).reshape(128, hc * wd))
        i0 += wd
    return np.ascontiguousarray(np.concatenate(blocks, axis=1))


def _wd_layout(w_down, cfg):
    D, ich, nob = cfg.D, cfg.ich, cfg.ob
    t = np.ascontiguousarray(w_down.T).reshape(ich, 128, D).transpose(1, 0, 2)
    parts = [np.ascontiguousarray(t[:, :, ob * 512:(ob + 1) * 512]).reshape(128, ich * 512)
             for ob in range(nob)]
    return np.ascontiguousarray(np.concatenate(parts, axis=1))


def host_layouts(x, w_gate, w_up, w_down, W, b, Wq, Wk, A, Bm, cfg):
    x = np.ascontiguousarray(np.asarray(x, np.float32).reshape(-1, cfg.D))
    lay = {
        "wg": _gu_layout(np.asarray(w_gate, np.float32), cfg),
        "wu": _gu_layout(np.asarray(w_up, np.float32), cfg),
        "wd": _wd_layout(np.asarray(w_down, np.float32), cfg),
        "wW": _gu_layout(np.asarray(W, np.float32), cfg),
        "wqk": _gu_layout(np.concatenate([np.asarray(Wq, np.float32),
                                          np.asarray(Wk, np.float32)], 0), cfg),
        "wa": _gu_layout(np.asarray(A, np.float32).reshape(ER, cfg.D), cfg),
        "wbm": np.ascontiguousarray(
            np.asarray(Bm, np.float32).transpose(0, 2, 1).reshape(ER, cfg.D)),
        "b": np.ascontiguousarray(np.asarray(b, np.float32).reshape(1, cfg.D)),
    }
    return x, lay


class SemCtr:
    def __init__(self):
        self.v = {}
        self.w = {}

    def inc(self, ins, sem, n=1):
        self.v[sem] = self.v.get(sem, 0) + n
        ins.then_inc(sem, n)
        return self.v[sem]

    def val(self, sem):
        return self.v.get(sem, 0)

    def wait(self, eng, engname, sem, val):
        if not val or val <= 0:
            return
        key = (engname, sem)
        if self.w.get(key, 0) >= val:
            return
        eng.wait_ge(sem, val)
        self.w[key] = val


def build_nc(cfg=CFG, debug_x1=False):
    D, I, hc, ich, nt, nh = cfg.D, cfg.I, cfg.hc, cfg.ich, cfg.nt, cfg.nh
    nob, gblk = cfg.ob, cfg.gblk
    Q = cfg.qsel
    nc = bass.Bass("TRN2", target_bir_lowering=False)

    x_h = nc.dram_tensor("x", [cfg.tokc, D], F32, kind="ExternalInput").ap()
    wg_h = nc.dram_tensor("wg", [128, hc * I], F32, kind="ExternalInput").ap()
    wu_h = nc.dram_tensor("wu", [128, hc * I], F32, kind="ExternalInput").ap()
    wd_h = nc.dram_tensor("wd", [128, nob * ich * 512], F32, kind="ExternalInput").ap()
    wW_h = nc.dram_tensor("wW", [128, hc * D], F32, kind="ExternalInput").ap()
    wqk_h = nc.dram_tensor("wqk", [128, hc * EDK * 2], F32, kind="ExternalInput").ap()
    wa_h = nc.dram_tensor("wa", [128, hc * ER], F32, kind="ExternalInput").ap()
    wbm_h = nc.dram_tensor("wbm", [128, D], F32, kind="ExternalInput").ap()
    b_h = nc.dram_tensor("b", [1, D], F32, kind="ExternalInput").ap()
    out_h = nc.dram_tensor("out", [cfg.tokc, D], F32, kind="ExternalOutput").ap()
    dbg_h = (nc.dram_tensor("dbg", [cfg.tokc, D], F32, kind="ExternalOutput").ap()
             if debug_x1 else None)
    dbg2_h = (nc.dram_tensor("dbg2", [cfg.tokc, D], F32, kind="ExternalOutput").ap()
              if debug_x1 else None)
    dbg3_h = (nc.dram_tensor("dbg3", [384, max(hc * nt * 128, nt * 512)], F32,
                             kind="ExternalOutput").ap()
              if debug_x1 else None)
    dbg4_h = (nc.dram_tensor("dbg4", [128, 256], F32, kind="ExternalOutput").ap()
              if debug_x1 else None)

    ctxs = []

    def alloc(maker):
        cm = maker
        h = cm.__enter__()
        ctxs.append(cm)
        return h

    C = SemCtr()

    wslot = [alloc(nc.sbuf_tensor(f"wslot{i}", [128, 4096], F16)) for i in range(3)]
    scr16 = alloc(nc.sbuf_tensor("scr16", [128, 8192], F16))
    xtile = alloc(nc.sbuf_tensor("xtile", [128, D], F32))
    stat = alloc(nc.sbuf_tensor("stat", [128, 256], F32))
    ident = alloc(nc.sbuf_tensor("ident", [128, 128], F16))
    idenf = alloc(nc.sbuf_tensor("idenf", [128, 128], F32))
    iota8 = alloc(nc.sbuf_tensor("iota8", [128, 8], F32))
    ones1 = alloc(nc.sbuf_tensor("ones1", [1, 128], F16))
    epsv = alloc(nc.sbuf_tensor("epsv", [128, 1], F32))
    b_sb = alloc(nc.sbuf_tensor("b_sb", [1, D], F16))
    msk_u = alloc(nc.sbuf_tensor("msk_u", [128, 1], mybir.dt.uint32))

    pA = alloc(nc.psum_tensor("pA", [128, 2048], F32))
    pB = alloc(nc.psum_tensor("pB", [128, 1024], F32))
    pT = alloc(nc.psum_tensor("pT", [128, 2048], F16))

    ch_t = alloc(nc.semaphore("ch_t"))
    ch_v = alloc(nc.semaphore("ch_v"))
    ch_s = alloc(nc.semaphore("ch_s"))
    ch_g = alloc(nc.semaphore("ch_g"))
    wsem = [alloc(nc.semaphore(f"wsem{i}")) for i in range(3)]
    xsem = alloc(nc.semaphore("xsem"))
    msem = alloc(nc.semaphore("msem"))
    osem = alloc(nc.semaphore("osem"))

    SC = {"sum": 0, "sq": 1, "lnm": 2, "lnr": 3, "t1": 4, "t2": 5, "t3": 6,
          "m": 8, "negm": 9, "cnt": 10, "upd": 11, "invu": 12, "msum": 13,
          "cidx": 14, "sqt": 15,
          "cq1": 16, "cq2": 20,
          "lo": 24, "hi": 26, "chi": 28, "clo": 30,
          "sg": 234, "frac": 238, "wbr": 239, "d1": 240, "d2": 241, "rr": 242,
          "t8q": 32, "t8g": 64, "thr": 72,
          "asum": 80, "asq": 112,
          "am2": 144, "ar2": 148,
          "neg8": 152, "sexp": 153, "winv": 154,
          "fsum": 160, "fsq": 192,
          "fm": 224, "fr": 228}

    def st(name, off=0):
        c = SC[name] + off
        return stat[:, c:c + 1]

    blk_cm = nc.Block()
    blk = blk_cm.__enter__()
    ctxs.append(blk_cm)

    NSLOT = 3
    slot_fill = [0] * NSLOT
    slot_freetok = [0] * NSLOT
    slot_rr = [0]

    def dma_w(g, src_ap, width):
        sl = slot_rr[0] % NSLOT
        slot_rr[0] += 1
        C.wait(g, "g", ch_t, slot_freetok[sl])
        ins = g.dma_start(out=wslot[sl][:, :width], in_=src_ap)
        slot_fill[sl] += 1
        C.inc(ins, wsem[sl], 16)
        return sl, slot_fill[sl] * 16

    pt_state = {"n": 0, "ev": {}}

    def emit_transposes(items):
        last = None
        for g0 in range(0, len(items), 2):
            grp = items[g0:g0 + 2]
            base = pt_state["n"]

            @blk.tensor
            def _(t, grp=grp, base=base):
                for k, (srcap, dstap, tw) in enumerate(grp):
                    n = base + k
                    j = n % 16
                    off = (j % 2) * 1024 + (j // 2) * 128
                    for sem, valv in tw:
                        C.wait(t, "t", sem, valv)
                    if n >= 2:
                        C.wait(t, "t", ch_s, pt_state["ev"][n - 2])
                    ins = t.transpose(pT[:, off:off + 128], srcap, ident[:])
                    pt_state["ev"][("tr", n)] = C.inc(ins, ch_t, 1)
                pt_state["n"] = base + len(grp)

            @blk.scalar
            def _(s, grp=grp, base=base):
                nonlocal last
                for k, (srcap, dstap, tw) in enumerate(grp):
                    n = base + k
                    j = n % 16
                    off = (j % 2) * 1024 + (j // 2) * 128
                    C.wait(s, "s", ch_t, pt_state["ev"][("tr", n)])
                    ins = s.copy(dstap, pT[:, off:off + 128])
                    tk = C.inc(ins, ch_s, 1)
                    pt_state["ev"][n] = tk
                    last = tk
        return last

    pa_free = [(None, 0)] * 4
    pb_free = [(None, 0)]
    out_dmas = [0]
    xtile_free = [0]
    msem_n = [0]
    init_tok = [0]
    iota_tok = [0]

    def init_phase():
        @blk.gpsimd
        def _(g):
            ins = g.iota(iota8[:], pattern=[[1, 8]], base=0, channel_multiplier=0,
                         allow_small_or_imprecise_dtypes=True)
            iota_tok[0] = C.inc(ins, ch_g, 1)
            ins = g.iota(idenf[:], pattern=[[1, 128]], base=0, channel_multiplier=-1,
                         allow_small_or_imprecise_dtypes=True)
            C.inc(ins, ch_g, 1)
            ins = g.dma_start(out=b_sb[:], in_=b_h[:])
            msem_n[0] += 1
            C.inc(ins, msem, 16)

        @blk.vector
        def _(v):
            v.memset(ones1[:], 1.0)
            v.memset(epsv[:], 1e-5)
            C.wait(v, "v", ch_g, C.val(ch_g))
            ins = v.tensor_scalar(ident[:], idenf[:], 0.0, None, op0=ALU.is_equal)
            init_tok[0] = C.inc(ins, ch_v, 1)

    def emit_ln_tile(tt, src, toks, xlT, key):
        @blk.scalar
        def _(s):
            w = toks.get((key, "srcwait", tt))
            if w:
                C.wait(s, "s", w[0], w[1])
            ins = s.activation(scr16[:, 4096:4096 + D], src, ACTF.Identity,
                               accum_out=st("sum"))
            t0 = C.inc(ins, ch_s, 1)
            C.wait(s, "s", ch_s, t0)
            ins = s.activation(scr16[:, 4096:4096 + D], src, ACTF.Square,
                               accum_out=st("sq"))
            toks[(key, "stats", tt)] = C.inc(ins, ch_s, 1)

        @blk.vector
        def _(v):
            C.wait(v, "v", ch_s, toks[(key, "stats", tt)])
            ins = v.tensor_scalar(st("lnm"), st("sum"), 1.0 / D, None, op0=ALU.mult)
            t0 = C.inc(ins, ch_v, 1)
            C.wait(v, "v", ch_v, t0)
            v.tensor_scalar(st("t1"), st("sq"), 1.0 / D, None, op0=ALU.mult)
            ins = v.tensor_mul(st("t2"), st("lnm"), st("lnm"))
            tv = C.inc(ins, ch_v, 1)
            C.wait(v, "v", ch_v, tv)
            ins = v.tensor_sub(st("t1"), st("t1"), st("t2"))
            toks[(key, "var", tt)] = C.inc(ins, ch_v, 1)

        @blk.scalar
        def _(s):
            C.wait(s, "s", ch_v, toks[(key, "var", tt)])
            C.wait(s, "s", ch_v, init_tok[0])
            ins = s.activation(st("sqt"), st("t1"), ACTF.Sqrt, bias=epsv[:])
            toks[(key, "sqt", tt)] = C.inc(ins, ch_s, 1)

        @blk.vector
        def _(v):
            C.wait(v, "v", ch_s, toks[(key, "sqt", tt)])
            ins = v.reciprocal(st("lnr"), st("sqt"))
            t0 = C.inc(ins, ch_v, 1)
            C.wait(v, "v", ch_v, t0)
            prev = toks.get((key, "tr_done", tt - 1))
            if prev:
                C.wait(v, "v", ch_t, prev)
            ins = v.tensor_scalar(scr16[:, 0:D], src, st("lnm"), st("lnr"),
                                  op0=ALU.subtract, op1=ALU.mult)
            toks[(key, "norm", tt)] = C.inc(ins, ch_v, 1)

        items = []
        for c in range(hc):
            items.append((scr16[:, c * 128:(c + 1) * 128],
                          xlT[:, (c * nt + tt) * 128:(c * nt + tt) * 128 + 128],
                          [(ch_v, toks[(key, "norm", tt)]), (ch_v, init_tok[0])]))
        toks[(key, "evac", tt)] = emit_transposes(items)
        toks[(key, "tr_done", tt)] = C.val(ch_t)

    def emit_ln1(h, xlT):
        toks = {}
        for tt in range(nt):
            @blk.sync
            def _(sy, tt=tt):
                C.wait(sy, "y", ch_v, toks.get(("ln1", "xfree", tt - 1),
                                               xtile_free[0]))
                C.wait(sy, "y", osem, out_dmas[0] * 16)
                rows = h * cfg.hb + tt * 128
                ins = sy.dma_start(out=xtile[:], in_=x_h[rows:rows + 128, :])
                toks[("ln1", "xload", tt)] = C.inc(ins, xsem, 16)

            toks[("ln1", "srcwait", tt)] = (xsem, toks[("ln1", "xload", tt)])
            emit_ln_tile(tt, xtile[:], toks, xlT, "ln1")
            toks[("ln1", "xfree", tt)] = toks[("ln1", "norm", tt)]
        xtile_free[0] = toks[("ln1", "norm", nt - 1)]
        return toks[("ln1", "evac", nt - 1)]

    def emit_ln2(h, x1, x1_toks, xlT2, dn_done):
        toks = {}
        for tt in range(nt):
            toks[("ln2", "srcwait", tt)] = (ch_v, x1_toks[tt])
            emit_ln_tile(tt, x1[:, tt * D:(tt + 1) * D], toks, xlT2, "ln2")
        return toks[("ln2", "evac", nt - 1)]

    def emit_gateup(h, which, xlT, s_t, xlT_tok, thr_toks=None):
        toks = {}
        wsrc = wg_h if which == "gate" else wu_h
        nq = (hc + 7) // 8

        for bi, (i0, wd) in enumerate(gblk):
            nchk = wd // 128

            for q in range(nq):
                @blk.gpsimd
                def _(g, bi=bi, i0=i0, wd=wd, q=q):
                    nch = min((q + 1) * 8, hc) - q * 8
                    src = wsrc[:, hc * i0 + q * 8 * wd: hc * i0 + (q * 8 + nch) * wd]
                    toks[("wdma", bi, q)] = dma_w(g, src, nch * wd)

                @blk.tensor
                def _(t, bi=bi, i0=i0, wd=wd, q=q):
                    C.wait(t, "t", ch_s, xlT_tok)
                    sl, wv = toks[("wdma", bi, q)]
                    C.wait(t, "t", wsem[sl], wv)
                    nch = min((q + 1) * 8, hc) - q * 8
                    for ci in range(nch):
                        c = q * 8 + ci
                        for tt in range(nt):
                            if c == 0:
                                sem, valv = pa_free[tt]
                                if sem is not None:
                                    C.wait(t, "t", sem, valv)
                            ins = t.matmul(
                                pA[:, tt * 512:tt * 512 + wd],
                                xlT[:, (c * nt + tt) * 128:(c * nt + tt) * 128 + 128],
                                wslot[sl][:, ci * wd:(ci + 1) * wd],
                                start=(c == 0), stop=(c == hc - 1),
                                skip_group_check=True)
                            if ci == nch - 1 and tt == nt - 1:
                                tok = C.inc(ins, ch_t, 1)
                                slot_freetok[sl] = tok
                                if q == nq - 1:
                                    toks[("mm", bi)] = tok

            if which == "gate":
                @blk.scalar
                def _(s, bi=bi, i0=i0, wd=wd):
                    C.wait(s, "s", ch_t, toks[("mm", bi)])
                    for tt in range(nt):
                        ins = s.activation(s_t[tt][:, i0:i0 + wd],
                                           pA[:, tt * 512:tt * 512 + wd], ACTF.Silu)
                        tok = C.inc(ins, ch_s, 1)
                        pa_free[tt] = (ch_s, tok)
                        toks[("ev", bi, tt)] = tok
                continue

            @blk.scalar
            def _(s, bi=bi, i0=i0, wd=wd):
                for tt in range(nt):
                    prev = toks.get(("mku", bi - 1, tt))
                    if prev:
                        C.wait(s, "s", ch_v, prev)
                    if thr_toks is not None:
                        C.wait(s, "s", ch_v, thr_toks[tt])
                    ins = s.activation(scr16[:, tt * 512:tt * 512 + wd],
                                       s_t[tt][:, i0:i0 + wd], ACTF.Abs)
                    toks[("abs", bi, tt)] = C.inc(ins, ch_s, 1)

            @blk.vector
            def _(v, bi=bi, i0=i0, wd=wd):
                C.wait(v, "v", ch_t, toks[("mm", bi)])
                for tt in range(nt):
                    C.wait(v, "v", ch_s, toks[("abs", bi, tt)])
                    if thr_toks is not None:
                        C.wait(v, "v", ch_v, thr_toks[tt])
                    prevtr = toks.get(("mtr_last", bi - 1, tt))
                    if prevtr:
                        C.wait(v, "v", ch_t, prevtr)
                    am = scr16[:, tt * 512:tt * 512 + wd]
                    mk = scr16[:, 2048 + tt * 512:2048 + tt * 512 + wd]
                    ins = v.scalar_tensor_tensor(mk, am, st("thr", tt),
                                                 s_t[tt][:, i0:i0 + wd],
                                                 op0=ALU.is_gt, op1=ALU.mult)
                    t0 = C.inc(ins, ch_v, 1)
                    C.wait(v, "v", ch_v, t0)
                    ins = v.tensor_mul(mk, mk, pA[:, tt * 512:tt * 512 + wd])
                    tok = C.inc(ins, ch_v, 1)
                    pa_free[tt] = (ch_v, tok)
                    toks[("mku", bi, tt)] = tok

            items = []
            for tt in range(nt):
                mk = scr16[:, 2048 + tt * 512:2048 + tt * 512 + wd]
                for cc in range(nchk):
                    items.append((mk[:, cc * 128:(cc + 1) * 128],
                                  s_t[tt][:, i0 + cc * 128:i0 + (cc + 1) * 128],
                                  [(ch_v, toks[("mku", bi, tt)])]))
            toks[("mev", bi, nt - 1)] = emit_transposes(items)
            for tt in range(nt):
                toks[("mtr_last", bi, tt)] = C.val(ch_t)
        return toks

    NIT_I = 19

    def emit_select(h, s_t, gate_toks, u_sb):
        thr_tok = [None] * nt
        Qh = Q
        for tt in range(nt):
            toksl = {}

            for qq in range(4):
                @blk.scalar
                def _(s, tt=tt, qq=qq):
                    C.wait(s, "s", ch_s, gate_toks[("ev", len(gblk) - 1, tt)])
                    if tt > 0:
                        C.wait(s, "s", ch_v, thr_tok[tt - 1])
                    ins = s.activation(u_sb[:, qq * Qh:(qq + 1) * Qh],
                                       s_t[tt][:, qq * Qh:(qq + 1) * Qh], ACTF.Abs)
                    toksl[("uabs", qq)] = C.inc(ins, ch_s, 1)

            @blk.vector
            def _(v, tt=tt):
                C.wait(v, "v", ch_s, gate_toks[("ev", len(gblk) - 1, tt)])
                ins = v.reduce_max(st("hi", 0), s_t[tt][:], axis=AXX,
                                   apply_absolute_value=True)
                C.inc(ins, ch_v, 1)
                C.wait(v, "v", ch_s, toksl[("uabs", 3)])
                v.memset(st("lo", 0), 0.0)
                v.memset(st("clo", 0), 0.0)
                ins = v.memset(st("chi", 0), float(I))
                t0 = C.inc(ins, ch_v, 1)
                C.wait(v, "v", ch_v, t0)

            par = 0
            for it in range(NIT_I):
                @blk.vector
                def _(v, tt=tt, it=it, par=par):
                    lo, hi = st("lo", par), st("hi", par)
                    clo, chi = st("clo", par), st("chi", par)
                    ins = v.tensor_sub(st("wbr"), hi, lo)
                    t0 = C.inc(ins, ch_v, 1)
                    if it == 0:
                        C.wait(v, "v", ch_v, t0)
                        ins = v.tensor_scalar(st("t1"), st("wbr"), 1.0 / 64, None,
                                              op0=ALU.mult)
                        t1 = C.inc(ins, ch_v, 1)
                        C.wait(v, "v", ch_v, t1)
                        ins = v.tensor_add(st("m"), lo, st("t1"))
                        tm = C.inc(ins, ch_v, 1)
                        C.wait(v, "v", ch_v, tm)
                    else:
                        v.tensor_sub(st("d1"), chi, clo)
                        ins = v.tensor_scalar(st("d2"), clo, -1.0, cfg.ksel,
                                              op0=ALU.mult, op1=ALU.add)
                        t1 = C.inc(ins, ch_v, 1)
                        C.wait(v, "v", ch_v, t1)
                        ins = v.reciprocal(st("rr"), st("d1"))
                        t2 = C.inc(ins, ch_v, 1)
                        C.wait(v, "v", ch_v, t2)
                        ins = v.tensor_mul(st("frac"), st("d2"), st("rr"))
                        t3 = C.inc(ins, ch_v, 1)
                        C.wait(v, "v", ch_v, t3)
                        ins = v.tensor_scalar_min(st("frac"), st("frac"), 63.0 / 64)
                        t3b = C.inc(ins, ch_v, 1)
                        C.wait(v, "v", ch_v, t3b)
                        ins = v.tensor_scalar_max(st("frac"), st("frac"), 1.0 / 64)
                        t4 = C.inc(ins, ch_v, 1)
                        C.wait(v, "v", ch_v, t4)
                        ins = v.tensor_mul(st("t1"), st("wbr"), st("frac"))
                        t5 = C.inc(ins, ch_v, 1)
                        C.wait(v, "v", ch_v, t5)
                        ins = v.tensor_add(st("m"), lo, st("t1"))
                        tm = C.inc(ins, ch_v, 1)
                        C.wait(v, "v", ch_v, tm)
                    toksl[("m", it)] = C.val(ch_v)
                    dmp = scr16[:, 4096:4096 + Qh]
                    for qq in range(2):
                        ins = v.tensor_scalar(dmp, u_sb[:, qq * Qh:(qq + 1) * Qh],
                                              st("m"), None,
                                              op0=ALU.is_le, op1=ALU.add,
                                              accum_out=st("cq1", qq))
                        t6 = C.inc(ins, ch_v, 1)
                        C.wait(v, "v", ch_v, t6)

                @blk.scalar
                def _(s, tt=tt, it=it):
                    C.wait(s, "s", ch_v, toksl[("m", it)])
                    dmp2 = scr16[:, 0:Qh]
                    for qq in (2, 3):
                        ins = s.activation(dmp2, u_sb[:, qq * Qh:(qq + 1) * Qh],
                                           ACTF.Sign, bias=st("m"), scale=-1.0,
                                           accum_out=st("sg", qq - 2))
                        tk = C.inc(ins, ch_s, 1)
                        C.wait(s, "s", ch_s, tk)
                    toksl[("actc", it)] = C.val(ch_s)

                @blk.vector
                def _(v, tt=tt, it=it, par=par):
                    lo, hi = st("lo", par), st("hi", par)
                    clo, chi = st("clo", par), st("chi", par)
                    nlo, nhi = st("lo", 1 - par), st("hi", 1 - par)
                    nclo, nchi = st("clo", 1 - par), st("chi", 1 - par)
                    C.wait(v, "v", ch_s, toksl[("actc", it)])
                    v.tensor_add(st("t1"), st("cq1", 0), st("cq1", 1))
                    ins = v.tensor_add(st("t2"), st("sg", 0), st("sg", 1))
                    t0 = C.inc(ins, ch_v, 1)
                    C.wait(v, "v", ch_v, t0)
                    ins = v.tensor_scalar(st("t2"), st("t2"), 0.5, float(Qh),
                                          op0=ALU.mult, op1=ALU.add)
                    t1 = C.inc(ins, ch_v, 1)
                    C.wait(v, "v", ch_v, t1)
                    ins = v.tensor_add(st("cnt"), st("t1"), st("t2"))
                    t2 = C.inc(ins, ch_v, 1)
                    C.wait(v, "v", ch_v, t2)
                    v.tensor_scalar(st("upd"), st("cnt"), cfg.ksel, None,
                                    op0=ALU.is_ge)
                    ins = v.tensor_scalar(st("invu"), st("cnt"), cfg.ksel, None,
                                          op0=ALU.is_lt)
                    t3 = C.inc(ins, ch_v, 1)
                    C.wait(v, "v", ch_v, t3)
                    v.tensor_mul(st("t1"), st("upd"), st("m"))
                    v.tensor_mul(st("t2"), st("invu"), hi)
                    v.tensor_mul(st("t3"), st("upd"), lo)
                    ins = v.tensor_mul(st("d1"), st("invu"), st("m"))
                    t4 = C.inc(ins, ch_v, 1)
                    C.wait(v, "v", ch_v, t4)
                    v.tensor_add(nhi, st("t1"), st("t2"))
                    ins = v.tensor_add(nlo, st("t3"), st("d1"))
                    t5 = C.inc(ins, ch_v, 1)
                    C.wait(v, "v", ch_v, t5)
                    v.tensor_mul(st("t1"), st("upd"), st("cnt"))
                    v.tensor_mul(st("t2"), st("invu"), chi)
                    v.tensor_mul(st("t3"), st("upd"), clo)
                    ins = v.tensor_mul(st("d1"), st("invu"), st("cnt"))
                    t6 = C.inc(ins, ch_v, 1)
                    C.wait(v, "v", ch_v, t6)
                    v.tensor_add(nchi, st("t1"), st("t2"))
                    ins = v.tensor_add(nclo, st("t3"), st("d1"))
                    t7 = C.inc(ins, ch_v, 1)
                    C.wait(v, "v", ch_v, t7)
                par = 1 - par

            @blk.vector
            def _(v, tt=tt, par=par):
                hi = st("hi", par)
                dmp = scr16[:, 4096:4096 + Qh]
                for qq in range(4):
                    ins = v.tensor_scalar(dmp, u_sb[:, qq * Qh:(qq + 1) * Qh],
                                          hi, None, op0=ALU.is_le, op1=ALU.add,
                                          accum_out=st("cq1", qq))
                    t0 = C.inc(ins, ch_v, 1)
                    C.wait(v, "v", ch_v, t0)
                ins = v.reduce_sum(st("chi", par), stat[:, SC["cq1"]:SC["cq1"] + 4],
                                   axis=AXX)
                t1 = C.inc(ins, ch_v, 1)
                C.wait(v, "v", ch_v, t1)

            for qq in range(4):
                @blk.vector
                def _(v, tt=tt, qq=qq, par=par):
                    if qq > 0:
                        C.wait(v, "v", ch_v, toksl[("m8", qq - 1)])
                    dmp = scr16[:, 4096:4096 + Qh]
                    uq = u_sb[:, qq * Qh:(qq + 1) * Qh]
                    ins = v.scalar_tensor_tensor(dmp, uq, st("hi", par), uq,
                                                 op0=ALU.is_le, op1=ALU.mult)
                    t0 = C.inc(ins, ch_v, 1)
                    C.wait(v, "v", ch_v, t0)
                    ins = v.max(out=stat[:, SC["t8q"] + qq * 8:SC["t8q"] + qq * 8 + 8],
                                in_=dmp)
                    toksl[("m8", qq)] = C.inc(ins, ch_v, 1)

            @blk.vector
            def _(v, tt=tt, par=par):
                C.wait(v, "v", ch_v, toksl[("m8", 3)])
                ins = v.max(out=stat[:, SC["t8g"]:SC["t8g"] + 8],
                            in_=stat[:, SC["t8q"]:SC["t8q"] + 32])
                t1 = C.inc(ins, ch_v, 1)
                C.wait(v, "v", ch_v, t1)
                ins = v.tensor_scalar(st("cidx"), st("chi", par), cfg.ksel, None,
                                      op0=ALU.subtract)
                tci = C.inc(ins, ch_v, 1)
                C.wait(v, "v", ch_v, tci)
                ins = v.tensor_scalar_min(st("cidx"), st("cidx"), 7.0)
                t2 = C.inc(ins, ch_v, 1)
                C.wait(v, "v", ch_v, t2)
                C.wait(v, "v", ch_g, iota_tok[0])
                ins = v.scalar_tensor_tensor(stat[:, SC["t8q"]:SC["t8q"] + 8],
                                             iota8[:], st("cidx"),
                                             stat[:, SC["t8g"]:SC["t8g"] + 8],
                                             op0=ALU.is_equal, op1=ALU.mult,
                                             accum_out=st("thr", tt))
                thr_tok[tt] = C.inc(ins, ch_v, 1)
        return thr_tok

    def emit_down(h, s_t, up_toks, attnx1):
        toks = {}
        ngrp = (ich + 7) // 8

        for ob in range(nob):
            for gi in range(ngrp):
                @blk.gpsimd
                def _(g, ob=ob, gi=gi):
                    ic0 = gi * 8
                    ncg = min(8, ich - ic0)
                    src = wd_h[:, (ob * ich + ic0) * 512:(ob * ich + ic0 + ncg) * 512]
                    toks[("wdma", ob, gi)] = dma_w(g, src, ncg * 512)

                @blk.tensor
                def _(t, ob=ob, gi=gi):
                    C.wait(t, "t", ch_s, up_toks[("mev", len(gblk) - 1, nt - 1)])
                    sl, wv = toks[("wdma", ob, gi)]
                    C.wait(t, "t", wsem[sl], wv)
                    ic0 = gi * 8
                    ncg = min(8, ich - ic0)
                    for ci in range(ncg):
                        ic = ic0 + ci
                        for tt in range(nt):
                            if ic == 0:
                                sem, valv = pa_free[tt]
                                if sem is not None:
                                    C.wait(t, "t", sem, valv)
                            ins = t.matmul(
                                pA[:, tt * 512:tt * 512 + 512],
                                s_t[tt][:, ic * 128:(ic + 1) * 128],
                                wslot[sl][:, ci * 512:(ci + 1) * 512],
                                start=(ic == 0), stop=(ic == ich - 1),
                                skip_group_check=True)
                            if ci == ncg - 1 and tt == nt - 1:
                                tok = C.inc(ins, ch_t, 1)
                                slot_freetok[sl] = tok
                                if gi == ngrp - 1:
                                    toks[("mm", ob)] = tok

            @blk.scalar
            def _(s, ob=ob):
                C.wait(s, "s", ch_t, toks[("mm", ob)])
                for tt in range(nt):
                    dst = attnx1[:, tt * D + ob * 512:tt * D + (ob + 1) * 512]
                    ins = s.activation(dst, pA[:, tt * 512:tt * 512 + 512],
                                       ACTF.Identity,
                                       accum_out=st("asum", tt * 8 + ob))
                    t0 = C.inc(ins, ch_s, 1)
                    C.wait(s, "s", ch_s, t0)
                    ins = s.activation(scr16[:, 4096:4096 + 512],
                                       pA[:, tt * 512:tt * 512 + 512],
                                       ACTF.Square, accum_out=st("asq", tt * 8 + ob))
                    tok = C.inc(ins, ch_s, 1)
                    pa_free[tt] = (ch_s, tok)
                    toks[("ev", ob, tt)] = tok

        x1_toks = [None] * nt
        for tt in range(nt):
            @blk.sync
            def _(sy, tt=tt):
                C.wait(sy, "y", ch_v, xtile_free[0])
                rows = h * cfg.hb + tt * 128
                ins = sy.dma_start(out=xtile[:], in_=x_h[rows:rows + 128, :])
                toks[("xre", tt)] = C.inc(ins, xsem, 16)

            @blk.vector
            def _(v, tt=tt):
                C.wait(v, "v", ch_s, toks[("ev", nob - 1, tt)])
                v.reduce_sum(st("t1"),
                             stat[:, SC["asum"] + tt * 8:SC["asum"] + tt * 8 + nob],
                             axis=AXX)
                ins = v.reduce_sum(st("t2"),
                                   stat[:, SC["asq"] + tt * 8:SC["asq"] + tt * 8 + nob],
                                   axis=AXX)
                t0 = C.inc(ins, ch_v, 1)
                C.wait(v, "v", ch_v, t0)
                v.tensor_scalar(st("am2", tt), st("t1"), 1.0 / D, None, op0=ALU.mult)
                ins = v.tensor_scalar(st("t2"), st("t2"), 1.0 / D, None, op0=ALU.mult)
                t1 = C.inc(ins, ch_v, 1)
                C.wait(v, "v", ch_v, t1)
                ins = v.tensor_mul(st("t3"), st("am2", tt), st("am2", tt))
                tv = C.inc(ins, ch_v, 1)
                C.wait(v, "v", ch_v, tv)
                ins = v.tensor_sub(st("t2"), st("t2"), st("t3"))
                toks[("var", tt)] = C.inc(ins, ch_v, 1)

            @blk.scalar
            def _(s, tt=tt):
                C.wait(s, "s", ch_v, toks[("var", tt)])
                ins = s.activation(st("t2"), st("t2"), ACTF.Sqrt, bias=epsv[:])
                toks[("sqt", tt)] = C.inc(ins, ch_s, 1)

            @blk.vector
            def _(v, tt=tt):
                C.wait(v, "v", ch_s, toks[("sqt", tt)])
                ins = v.reciprocal(st("ar2", tt), st("t2"))
                t0 = C.inc(ins, ch_v, 1)
                C.wait(v, "v", ch_v, t0)
                sl = attnx1[:, tt * D:(tt + 1) * D]
                ins = v.tensor_scalar(sl, sl, st("am2", tt), st("ar2", tt),
                                      op0=ALU.subtract, op1=ALU.mult)
                t1 = C.inc(ins, ch_v, 1)
                C.wait(v, "v", ch_v, t1)
                C.wait(v, "v", xsem, toks[("xre", tt)])
                ins = v.tensor_add(sl, sl, xtile[:])
                tok = C.inc(ins, ch_v, 1)
                xtile_free[0] = tok
                x1_toks[tt] = tok
        return x1_toks, C.val(ch_t)

    def emit_ffn(h, x1, x1_toks, xlT2, ffn_sb, qk_sb, hg_sb, wbm_sb, dn_done):
        toks = {}
        ln2_tok = emit_ln2(h, x1, x1_toks, xlT2, dn_done)
        nq = (hc + 7) // 8

        @blk.gpsimd
        def _(g):
            C.wait(g, "g", ch_t, dn_done)
            ins = g.dma_start(out=wbm_sb[:], in_=wbm_h[:])
            msem_n[0] += 1
            toks["wbm"] = C.inc(ins, msem, 16)

        nqk = max(1, (hc * EDK * 2) // 4096)
        cpq = 4096 // (EDK * 2)
        for q in range(nqk):
            @blk.gpsimd
            def _(g, q=q):
                w = min(4096, hc * EDK * 2 - q * 4096)
                toks[("qkdma", q)] = dma_w(g, wqk_h[:, q * 4096:q * 4096 + w], w)

            @blk.tensor
            def _(t, q=q):
                C.wait(t, "t", ch_s, ln2_tok)
                sl, wv = toks[("qkdma", q)]
                C.wait(t, "t", wsem[sl], wv)
                for ci in range(cpq):
                    c = q * cpq + ci
                    if c >= hc:
                        break
                    for tt in range(nt):
                        if c == 0:
                            sem, valv = pa_free[tt]
                            if sem is not None:
                                C.wait(t, "t", sem, valv)
                        ins = t.matmul(
                            pA[:, tt * 512:tt * 512 + EDK * 2],
                            xlT2[:, (c * nt + tt) * 128:(c * nt + tt) * 128 + 128],
                            wslot[sl][:, ci * EDK * 2:(ci + 1) * EDK * 2],
                            start=(c == 0), stop=(c == hc - 1),
                            skip_group_check=True)
                        if (ci == cpq - 1 or c == hc - 1) and tt == nt - 1:
                            tok = C.inc(ins, ch_t, 1)
                            slot_freetok[sl] = tok
                            if q == nqk - 1:
                                toks["qkmm"] = tok

        for tt in range(nt):
            @blk.scalar
            def _(s, tt=tt):
                C.wait(s, "s", ch_t, toks["qkmm"])
                ins = s.copy(qk_sb[:, tt * 512:(tt + 1) * 512],
                             pA[:, tt * 512:tt * 512 + 512])
                tok = C.inc(ins, ch_s, 1)
                pa_free[tt] = (ch_s, tok)
                toks[("qkev", tt)] = tok

        na = max(1, (hc * ER) // 4096)
        cpa = min(4096 // ER, hc)
        for q in range(na):
            @blk.gpsimd
            def _(g, q=q):
                w = min(4096, hc * ER - q * 4096)
                toks[("adma", q)] = dma_w(g, wa_h[:, q * 4096:q * 4096 + w], w)

            @blk.tensor
            def _(t, q=q):
                sl, wv = toks[("adma", q)]
                C.wait(t, "t", wsem[sl], wv)
                for ci in range(cpa):
                    c = q * cpa + ci
                    if c >= hc:
                        break
                    for tt in range(nt):
                        if c == 0:
                            sem, valv = pa_free[tt]
                            if sem is not None:
                                C.wait(t, "t", sem, valv)
                        ins = t.matmul(
                            pA[:, tt * 512:tt * 512 + ER],
                            xlT2[:, (c * nt + tt) * 128:(c * nt + tt) * 128 + 128],
                            wslot[sl][:, ci * ER:(ci + 1) * ER],
                            start=(c == 0), stop=(c == hc - 1),
                            skip_group_check=True)
                        if (ci == cpa - 1 or c == hc - 1) and tt == nt - 1:
                            tok = C.inc(ins, ch_t, 1)
                            slot_freetok[sl] = tok
                            if q == na - 1:
                                toks["hmm"] = tok

        gT_toks = [None] * nt
        for tt in range(nt):
            @blk.scalar
            def _(s, tt=tt):
                C.wait(s, "s", ch_t, toks["hmm"])
                ins = s.copy(hg_sb[:, tt * 128:(tt + 1) * 128],
                             pA[:, tt * 512:tt * 512 + ER])
                tok = C.inc(ins, ch_s, 1)
                pa_free[tt] = (ch_s, tok)
                toks[("hev", tt)] = tok

            @blk.vector
            def _(v, tt=tt):
                C.wait(v, "v", ch_s, toks[("qkev", tt)])
                prod = scr16[:, 6848:6848 + EDK]
                ins = v.tensor_mul(prod, qk_sb[:, tt * 512:tt * 512 + EDK],
                                   qk_sb[:, tt * 512 + EDK:tt * 512 + 2 * EDK])
                t0 = C.inc(ins, ch_v, 1)
                C.wait(v, "v", ch_v, t0)
                ins = v.reduce_sum(stat[:, SC["t8g"]:SC["t8g"] + 8],
                                   prod.rearrange("p (e d) -> p e d", e=8), axis=AXX)
                t1 = C.inc(ins, ch_v, 1)
                C.wait(v, "v", ch_v, t1)
                ins = v.tensor_scalar(stat[:, SC["t8g"]:SC["t8g"] + 8],
                                      stat[:, SC["t8g"]:SC["t8g"] + 8],
                                      1.0 / float(np.sqrt(DK_)), None, op0=ALU.mult)
                t2 = C.inc(ins, ch_v, 1)
                C.wait(v, "v", ch_v, t2)
                ins = v.reduce_max(st("neg8"), stat[:, SC["t8g"]:SC["t8g"] + 8],
                                   axis=AXX, negate=True)
                toks[("sneg", tt)] = C.inc(ins, ch_v, 1)

            @blk.scalar
            def _(s, tt=tt):
                C.wait(s, "s", ch_v, toks[("sneg", tt)])
                ins = s.activation(stat[:, SC["t8q"]:SC["t8q"] + 8],
                                   stat[:, SC["t8g"]:SC["t8g"] + 8], ACTF.Exp,
                                   bias=st("neg8"), accum_out=st("sexp"))
                toks[("exp", tt)] = C.inc(ins, ch_s, 1)

            @blk.vector
            def _(v, tt=tt):
                C.wait(v, "v", ch_s, toks[("exp", tt)])
                ins = v.reciprocal(st("winv"), st("sexp"))
                t0 = C.inc(ins, ch_v, 1)
                C.wait(v, "v", ch_v, t0)
                ins = v.tensor_scalar(stat[:, SC["t8q"]:SC["t8q"] + 8],
                                      stat[:, SC["t8q"]:SC["t8q"] + 8],
                                      st("winv"), None, op0=ALU.mult)
                t1 = C.inc(ins, ch_v, 1)
                C.wait(v, "v", ch_v, t1)
                C.wait(v, "v", ch_s, toks[("hev", tt)])
                gdst = hg_sb[:, nt * 128 + tt * 128:nt * 128 + (tt + 1) * 128]
                ins = v.tensor_mul(
                    gdst.rearrange("p (e r) -> p e r", e=8),
                    hg_sb[:, tt * 128:(tt + 1) * 128].rearrange(
                        "p (e r) -> p e r", e=8),
                    stat[:, SC["t8q"]:SC["t8q"] + 8]
                    .rearrange("p (e o) -> p e o", o=1).to_broadcast([128, 8, R_]))
                toks[("g", tt)] = C.inc(ins, ch_v, 1)

            gdst = hg_sb[:, nt * 128 + tt * 128:nt * 128 + (tt + 1) * 128]
            gt = hg_sb[:, 2 * nt * 128 + tt * 128:2 * nt * 128 + (tt + 1) * 128]
            gT_toks[tt] = emit_transposes(
                [(gdst, gt, [(ch_v, toks[("g", tt)])])])

        for ob in range(nob):
            for q in range(nq):
                @blk.gpsimd
                def _(g, ob=ob, q=q):
                    nch = min((q + 1) * 8, hc) - q * 8
                    src = wW_h[:, hc * ob * 512 + q * 8 * 512:
                               hc * ob * 512 + (q * 8 + nch) * 512]
                    toks[("Wdma", ob, q)] = dma_w(g, src, nch * 512)

                @blk.tensor
                def _(t, ob=ob, q=q):
                    C.wait(t, "t", msem, toks["wbm"])
                    C.wait(t, "t", msem, 16)
                    C.wait(t, "t", ch_v, init_tok[0])
                    for tt in range(nt):
                        C.wait(t, "t", ch_s, gT_toks[tt])
                    sl, wv = toks[("Wdma", ob, q)]
                    C.wait(t, "t", wsem[sl], wv)
                    nch = min((q + 1) * 8, hc) - q * 8
                    for ci in range(nch):
                        c = q * 8 + ci
                        for tt in range(nt):
                            if c == 0:
                                sem, valv = pa_free[tt]
                                if sem is not None:
                                    C.wait(t, "t", sem, valv)
                            ins = t.matmul(
                                pA[:, tt * 512:tt * 512 + 512],
                                xlT2[:, (c * nt + tt) * 128:(c * nt + tt) * 128 + 128],
                                wslot[sl][:, ci * 512:(ci + 1) * 512],
                                start=(c == 0), stop=False,
                                skip_group_check=True)
                            if ci == nch - 1 and tt == nt - 1:
                                tok = C.inc(ins, ch_t, 1)
                                slot_freetok[sl] = tok

            @blk.tensor
            def _(t, ob=ob):
                for tt in range(nt):
                    t.matmul(pA[:, tt * 512:tt * 512 + 512], ones1[:],
                             b_sb[:, ob * 512:(ob + 1) * 512],
                             start=False, stop=False, skip_group_check=True)
                    gt = hg_sb[:, 2 * nt * 128 + tt * 128:
                               2 * nt * 128 + (tt + 1) * 128]
                    ins = t.matmul(pA[:, tt * 512:tt * 512 + 512], gt,
                                   wbm_sb[:, ob * 512:(ob + 1) * 512],
                                   start=False, stop=True, skip_group_check=True)
                    if tt == nt - 1:
                        toks[("Wmm", ob)] = C.inc(ins, ch_t, 1)

            @blk.scalar
            def _(s, ob=ob):
                C.wait(s, "s", ch_t, toks[("Wmm", ob)])
                for tt in range(nt):
                    dst = ffn_sb[:, tt * D + ob * 512:tt * D + (ob + 1) * 512]
                    ins = s.activation(dst, pA[:, tt * 512:tt * 512 + 512],
                                       ACTF.Identity,
                                       accum_out=st("fsum", tt * 8 + ob))
                    t0 = C.inc(ins, ch_s, 1)
                    C.wait(s, "s", ch_s, t0)
                    ins = s.activation(scr16[:, 4096:4096 + 512],
                                       pA[:, tt * 512:tt * 512 + 512],
                                       ACTF.Square, accum_out=st("fsq", tt * 8 + ob))
                    tok = C.inc(ins, ch_s, 1)
                    pa_free[tt] = (ch_s, tok)
                    toks[("fev", ob, tt)] = tok

        for tt in range(nt):
            @blk.vector
            def _(v, tt=tt):
                C.wait(v, "v", ch_s, toks[("fev", nob - 1, tt)])
                v.reduce_sum(st("t1"),
                             stat[:, SC["fsum"] + tt * 8:SC["fsum"] + tt * 8 + nob],
                             axis=AXX)
                ins = v.reduce_sum(st("t2"),
                                   stat[:, SC["fsq"] + tt * 8:SC["fsq"] + tt * 8 + nob],
                                   axis=AXX)
                t0 = C.inc(ins, ch_v, 1)
                C.wait(v, "v", ch_v, t0)
                v.tensor_scalar(st("fm", tt), st("t1"), 1.0 / D, None, op0=ALU.mult)
                ins = v.tensor_scalar(st("t2"), st("t2"), 1.0 / D, None, op0=ALU.mult)
                t1 = C.inc(ins, ch_v, 1)
                C.wait(v, "v", ch_v, t1)
                ins = v.tensor_mul(st("t3"), st("fm", tt), st("fm", tt))
                tv = C.inc(ins, ch_v, 1)
                C.wait(v, "v", ch_v, tv)
                ins = v.tensor_sub(st("t2"), st("t2"), st("t3"))
                toks[("fvar", tt)] = C.inc(ins, ch_v, 1)

            @blk.scalar
            def _(s, tt=tt):
                C.wait(s, "s", ch_v, toks[("fvar", tt)])
                ins = s.activation(st("t2"), st("t2"), ACTF.Sqrt, bias=epsv[:])
                toks[("fsqt", tt)] = C.inc(ins, ch_s, 1)

            @blk.vector
            def _(v, tt=tt):
                C.wait(v, "v", ch_s, toks[("fsqt", tt)])
                ins = v.reciprocal(st("fr", tt), st("t2"))
                t0 = C.inc(ins, ch_v, 1)
                C.wait(v, "v", ch_v, t0)
                C.wait(v, "v", ch_v, xtile_free[0])
                if out_dmas[0] > 0:
                    C.wait(v, "v", osem, out_dmas[0] * 16)
                ins = v.tensor_scalar(xtile[:], ffn_sb[:, tt * D:(tt + 1) * D],
                                      st("fm", tt), st("fr", tt),
                                      op0=ALU.subtract, op1=ALU.mult)
                t1 = C.inc(ins, ch_v, 1)
                C.wait(v, "v", ch_v, t1)
                ins = v.tensor_add(xtile[:], xtile[:], x1[:, tt * D:(tt + 1) * D])
                tok = C.inc(ins, ch_v, 1)
                xtile_free[0] = tok
                toks[("out", tt)] = tok

            @blk.sync
            def _(sy, tt=tt):
                C.wait(sy, "y", ch_v, toks[("out", tt)])
                rows = h * cfg.hb + tt * 128
                ins = sy.dma_start(out=out_h[rows:rows + 128, :], in_=xtile[:])
                out_dmas[0] += 1
                C.inc(ins, osem, 16)

    init_phase()

    for h in range(nh):
        if debug_x1 and h > 0:
            for sec, eng_name in ((blk.scalar, "s"), (blk.vector, "v"),
                                  (blk.tensor, "t"), (blk.gpsimd, "g"),
                                  (blk.sync, "y")):
                def _bar(e, eng_name=eng_name):
                    C.wait(e, eng_name, msem, msem_n[0] * 16)
                sec(_bar)
        xlT_cm = nc.sbuf_tensor("xlT%d" % h, [128, hc * nt * 128], F16, side="right")
        xlT = xlT_cm.__enter__()
        s_cm = [nc.sbuf_tensor(f"s{h}_{tt}", [128, I], F16) for tt in range(nt)]
        s_t = [cm.__enter__() for cm in s_cm]

        u_cm = nc.sbuf_tensor("u%d" % h, [128, I], F16, side="right")
        u_sb = u_cm.__enter__()

        ln1_tok = emit_ln1(h, xlT)
        gate_toks = emit_gateup(h, "gate", xlT, s_t, ln1_tok)
        thr_toks = emit_select(h, s_t, gate_toks, u_sb)
        up_toks = emit_gateup(h, "up", xlT, s_t, ln1_tok, thr_toks=thr_toks)

        u_cm.__exit__(None, None, None)
        xlT_cm.__exit__(None, None, None)
        attnx1_cm = nc.sbuf_tensor("ax1_%d" % h, [128, nt * D], F16, side="right")
        attnx1 = attnx1_cm.__enter__()

        x1_toks, dn_done = emit_down(h, s_t, up_toks, attnx1)

        if debug_x1:
            @blk.gpsimd
            def _(g, h=h, attnx1=attnx1, x1_toks=x1_toks):
                if h == 0:
                    ins = g.dma_start(out=dbg4_h[:, 72:72 + nt], in_=stat[:, 72:72 + nt])
                    msem_n[0] += 1
                    C.inc(ins, msem, 16)
                for tt in range(nt):
                    C.wait(g, "g", ch_v, x1_toks[tt])
                    rows = h * cfg.hb + tt * 128
                    ins = g.dma_start(out=dbg_h[rows:rows + 128, :],
                                      in_=attnx1[:, tt * D:(tt + 1) * D])
                    msem_n[0] += 1
                    C.inc(ins, msem, 16)

        for cm in reversed(s_cm):
            cm.__exit__(None, None, None)

        xlT2_cm = nc.sbuf_tensor("xlT2_%d" % h, [128, hc * nt * 128], F16)
        xlT2 = xlT2_cm.__enter__()
        ffn_cm = nc.sbuf_tensor("ffn_%d" % h, [128, nt * D], F16)
        ffn_sb = ffn_cm.__enter__()
        qk_cm = nc.sbuf_tensor("qk_%d" % h, [128, nt * 512], F16)
        qk_sb = qk_cm.__enter__()
        hg_cm = nc.sbuf_tensor("hg_%d" % h, [128, 3 * nt * 128], F16)
        hg_sb = hg_cm.__enter__()
        wbm_cm = nc.sbuf_tensor("wbm_%d" % h, [128, D], F16)
        wbm_sb = wbm_cm.__enter__()

        emit_ffn(h, attnx1, x1_toks, xlT2, ffn_sb, qk_sb, hg_sb, wbm_sb, dn_done)

        if debug_x1:
            @blk.gpsimd
            def _(g, h=h, ffn_sb=ffn_sb, qk_sb=qk_sb, hg_sb=hg_sb, xlT2=xlT2):
                C.wait(g, "g", ch_v, C.val(ch_v))
                C.wait(g, "g", ch_s, C.val(ch_s))
                for tt in range(nt):
                    rows = h * cfg.hb + tt * 128
                    ins = g.dma_start(out=dbg2_h[rows:rows + 128, :],
                                      in_=ffn_sb[:, tt * D:(tt + 1) * D])
                    msem_n[0] += 1
                    C.inc(ins, msem, 16)
                if h == nh - 1:
                    ins = g.dma_start(out=dbg3_h[0:128, 0:nt * 512], in_=qk_sb[:])
                    msem_n[0] += 1
                    C.inc(ins, msem, 16)
                    ins = g.dma_start(out=dbg3_h[128:256, 0:3 * nt * 128],
                                      in_=hg_sb[:])
                    msem_n[0] += 1
                    C.inc(ins, msem, 16)
                    ins = g.dma_start(out=dbg3_h[256:384, 0:hc * nt * 128],
                                      in_=xlT2[:])
                    msem_n[0] += 1
                    C.inc(ins, msem, 16)

        for cm in (wbm_cm, hg_cm, qk_cm, ffn_cm, xlT2_cm):
            cm.__exit__(None, None, None)
        attnx1_cm.__exit__(None, None, None)

    @blk.sync
    def _(sy):
        sy.wait_ge(osem, out_dmas[0] * 16)
        sy.wait_ge(msem, msem_n[0] * 16)

    for cm in reversed(ctxs):
        cm.__exit__(None, None, None)
    return nc


def kernel(L0_attn_x, L0_attn_w_gate, L0_attn_w_up, L0_attn_w_down,
           L0_ffn_W, L0_ffn_b, L0_ffn_Wq, L0_ffn_Wk, L0_ffn_A, L0_ffn_Bm):
    cfg = CFG
    x, lay = host_layouts(L0_attn_x, L0_attn_w_gate, L0_attn_w_up, L0_attn_w_down,
                          L0_ffn_W, L0_ffn_b, L0_ffn_Wq, L0_ffn_Wk,
                          L0_ffn_A, L0_ffn_Bm, cfg)
    nc = build_nc(cfg)
    in_maps = []
    for c in range(NCORES):
        m = {"x": np.ascontiguousarray(x[c * cfg.tokc:(c + 1) * cfg.tokc])}
        m.update(lay)
        in_maps.append(m)
    res = run_bass_kernel_spmd(nc, in_maps, core_ids=list(range(NCORES)))
    out = np.concatenate([np.asarray(res.results[c]["out"]) for c in range(NCORES)], 0)
    return np.ascontiguousarray(out.reshape(2, 4096, 4096).astype(np.float32))


# version 12
# speedup vs baseline: 1.0349x; 1.0349x over previous
import numpy as np

import concourse.bass as bass
import concourse.mybir as mybir
from concourse.bass_utils import run_bass_kernel_spmd

F32 = mybir.dt.float32
F16 = mybir.dt.float16
ALU = mybir.AluOpType
ACTF = mybir.ActivationFunctionType
AXX = mybir.AxisListType.X

NCORES = 8
E_, DK_, R_ = 8, 32, 16
EDK = E_ * DK_
ER = E_ * R_
NIT = 28


class Cfg:
    def __init__(self, D=4096, I=11008, tokc=1024, nt=4):
        self.D = D
        self.I = I
        self.tokc = tokc
        self.nt = nt
        self.hb = nt * 128
        self.nh = tokc // self.hb
        assert self.nh * self.hb == tokc
        self.hc = D // 128
        self.ich = I // 128
        self.gblk = []
        i0 = 0
        while i0 < I:
            w = min(512, I - i0)
            self.gblk.append((i0, w))
            i0 += w
        self.ksel = float(I // 2)
        self.ob = D // 512
        self.qsel = I // 4
        assert self.qsel * 4 == I and self.qsel <= 2752


CFG = Cfg()


def _gu_layout(w, cfg):
    hc = cfg.hc
    O = w.shape[0]
    t = np.ascontiguousarray(w.T).reshape(hc, 128, O).transpose(1, 0, 2)
    blocks = []
    i0 = 0
    while i0 < O:
        wd = min(512, O - i0)
        blocks.append(np.ascontiguousarray(t[:, :, i0:i0 + wd]).reshape(128, hc * wd))
        i0 += wd
    return np.ascontiguousarray(np.concatenate(blocks, axis=1))


def _wd_layout(w_down, cfg):
    D, ich, nob = cfg.D, cfg.ich, cfg.ob
    t = np.ascontiguousarray(w_down.T).reshape(ich, 128, D).transpose(1, 0, 2)
    parts = [np.ascontiguousarray(t[:, :, ob * 512:(ob + 1) * 512]).reshape(128, ich * 512)
             for ob in range(nob)]
    return np.ascontiguousarray(np.concatenate(parts, axis=1))


def host_layouts(x, w_gate, w_up, w_down, W, b, Wq, Wk, A, Bm, cfg):
    x = np.ascontiguousarray(np.asarray(x, np.float32).reshape(-1, cfg.D))
    lay = {
        "wg": _gu_layout(np.asarray(w_gate, np.float32), cfg),
        "wu": _gu_layout(np.asarray(w_up, np.float32), cfg),
        "wd": _wd_layout(np.asarray(w_down, np.float32), cfg),
        "wW": _gu_layout(np.asarray(W, np.float32), cfg),
        "wqk": _gu_layout(np.concatenate([np.asarray(Wq, np.float32),
                                          np.asarray(Wk, np.float32)], 0), cfg),
        "wa": _gu_layout(np.asarray(A, np.float32).reshape(ER, cfg.D), cfg),
        "wbm": np.ascontiguousarray(
            np.asarray(Bm, np.float32).transpose(0, 2, 1).reshape(ER, cfg.D)),
        "b": np.ascontiguousarray(np.asarray(b, np.float32).reshape(1, cfg.D)),
    }
    return x, lay


class SemCtr:
    def __init__(self):
        self.v = {}
        self.w = {}

    def inc(self, ins, sem, n=1):
        self.v[sem] = self.v.get(sem, 0) + n
        ins.then_inc(sem, n)
        return self.v[sem]

    def val(self, sem):
        return self.v.get(sem, 0)

    def wait(self, eng, engname, sem, val):
        if not val or val <= 0:
            return
        key = (engname, sem)
        if self.w.get(key, 0) >= val:
            return
        eng.wait_ge(sem, val)
        self.w[key] = val


def build_nc(cfg=CFG, debug_x1=False):
    D, I, hc, ich, nt, nh = cfg.D, cfg.I, cfg.hc, cfg.ich, cfg.nt, cfg.nh
    nob, gblk = cfg.ob, cfg.gblk
    Q = cfg.qsel
    nc = bass.Bass("TRN2", target_bir_lowering=False)

    x_h = nc.dram_tensor("x", [cfg.tokc, D], F32, kind="ExternalInput").ap()
    wg_h = nc.dram_tensor("wg", [128, hc * I], F32, kind="ExternalInput").ap()
    wu_h = nc.dram_tensor("wu", [128, hc * I], F32, kind="ExternalInput").ap()
    wd_h = nc.dram_tensor("wd", [128, nob * ich * 512], F32, kind="ExternalInput").ap()
    wW_h = nc.dram_tensor("wW", [128, hc * D], F32, kind="ExternalInput").ap()
    wqk_h = nc.dram_tensor("wqk", [128, hc * EDK * 2], F32, kind="ExternalInput").ap()
    wa_h = nc.dram_tensor("wa", [128, hc * ER], F32, kind="ExternalInput").ap()
    wbm_h = nc.dram_tensor("wbm", [128, D], F32, kind="ExternalInput").ap()
    b_h = nc.dram_tensor("b", [1, D], F32, kind="ExternalInput").ap()
    out_h = nc.dram_tensor("out", [cfg.tokc, D], F32, kind="ExternalOutput").ap()
    dbg_h = (nc.dram_tensor("dbg", [cfg.tokc, D], F32, kind="ExternalOutput").ap()
             if debug_x1 else None)
    dbg2_h = (nc.dram_tensor("dbg2", [cfg.tokc, D], F32, kind="ExternalOutput").ap()
              if debug_x1 else None)
    dbg3_h = (nc.dram_tensor("dbg3", [384, max(hc * nt * 128, nt * 512)], F32,
                             kind="ExternalOutput").ap()
              if debug_x1 else None)
    dbg4_h = (nc.dram_tensor("dbg4", [128, 256], F32, kind="ExternalOutput").ap()
              if debug_x1 else None)

    ctxs = []

    def alloc(maker):
        cm = maker
        h = cm.__enter__()
        ctxs.append(cm)
        return h

    C = SemCtr()

    wslot = [alloc(nc.sbuf_tensor(f"wslot{i}", [128, 4096], F16)) for i in range(3)]
    scr16 = alloc(nc.sbuf_tensor("scr16", [128, 8192], F16))
    xtile = alloc(nc.sbuf_tensor("xtile", [128, D], F32))
    stat = alloc(nc.sbuf_tensor("stat", [128, 256], F32))
    ident = alloc(nc.sbuf_tensor("ident", [128, 128], F16))
    idenf = alloc(nc.sbuf_tensor("idenf", [128, 128], F32))
    iota8 = alloc(nc.sbuf_tensor("iota8", [128, 8], F32))
    ones1 = alloc(nc.sbuf_tensor("ones1", [1, 128], F16))
    epsv = alloc(nc.sbuf_tensor("epsv", [128, 1], F32))
    b_sb = alloc(nc.sbuf_tensor("b_sb", [1, D], F16))
    msk_u = alloc(nc.sbuf_tensor("msk_u", [128, 1], mybir.dt.uint32))

    pA = alloc(nc.psum_tensor("pA", [128, 2048], F32))
    pT = alloc(nc.psum_tensor("pT", [128, 4096], F16))

    ch_t = alloc(nc.semaphore("ch_t"))
    ch_v = alloc(nc.semaphore("ch_v"))
    ch_s = alloc(nc.semaphore("ch_s"))
    ch_g = alloc(nc.semaphore("ch_g"))
    wsem = [alloc(nc.semaphore(f"wsem{i}")) for i in range(3)]
    xsem = alloc(nc.semaphore("xsem"))
    msem = alloc(nc.semaphore("msem"))
    osem = alloc(nc.semaphore("osem"))

    SC = {"sum": 0, "sq": 1, "lnm": 2, "lnr": 3, "t1": 4, "t2": 5, "t3": 6,
          "m": 8, "negm": 9, "cnt": 10, "upd": 11, "invu": 12, "msum": 13,
          "cidx": 14, "sqt": 15,
          "cq1": 16, "cq2": 20,
          "lo": 24, "hi": 26, "chi": 28, "clo": 30,
          "sg": 234, "frac": 238, "wbr": 239, "d1": 240, "d2": 241, "rr": 242,
          "t8q": 32, "t8g": 64, "thr": 72,
          "asum": 80, "asq": 112,
          "am2": 144, "ar2": 148,
          "neg8": 152, "sexp": 153, "winv": 154,
          "fsum": 160, "fsq": 192,
          "fm": 224, "fr": 228}

    def st(name, off=0):
        c = SC[name] + off
        return stat[:, c:c + 1]

    blk_cm = nc.Block()
    blk = blk_cm.__enter__()
    ctxs.append(blk_cm)

    NSLOT = 3
    slot_fill = [0] * NSLOT
    slot_freetok = [0] * NSLOT
    slot_rr = [0]

    def dma_w(g, src_ap, width):
        sl = slot_rr[0] % NSLOT
        slot_rr[0] += 1
        C.wait(g, "g", ch_t, slot_freetok[sl])
        ins = g.dma_start(out=wslot[sl][:, :width], in_=src_ap)
        slot_fill[sl] += 1
        C.inc(ins, wsem[sl], 16)
        return sl, slot_fill[sl] * 16

    pt_state = {"n": 0, "ev": {}}

    def emit_transposes(items):
        last = None
        for g0 in range(0, len(items), 2):
            grp = items[g0:g0 + 2]
            base = pt_state["n"]

            @blk.tensor
            def _(t, grp=grp, base=base):
                for k, (srcap, dstap, tw) in enumerate(grp):
                    n = base + k
                    j = n % 32
                    off = (j % 4) * 1024 + (j // 4) * 128
                    for sem, valv in tw:
                        C.wait(t, "t", sem, valv)
                    if n >= 4:
                        C.wait(t, "t", ch_s, pt_state["ev"][n - 4])
                    ins = t.transpose(pT[:, off:off + 128], srcap, ident[:])
                    pt_state["ev"][("tr", n)] = C.inc(ins, ch_t, 1)
                pt_state["n"] = base + len(grp)

            @blk.scalar
            def _(s, grp=grp, base=base):
                nonlocal last
                for k, (srcap, dstap, tw) in enumerate(grp):
                    n = base + k
                    j = n % 32
                    off = (j % 4) * 1024 + (j // 4) * 128
                    C.wait(s, "s", ch_t, pt_state["ev"][("tr", n)])
                    ins = s.copy(dstap, pT[:, off:off + 128])
                    tk = C.inc(ins, ch_s, 1)
                    pt_state["ev"][n] = tk
                    last = tk
        return last

    pa_free = [(None, 0)] * 4
    pb_free = [(None, 0)]
    out_dmas = [0]
    xtile_free = [0]
    msem_n = [0]
    init_tok = [0]
    iota_tok = [0]

    def init_phase():
        @blk.gpsimd
        def _(g):
            ins = g.iota(iota8[:], pattern=[[1, 8]], base=0, channel_multiplier=0,
                         allow_small_or_imprecise_dtypes=True)
            iota_tok[0] = C.inc(ins, ch_g, 1)
            ins = g.iota(idenf[:], pattern=[[1, 128]], base=0, channel_multiplier=-1,
                         allow_small_or_imprecise_dtypes=True)
            C.inc(ins, ch_g, 1)
            ins = g.dma_start(out=b_sb[:], in_=b_h[:])
            msem_n[0] += 1
            C.inc(ins, msem, 16)

        @blk.vector
        def _(v):
            v.memset(ones1[:], 1.0)
            v.memset(epsv[:], 1e-5)
            C.wait(v, "v", ch_g, C.val(ch_g))
            ins = v.tensor_scalar(ident[:], idenf[:], 0.0, None, op0=ALU.is_equal)
            init_tok[0] = C.inc(ins, ch_v, 1)

    def emit_ln_tile(tt, src, toks, xlT, key):
        @blk.scalar
        def _(s):
            w = toks.get((key, "srcwait", tt))
            if w:
                C.wait(s, "s", w[0], w[1])
            ins = s.activation(scr16[:, 4096:4096 + D], src, ACTF.Identity,
                               accum_out=st("sum"))
            t0 = C.inc(ins, ch_s, 1)
            C.wait(s, "s", ch_s, t0)
            ins = s.activation(scr16[:, 4096:4096 + D], src, ACTF.Square,
                               accum_out=st("sq"))
            toks[(key, "stats", tt)] = C.inc(ins, ch_s, 1)

        @blk.vector
        def _(v):
            C.wait(v, "v", ch_s, toks[(key, "stats", tt)])
            ins = v.tensor_scalar(st("lnm"), st("sum"), 1.0 / D, None, op0=ALU.mult)
            t0 = C.inc(ins, ch_v, 1)
            C.wait(v, "v", ch_v, t0)
            v.tensor_scalar(st("t1"), st("sq"), 1.0 / D, None, op0=ALU.mult)
            ins = v.tensor_mul(st("t2"), st("lnm"), st("lnm"))
            tv = C.inc(ins, ch_v, 1)
            C.wait(v, "v", ch_v, tv)
            ins = v.tensor_sub(st("t1"), st("t1"), st("t2"))
            toks[(key, "var", tt)] = C.inc(ins, ch_v, 1)

        @blk.scalar
        def _(s):
            C.wait(s, "s", ch_v, toks[(key, "var", tt)])
            C.wait(s, "s", ch_v, init_tok[0])
            ins = s.activation(st("sqt"), st("t1"), ACTF.Sqrt, bias=epsv[:])
            toks[(key, "sqt", tt)] = C.inc(ins, ch_s, 1)

        @blk.vector
        def _(v):
            C.wait(v, "v", ch_s, toks[(key, "sqt", tt)])
            ins = v.reciprocal(st("lnr"), st("sqt"))
            t0 = C.inc(ins, ch_v, 1)
            C.wait(v, "v", ch_v, t0)
            prev = toks.get((key, "tr_done", tt - 1))
            if prev:
                C.wait(v, "v", ch_t, prev)
            ins = v.tensor_scalar(scr16[:, 0:D], src, st("lnm"), st("lnr"),
                                  op0=ALU.subtract, op1=ALU.mult)
            toks[(key, "norm", tt)] = C.inc(ins, ch_v, 1)

        items = []
        for c in range(hc):
            items.append((scr16[:, c * 128:(c + 1) * 128],
                          xlT[:, (c * nt + tt) * 128:(c * nt + tt) * 128 + 128],
                          [(ch_v, toks[(key, "norm", tt)]), (ch_v, init_tok[0])]))
        toks[(key, "evac", tt)] = emit_transposes(items)
        toks[(key, "tr_done", tt)] = C.val(ch_t)

    def emit_ln1(h, xlT):
        toks = {}
        for tt in range(nt):
            @blk.sync
            def _(sy, tt=tt):
                C.wait(sy, "y", ch_v, toks.get(("ln1", "xfree", tt - 1),
                                               xtile_free[0]))
                C.wait(sy, "y", osem, out_dmas[0] * 16)
                rows = h * cfg.hb + tt * 128
                ins = sy.dma_start(out=xtile[:], in_=x_h[rows:rows + 128, :])
                toks[("ln1", "xload", tt)] = C.inc(ins, xsem, 16)

            toks[("ln1", "srcwait", tt)] = (xsem, toks[("ln1", "xload", tt)])
            emit_ln_tile(tt, xtile[:], toks, xlT, "ln1")
            toks[("ln1", "xfree", tt)] = toks[("ln1", "norm", tt)]
        xtile_free[0] = toks[("ln1", "norm", nt - 1)]
        return toks[("ln1", "evac", nt - 1)]

    def emit_ln2(h, x1, x1_toks, xlT2, dn_done):
        toks = {}
        for tt in range(nt):
            toks[("ln2", "srcwait", tt)] = (ch_v, x1_toks[tt])
            emit_ln_tile(tt, x1[:, tt * D:(tt + 1) * D], toks, xlT2, "ln2")
        return toks[("ln2", "evac", nt - 1)]

    def emit_gateup(h, which, xlT, s_t, xlT_tok, thr_toks=None):
        toks = {}
        wsrc = wg_h if which == "gate" else wu_h
        nq = (hc + 7) // 8

        for bi, (i0, wd) in enumerate(gblk):
            nchk = wd // 128

            for q in range(nq):
                @blk.gpsimd
                def _(g, bi=bi, i0=i0, wd=wd, q=q):
                    nch = min((q + 1) * 8, hc) - q * 8
                    src = wsrc[:, hc * i0 + q * 8 * wd: hc * i0 + (q * 8 + nch) * wd]
                    toks[("wdma", bi, q)] = dma_w(g, src, nch * wd)

                @blk.tensor
                def _(t, bi=bi, i0=i0, wd=wd, q=q):
                    C.wait(t, "t", ch_s, xlT_tok)
                    sl, wv = toks[("wdma", bi, q)]
                    C.wait(t, "t", wsem[sl], wv)
                    nch = min((q + 1) * 8, hc) - q * 8
                    for ci in range(nch):
                        c = q * 8 + ci
                        for tt in range(nt):
                            if c == 0:
                                sem, valv = pa_free[tt]
                                if sem is not None:
                                    C.wait(t, "t", sem, valv)
                            ins = t.matmul(
                                pA[:, tt * 512:tt * 512 + wd],
                                xlT[:, (c * nt + tt) * 128:(c * nt + tt) * 128 + 128],
                                wslot[sl][:, ci * wd:(ci + 1) * wd],
                                start=(c == 0), stop=(c == hc - 1),
                                skip_group_check=True)
                            if ci == nch - 1 and tt == nt - 1:
                                tok = C.inc(ins, ch_t, 1)
                                slot_freetok[sl] = tok
                                if q == nq - 1:
                                    toks[("mm", bi)] = tok

            if which == "gate":
                @blk.scalar
                def _(s, bi=bi, i0=i0, wd=wd):
                    C.wait(s, "s", ch_t, toks[("mm", bi)])
                    for tt in range(nt):
                        ins = s.activation(s_t[tt][:, i0:i0 + wd],
                                           pA[:, tt * 512:tt * 512 + wd], ACTF.Silu)
                        tok = C.inc(ins, ch_s, 1)
                        pa_free[tt] = (ch_s, tok)
                        toks[("ev", bi, tt)] = tok
                continue

            @blk.scalar
            def _(s, bi=bi, i0=i0, wd=wd):
                for tt in range(nt):
                    prev = toks.get(("mku", bi - 1, tt))
                    if prev:
                        C.wait(s, "s", ch_v, prev)
                    if thr_toks is not None:
                        C.wait(s, "s", ch_v, thr_toks[tt])
                    ins = s.activation(scr16[:, tt * 512:tt * 512 + wd],
                                       s_t[tt][:, i0:i0 + wd], ACTF.Abs)
                    toks[("abs", bi, tt)] = C.inc(ins, ch_s, 1)

            @blk.vector
            def _(v, bi=bi, i0=i0, wd=wd):
                C.wait(v, "v", ch_t, toks[("mm", bi)])
                for tt in range(nt):
                    C.wait(v, "v", ch_s, toks[("abs", bi, tt)])
                    if thr_toks is not None:
                        C.wait(v, "v", ch_v, thr_toks[tt])
                    prevtr = toks.get(("mtr_last", bi - 1, tt))
                    if prevtr:
                        C.wait(v, "v", ch_t, prevtr)
                    am = scr16[:, tt * 512:tt * 512 + wd]
                    mk = scr16[:, 2048 + tt * 512:2048 + tt * 512 + wd]
                    ins = v.scalar_tensor_tensor(mk, am, st("thr", tt),
                                                 s_t[tt][:, i0:i0 + wd],
                                                 op0=ALU.is_gt, op1=ALU.mult)
                    t0 = C.inc(ins, ch_v, 1)
                    C.wait(v, "v", ch_v, t0)
                    ins = v.tensor_mul(mk, mk, pA[:, tt * 512:tt * 512 + wd])
                    tok = C.inc(ins, ch_v, 1)
                    pa_free[tt] = (ch_v, tok)
                    toks[("mku", bi, tt)] = tok

            items = []
            for tt in range(nt):
                mk = scr16[:, 2048 + tt * 512:2048 + tt * 512 + wd]
                for cc in range(nchk):
                    items.append((mk[:, cc * 128:(cc + 1) * 128],
                                  s_t[tt][:, i0 + cc * 128:i0 + (cc + 1) * 128],
                                  [(ch_v, toks[("mku", bi, tt)])]))
            toks[("mev", bi, nt - 1)] = emit_transposes(items)
            for tt in range(nt):
                toks[("mtr_last", bi, tt)] = C.val(ch_t)
        return toks

    NIT_I = 20

    def emit_select(h, s_t, gate_toks, u_sb):
        thr_tok = [None] * nt
        Qh = Q
        for tt in range(nt):
            toksl = {}

            for qq in range(4):
                @blk.scalar
                def _(s, tt=tt, qq=qq):
                    C.wait(s, "s", ch_s, gate_toks[("ev", len(gblk) - 1, tt)])
                    if tt > 0:
                        C.wait(s, "s", ch_v, thr_tok[tt - 1])
                    ins = s.activation(u_sb[:, qq * Qh:(qq + 1) * Qh],
                                       s_t[tt][:, qq * Qh:(qq + 1) * Qh], ACTF.Abs)
                    toksl[("uabs", qq)] = C.inc(ins, ch_s, 1)

            @blk.vector
            def _(v, tt=tt):
                C.wait(v, "v", ch_s, toksl[("uabs", 3)])
                ins = v.reduce_max(st("hi", 0), u_sb[:], axis=AXX)
                C.inc(ins, ch_v, 1)
                v.memset(st("lo", 0), 0.0)
                v.memset(st("clo", 0), 0.0)
                ins = v.memset(st("chi", 0), float(I))
                t0 = C.inc(ins, ch_v, 1)
                C.wait(v, "v", ch_v, t0)

            par = 0
            for it in range(NIT_I):
                @blk.vector
                def _(v, tt=tt, it=it, par=par):
                    lo, hi = st("lo", par), st("hi", par)
                    clo, chi = st("clo", par), st("chi", par)
                    ins = v.tensor_sub(st("wbr"), hi, lo)
                    t0 = C.inc(ins, ch_v, 1)
                    if it == 0:
                        C.wait(v, "v", ch_v, t0)
                        ins = v.tensor_scalar(st("t1"), st("wbr"), 1.0 / 64, None,
                                              op0=ALU.mult)
                        t1 = C.inc(ins, ch_v, 1)
                        C.wait(v, "v", ch_v, t1)
                        ins = v.tensor_add(st("m"), lo, st("t1"))
                        tm = C.inc(ins, ch_v, 1)
                        C.wait(v, "v", ch_v, tm)
                    else:
                        v.tensor_sub(st("d1"), chi, clo)
                        ins = v.tensor_scalar(st("d2"), clo, -1.0, cfg.ksel,
                                              op0=ALU.mult, op1=ALU.add)
                        t1 = C.inc(ins, ch_v, 1)
                        C.wait(v, "v", ch_v, t1)
                        ins = v.reciprocal(st("rr"), st("d1"))
                        t2 = C.inc(ins, ch_v, 1)
                        C.wait(v, "v", ch_v, t2)
                        ins = v.tensor_mul(st("frac"), st("d2"), st("rr"))
                        t3 = C.inc(ins, ch_v, 1)
                        C.wait(v, "v", ch_v, t3)
                        ins = v.tensor_scalar_min(st("frac"), st("frac"), 63.0 / 64)
                        t3b = C.inc(ins, ch_v, 1)
                        C.wait(v, "v", ch_v, t3b)
                        ins = v.tensor_scalar_max(st("frac"), st("frac"), 1.0 / 64)
                        t4 = C.inc(ins, ch_v, 1)
                        C.wait(v, "v", ch_v, t4)
                        ins = v.tensor_mul(st("t1"), st("wbr"), st("frac"))
                        t5 = C.inc(ins, ch_v, 1)
                        C.wait(v, "v", ch_v, t5)
                        ins = v.tensor_add(st("m"), lo, st("t1"))
                        tm = C.inc(ins, ch_v, 1)
                        C.wait(v, "v", ch_v, tm)
                    toksl[("m", it)] = C.val(ch_v)
                    dmp = scr16[:, 4096:4096 + Qh]
                    for qq in range(2):
                        ins = v.tensor_scalar(dmp, u_sb[:, qq * Qh:(qq + 1) * Qh],
                                              st("m"), None,
                                              op0=ALU.is_le, op1=ALU.add,
                                              accum_out=st("cq1", qq))
                        t6 = C.inc(ins, ch_v, 1)
                        C.wait(v, "v", ch_v, t6)

                @blk.scalar
                def _(s, tt=tt, it=it):
                    C.wait(s, "s", ch_v, toksl[("m", it)])
                    dmp2 = scr16[:, 0:Qh]
                    for qq in (2, 3):
                        ins = s.activation(dmp2, u_sb[:, qq * Qh:(qq + 1) * Qh],
                                           ACTF.Sign, bias=st("m"), scale=-1.0,
                                           accum_out=st("sg", qq - 2))
                        tk = C.inc(ins, ch_s, 1)
                        C.wait(s, "s", ch_s, tk)
                    toksl[("actc", it)] = C.val(ch_s)

                @blk.vector
                def _(v, tt=tt, it=it, par=par):
                    lo, hi = st("lo", par), st("hi", par)
                    clo, chi = st("clo", par), st("chi", par)
                    nlo, nhi = st("lo", 1 - par), st("hi", 1 - par)
                    nclo, nchi = st("clo", 1 - par), st("chi", 1 - par)
                    C.wait(v, "v", ch_s, toksl[("actc", it)])
                    v.tensor_add(st("t1"), st("cq1", 0), st("cq1", 1))
                    ins = v.tensor_add(st("t2"), st("sg", 0), st("sg", 1))
                    t0 = C.inc(ins, ch_v, 1)
                    C.wait(v, "v", ch_v, t0)
                    ins = v.tensor_scalar(st("t2"), st("t2"), 0.5, float(Qh),
                                          op0=ALU.mult, op1=ALU.add)
                    t1 = C.inc(ins, ch_v, 1)
                    C.wait(v, "v", ch_v, t1)
                    ins = v.tensor_add(st("cnt"), st("t1"), st("t2"))
                    t2 = C.inc(ins, ch_v, 1)
                    C.wait(v, "v", ch_v, t2)
                    v.tensor_scalar(st("upd"), st("cnt"), cfg.ksel, None,
                                    op0=ALU.is_ge)
                    ins = v.tensor_scalar(st("invu"), st("cnt"), cfg.ksel, None,
                                          op0=ALU.is_lt)
                    t3 = C.inc(ins, ch_v, 1)
                    C.wait(v, "v", ch_v, t3)
                    v.tensor_mul(st("t1"), st("upd"), st("m"))
                    v.tensor_mul(st("t2"), st("invu"), hi)
                    v.tensor_mul(st("t3"), st("upd"), lo)
                    ins = v.tensor_mul(st("d1"), st("invu"), st("m"))
                    t4 = C.inc(ins, ch_v, 1)
                    C.wait(v, "v", ch_v, t4)
                    v.tensor_add(nhi, st("t1"), st("t2"))
                    ins = v.tensor_add(nlo, st("t3"), st("d1"))
                    t5 = C.inc(ins, ch_v, 1)
                    C.wait(v, "v", ch_v, t5)
                    v.tensor_mul(st("t1"), st("upd"), st("cnt"))
                    v.tensor_mul(st("t2"), st("invu"), chi)
                    v.tensor_mul(st("t3"), st("upd"), clo)
                    ins = v.tensor_mul(st("d1"), st("invu"), st("cnt"))
                    t6 = C.inc(ins, ch_v, 1)
                    C.wait(v, "v", ch_v, t6)
                    v.tensor_add(nchi, st("t1"), st("t2"))
                    ins = v.tensor_add(nclo, st("t3"), st("d1"))
                    t7 = C.inc(ins, ch_v, 1)
                    C.wait(v, "v", ch_v, t7)
                par = 1 - par

            @blk.vector
            def _(v, tt=tt, par=par):
                hi = st("hi", par)
                dmp = scr16[:, 4096:4096 + Qh]
                for qq in range(4):
                    ins = v.tensor_scalar(dmp, u_sb[:, qq * Qh:(qq + 1) * Qh],
                                          hi, None, op0=ALU.is_le, op1=ALU.add,
                                          accum_out=st("cq1", qq))
                    t0 = C.inc(ins, ch_v, 1)
                    C.wait(v, "v", ch_v, t0)
                ins = v.reduce_sum(st("chi", par), stat[:, SC["cq1"]:SC["cq1"] + 4],
                                   axis=AXX)
                t1 = C.inc(ins, ch_v, 1)
                C.wait(v, "v", ch_v, t1)

            for qq in range(4):
                @blk.vector
                def _(v, tt=tt, qq=qq, par=par):
                    if qq > 0:
                        C.wait(v, "v", ch_v, toksl[("m8", qq - 1)])
                    dmp = scr16[:, 4096:4096 + Qh]
                    uq = u_sb[:, qq * Qh:(qq + 1) * Qh]
                    ins = v.scalar_tensor_tensor(dmp, uq, st("hi", par), uq,
                                                 op0=ALU.is_le, op1=ALU.mult)
                    t0 = C.inc(ins, ch_v, 1)
                    C.wait(v, "v", ch_v, t0)
                    ins = v.max(out=stat[:, SC["t8q"] + qq * 8:SC["t8q"] + qq * 8 + 8],
                                in_=dmp)
                    toksl[("m8", qq)] = C.inc(ins, ch_v, 1)

            @blk.vector
            def _(v, tt=tt, par=par):
                C.wait(v, "v", ch_v, toksl[("m8", 3)])
                ins = v.max(out=stat[:, SC["t8g"]:SC["t8g"] + 8],
                            in_=stat[:, SC["t8q"]:SC["t8q"] + 32])
                t1 = C.inc(ins, ch_v, 1)
                C.wait(v, "v", ch_v, t1)
                ins = v.tensor_scalar(st("cidx"), st("chi", par), cfg.ksel, None,
                                      op0=ALU.subtract)
                tci = C.inc(ins, ch_v, 1)
                C.wait(v, "v", ch_v, tci)
                ins = v.tensor_scalar_min(st("cidx"), st("cidx"), 7.0)
                t2 = C.inc(ins, ch_v, 1)
                C.wait(v, "v", ch_v, t2)
                C.wait(v, "v", ch_g, iota_tok[0])
                ins = v.scalar_tensor_tensor(stat[:, SC["t8q"]:SC["t8q"] + 8],
                                             iota8[:], st("cidx"),
                                             stat[:, SC["t8g"]:SC["t8g"] + 8],
                                             op0=ALU.is_equal, op1=ALU.mult,
                                             accum_out=st("thr", tt))
                thr_tok[tt] = C.inc(ins, ch_v, 1)
        return thr_tok

    def emit_down(h, s_t, up_toks, attnx1):
        toks = {}
        ngrp = (ich + 7) // 8

        for ob in range(nob):
            for gi in range(ngrp):
                @blk.gpsimd
                def _(g, ob=ob, gi=gi):
                    ic0 = gi * 8
                    ncg = min(8, ich - ic0)
                    src = wd_h[:, (ob * ich + ic0) * 512:(ob * ich + ic0 + ncg) * 512]
                    toks[("wdma", ob, gi)] = dma_w(g, src, ncg * 512)

                @blk.tensor
                def _(t, ob=ob, gi=gi):
                    C.wait(t, "t", ch_s, up_toks[("mev", len(gblk) - 1, nt - 1)])
                    sl, wv = toks[("wdma", ob, gi)]
                    C.wait(t, "t", wsem[sl], wv)
                    ic0 = gi * 8
                    ncg = min(8, ich - ic0)
                    for ci in range(ncg):
                        ic = ic0 + ci
                        for tt in range(nt):
                            if ic == 0:
                                sem, valv = pa_free[tt]
                                if sem is not None:
                                    C.wait(t, "t", sem, valv)
                            ins = t.matmul(
                                pA[:, tt * 512:tt * 512 + 512],
                                s_t[tt][:, ic * 128:(ic + 1) * 128],
                                wslot[sl][:, ci * 512:(ci + 1) * 512],
                                start=(ic == 0), stop=(ic == ich - 1),
                                skip_group_check=True)
                            if ci == ncg - 1 and tt == nt - 1:
                                tok = C.inc(ins, ch_t, 1)
                                slot_freetok[sl] = tok
                                if gi == ngrp - 1:
                                    toks[("mm", ob)] = tok

            @blk.scalar
            def _(s, ob=ob):
                C.wait(s, "s", ch_t, toks[("mm", ob)])
                for tt in range(nt):
                    dst = attnx1[:, tt * D + ob * 512:tt * D + (ob + 1) * 512]
                    ins = s.activation(dst, pA[:, tt * 512:tt * 512 + 512],
                                       ACTF.Identity,
                                       accum_out=st("asum", tt * 8 + ob))
                    t0 = C.inc(ins, ch_s, 1)
                    C.wait(s, "s", ch_s, t0)
                    ins = s.activation(scr16[:, 4096:4096 + 512],
                                       pA[:, tt * 512:tt * 512 + 512],
                                       ACTF.Square, accum_out=st("asq", tt * 8 + ob))
                    tok = C.inc(ins, ch_s, 1)
                    pa_free[tt] = (ch_s, tok)
                    toks[("ev", ob, tt)] = tok

        x1_toks = [None] * nt
        for tt in range(nt):
            @blk.sync
            def _(sy, tt=tt):
                C.wait(sy, "y", ch_v, xtile_free[0])
                rows = h * cfg.hb + tt * 128
                ins = sy.dma_start(out=xtile[:], in_=x_h[rows:rows + 128, :])
                toks[("xre", tt)] = C.inc(ins, xsem, 16)

            @blk.vector
            def _(v, tt=tt):
                C.wait(v, "v", ch_s, toks[("ev", nob - 1, tt)])
                v.reduce_sum(st("t1"),
                             stat[:, SC["asum"] + tt * 8:SC["asum"] + tt * 8 + nob],
                             axis=AXX)
                ins = v.reduce_sum(st("t2"),
                                   stat[:, SC["asq"] + tt * 8:SC["asq"] + tt * 8 + nob],
                                   axis=AXX)
                t0 = C.inc(ins, ch_v, 1)
                C.wait(v, "v", ch_v, t0)
                v.tensor_scalar(st("am2", tt), st("t1"), 1.0 / D, None, op0=ALU.mult)
                ins = v.tensor_scalar(st("t2"), st("t2"), 1.0 / D, None, op0=ALU.mult)
                t1 = C.inc(ins, ch_v, 1)
                C.wait(v, "v", ch_v, t1)
                ins = v.tensor_mul(st("t3"), st("am2", tt), st("am2", tt))
                tv = C.inc(ins, ch_v, 1)
                C.wait(v, "v", ch_v, tv)
                ins = v.tensor_sub(st("t2"), st("t2"), st("t3"))
                toks[("var", tt)] = C.inc(ins, ch_v, 1)

            @blk.scalar
            def _(s, tt=tt):
                C.wait(s, "s", ch_v, toks[("var", tt)])
                ins = s.activation(st("t2"), st("t2"), ACTF.Sqrt, bias=epsv[:])
                toks[("sqt", tt)] = C.inc(ins, ch_s, 1)

            @blk.vector
            def _(v, tt=tt):
                C.wait(v, "v", ch_s, toks[("sqt", tt)])
                ins = v.reciprocal(st("ar2", tt), st("t2"))
                t0 = C.inc(ins, ch_v, 1)
                C.wait(v, "v", ch_v, t0)
                sl = attnx1[:, tt * D:(tt + 1) * D]
                ins = v.tensor_scalar(sl, sl, st("am2", tt), st("ar2", tt),
                                      op0=ALU.subtract, op1=ALU.mult)
                t1 = C.inc(ins, ch_v, 1)
                C.wait(v, "v", ch_v, t1)
                C.wait(v, "v", xsem, toks[("xre", tt)])
                ins = v.tensor_add(sl, sl, xtile[:])
                tok = C.inc(ins, ch_v, 1)
                xtile_free[0] = tok
                x1_toks[tt] = tok
        return x1_toks, C.val(ch_t)

    def emit_ffn(h, x1, x1_toks, xlT2, ffn_sb, qk_sb, hg_sb, wbm_sb, dn_done):
        toks = {}
        ln2_tok = emit_ln2(h, x1, x1_toks, xlT2, dn_done)
        nq = (hc + 7) // 8

        @blk.gpsimd
        def _(g):
            C.wait(g, "g", ch_t, dn_done)
            ins = g.dma_start(out=wbm_sb[:], in_=wbm_h[:])
            msem_n[0] += 1
            toks["wbm"] = C.inc(ins, msem, 16)

        nqk = max(1, (hc * EDK * 2) // 4096)
        cpq = 4096 // (EDK * 2)
        for q in range(nqk):
            @blk.gpsimd
            def _(g, q=q):
                w = min(4096, hc * EDK * 2 - q * 4096)
                toks[("qkdma", q)] = dma_w(g, wqk_h[:, q * 4096:q * 4096 + w], w)

            @blk.tensor
            def _(t, q=q):
                C.wait(t, "t", ch_s, ln2_tok)
                sl, wv = toks[("qkdma", q)]
                C.wait(t, "t", wsem[sl], wv)
                for ci in range(cpq):
                    c = q * cpq + ci
                    if c >= hc:
                        break
                    for tt in range(nt):
                        if c == 0:
                            sem, valv = pa_free[tt]
                            if sem is not None:
                                C.wait(t, "t", sem, valv)
                        ins = t.matmul(
                            pA[:, tt * 512:tt * 512 + EDK * 2],
                            xlT2[:, (c * nt + tt) * 128:(c * nt + tt) * 128 + 128],
                            wslot[sl][:, ci * EDK * 2:(ci + 1) * EDK * 2],
                            start=(c == 0), stop=(c == hc - 1),
                            skip_group_check=True)
                        if (ci == cpq - 1 or c == hc - 1) and tt == nt - 1:
                            tok = C.inc(ins, ch_t, 1)
                            slot_freetok[sl] = tok
                            if q == nqk - 1:
                                toks["qkmm"] = tok

        for tt in range(nt):
            @blk.scalar
            def _(s, tt=tt):
                C.wait(s, "s", ch_t, toks["qkmm"])
                ins = s.copy(qk_sb[:, tt * 512:(tt + 1) * 512],
                             pA[:, tt * 512:tt * 512 + 512])
                tok = C.inc(ins, ch_s, 1)
                pa_free[tt] = (ch_s, tok)
                toks[("qkev", tt)] = tok

        na = max(1, (hc * ER) // 4096)
        cpa = min(4096 // ER, hc)
        for q in range(na):
            @blk.gpsimd
            def _(g, q=q):
                w = min(4096, hc * ER - q * 4096)
                toks[("adma", q)] = dma_w(g, wa_h[:, q * 4096:q * 4096 + w], w)

            @blk.tensor
            def _(t, q=q):
                sl, wv = toks[("adma", q)]
                C.wait(t, "t", wsem[sl], wv)
                for ci in range(cpa):
                    c = q * cpa + ci
                    if c >= hc:
                        break
                    for tt in range(nt):
                        if c == 0:
                            sem, valv = pa_free[tt]
                            if sem is not None:
                                C.wait(t, "t", sem, valv)
                        ins = t.matmul(
                            pA[:, tt * 512:tt * 512 + ER],
                            xlT2[:, (c * nt + tt) * 128:(c * nt + tt) * 128 + 128],
                            wslot[sl][:, ci * ER:(ci + 1) * ER],
                            start=(c == 0), stop=(c == hc - 1),
                            skip_group_check=True)
                        if (ci == cpa - 1 or c == hc - 1) and tt == nt - 1:
                            tok = C.inc(ins, ch_t, 1)
                            slot_freetok[sl] = tok
                            if q == na - 1:
                                toks["hmm"] = tok

        gT_toks = [None] * nt
        for tt in range(nt):
            @blk.scalar
            def _(s, tt=tt):
                C.wait(s, "s", ch_t, toks["hmm"])
                ins = s.copy(hg_sb[:, tt * 128:(tt + 1) * 128],
                             pA[:, tt * 512:tt * 512 + ER])
                tok = C.inc(ins, ch_s, 1)
                pa_free[tt] = (ch_s, tok)
                toks[("hev", tt)] = tok

            @blk.vector
            def _(v, tt=tt):
                C.wait(v, "v", ch_s, toks[("qkev", tt)])
                prod = scr16[:, 6848:6848 + EDK]
                ins = v.tensor_mul(prod, qk_sb[:, tt * 512:tt * 512 + EDK],
                                   qk_sb[:, tt * 512 + EDK:tt * 512 + 2 * EDK])
                t0 = C.inc(ins, ch_v, 1)
                C.wait(v, "v", ch_v, t0)
                ins = v.reduce_sum(stat[:, SC["t8g"]:SC["t8g"] + 8],
                                   prod.rearrange("p (e d) -> p e d", e=8), axis=AXX)
                t1 = C.inc(ins, ch_v, 1)
                C.wait(v, "v", ch_v, t1)
                ins = v.tensor_scalar(stat[:, SC["t8g"]:SC["t8g"] + 8],
                                      stat[:, SC["t8g"]:SC["t8g"] + 8],
                                      1.0 / float(np.sqrt(DK_)), None, op0=ALU.mult)
                t2 = C.inc(ins, ch_v, 1)
                C.wait(v, "v", ch_v, t2)
                ins = v.reduce_max(st("neg8"), stat[:, SC["t8g"]:SC["t8g"] + 8],
                                   axis=AXX, negate=True)
                toks[("sneg", tt)] = C.inc(ins, ch_v, 1)

            @blk.scalar
            def _(s, tt=tt):
                C.wait(s, "s", ch_v, toks[("sneg", tt)])
                ins = s.activation(stat[:, SC["t8q"]:SC["t8q"] + 8],
                                   stat[:, SC["t8g"]:SC["t8g"] + 8], ACTF.Exp,
                                   bias=st("neg8"), accum_out=st("sexp"))
                toks[("exp", tt)] = C.inc(ins, ch_s, 1)

            @blk.vector
            def _(v, tt=tt):
                C.wait(v, "v", ch_s, toks[("exp", tt)])
                ins = v.reciprocal(st("winv"), st("sexp"))
                t0 = C.inc(ins, ch_v, 1)
                C.wait(v, "v", ch_v, t0)
                ins = v.tensor_scalar(stat[:, SC["t8q"]:SC["t8q"] + 8],
                                      stat[:, SC["t8q"]:SC["t8q"] + 8],
                                      st("winv"), None, op0=ALU.mult)
                t1 = C.inc(ins, ch_v, 1)
                C.wait(v, "v", ch_v, t1)
                C.wait(v, "v", ch_s, toks[("hev", tt)])
                gdst = hg_sb[:, nt * 128 + tt * 128:nt * 128 + (tt + 1) * 128]
                ins = v.tensor_mul(
                    gdst.rearrange("p (e r) -> p e r", e=8),
                    hg_sb[:, tt * 128:(tt + 1) * 128].rearrange(
                        "p (e r) -> p e r", e=8),
                    stat[:, SC["t8q"]:SC["t8q"] + 8]
                    .rearrange("p (e o) -> p e o", o=1).to_broadcast([128, 8, R_]))
                toks[("g", tt)] = C.inc(ins, ch_v, 1)

            gdst = hg_sb[:, nt * 128 + tt * 128:nt * 128 + (tt + 1) * 128]
            gt = hg_sb[:, 2 * nt * 128 + tt * 128:2 * nt * 128 + (tt + 1) * 128]
            gT_toks[tt] = emit_transposes(
                [(gdst, gt, [(ch_v, toks[("g", tt)])])])

        for ob in range(nob):
            for q in range(nq):
                @blk.gpsimd
                def _(g, ob=ob, q=q):
                    nch = min((q + 1) * 8, hc) - q * 8
                    src = wW_h[:, hc * ob * 512 + q * 8 * 512:
                               hc * ob * 512 + (q * 8 + nch) * 512]
                    toks[("Wdma", ob, q)] = dma_w(g, src, nch * 512)

                @blk.tensor
                def _(t, ob=ob, q=q):
                    C.wait(t, "t", msem, toks["wbm"])
                    C.wait(t, "t", msem, 16)
                    C.wait(t, "t", ch_v, init_tok[0])
                    for tt in range(nt):
                        C.wait(t, "t", ch_s, gT_toks[tt])
                    sl, wv = toks[("Wdma", ob, q)]
                    C.wait(t, "t", wsem[sl], wv)
                    nch = min((q + 1) * 8, hc) - q * 8
                    for ci in range(nch):
                        c = q * 8 + ci
                        for tt in range(nt):
                            if c == 0:
                                sem, valv = pa_free[tt]
                                if sem is not None:
                                    C.wait(t, "t", sem, valv)
                            ins = t.matmul(
                                pA[:, tt * 512:tt * 512 + 512],
                                xlT2[:, (c * nt + tt) * 128:(c * nt + tt) * 128 + 128],
                                wslot[sl][:, ci * 512:(ci + 1) * 512],
                                start=(c == 0), stop=False,
                                skip_group_check=True)
                            if ci == nch - 1 and tt == nt - 1:
                                tok = C.inc(ins, ch_t, 1)
                                slot_freetok[sl] = tok

            @blk.tensor
            def _(t, ob=ob):
                for tt in range(nt):
                    t.matmul(pA[:, tt * 512:tt * 512 + 512], ones1[:],
                             b_sb[:, ob * 512:(ob + 1) * 512],
                             start=False, stop=False, skip_group_check=True)
                    gt = hg_sb[:, 2 * nt * 128 + tt * 128:
                               2 * nt * 128 + (tt + 1) * 128]
                    ins = t.matmul(pA[:, tt * 512:tt * 512 + 512], gt,
                                   wbm_sb[:, ob * 512:(ob + 1) * 512],
                                   start=False, stop=True, skip_group_check=True)
                    if tt == nt - 1:
                        toks[("Wmm", ob)] = C.inc(ins, ch_t, 1)

            @blk.scalar
            def _(s, ob=ob):
                C.wait(s, "s", ch_t, toks[("Wmm", ob)])
                for tt in range(nt):
                    dst = ffn_sb[:, tt * D + ob * 512:tt * D + (ob + 1) * 512]
                    ins = s.activation(dst, pA[:, tt * 512:tt * 512 + 512],
                                       ACTF.Identity,
                                       accum_out=st("fsum", tt * 8 + ob))
                    t0 = C.inc(ins, ch_s, 1)
                    C.wait(s, "s", ch_s, t0)
                    ins = s.activation(scr16[:, 4096:4096 + 512],
                                       pA[:, tt * 512:tt * 512 + 512],
                                       ACTF.Square, accum_out=st("fsq", tt * 8 + ob))
                    tok = C.inc(ins, ch_s, 1)
                    pa_free[tt] = (ch_s, tok)
                    toks[("fev", ob, tt)] = tok

        for tt in range(nt):
            @blk.vector
            def _(v, tt=tt):
                C.wait(v, "v", ch_s, toks[("fev", nob - 1, tt)])
                v.reduce_sum(st("t1"),
                             stat[:, SC["fsum"] + tt * 8:SC["fsum"] + tt * 8 + nob],
                             axis=AXX)
                ins = v.reduce_sum(st("t2"),
                                   stat[:, SC["fsq"] + tt * 8:SC["fsq"] + tt * 8 + nob],
                                   axis=AXX)
                t0 = C.inc(ins, ch_v, 1)
                C.wait(v, "v", ch_v, t0)
                v.tensor_scalar(st("fm", tt), st("t1"), 1.0 / D, None, op0=ALU.mult)
                ins = v.tensor_scalar(st("t2"), st("t2"), 1.0 / D, None, op0=ALU.mult)
                t1 = C.inc(ins, ch_v, 1)
                C.wait(v, "v", ch_v, t1)
                ins = v.tensor_mul(st("t3"), st("fm", tt), st("fm", tt))
                tv = C.inc(ins, ch_v, 1)
                C.wait(v, "v", ch_v, tv)
                ins = v.tensor_sub(st("t2"), st("t2"), st("t3"))
                toks[("fvar", tt)] = C.inc(ins, ch_v, 1)

            @blk.scalar
            def _(s, tt=tt):
                C.wait(s, "s", ch_v, toks[("fvar", tt)])
                ins = s.activation(st("t2"), st("t2"), ACTF.Sqrt, bias=epsv[:])
                toks[("fsqt", tt)] = C.inc(ins, ch_s, 1)

            @blk.vector
            def _(v, tt=tt):
                C.wait(v, "v", ch_s, toks[("fsqt", tt)])
                ins = v.reciprocal(st("fr", tt), st("t2"))
                t0 = C.inc(ins, ch_v, 1)
                C.wait(v, "v", ch_v, t0)
                C.wait(v, "v", ch_v, xtile_free[0])
                if out_dmas[0] > 0:
                    C.wait(v, "v", osem, out_dmas[0] * 16)
                ins = v.tensor_scalar(xtile[:], ffn_sb[:, tt * D:(tt + 1) * D],
                                      st("fm", tt), st("fr", tt),
                                      op0=ALU.subtract, op1=ALU.mult)
                t1 = C.inc(ins, ch_v, 1)
                C.wait(v, "v", ch_v, t1)
                ins = v.tensor_add(xtile[:], xtile[:], x1[:, tt * D:(tt + 1) * D])
                tok = C.inc(ins, ch_v, 1)
                xtile_free[0] = tok
                toks[("out", tt)] = tok

            @blk.sync
            def _(sy, tt=tt):
                C.wait(sy, "y", ch_v, toks[("out", tt)])
                rows = h * cfg.hb + tt * 128
                ins = sy.dma_start(out=out_h[rows:rows + 128, :], in_=xtile[:])
                out_dmas[0] += 1
                C.inc(ins, osem, 16)

    init_phase()

    for h in range(nh):
        if debug_x1 and h > 0:
            for sec, eng_name in ((blk.scalar, "s"), (blk.vector, "v"),
                                  (blk.tensor, "t"), (blk.gpsimd, "g"),
                                  (blk.sync, "y")):
                def _bar(e, eng_name=eng_name):
                    C.wait(e, eng_name, msem, msem_n[0] * 16)
                sec(_bar)
        xlT_cm = nc.sbuf_tensor("xlT%d" % h, [128, hc * nt * 128], F16, side="right")
        xlT = xlT_cm.__enter__()
        s_cm = [nc.sbuf_tensor(f"s{h}_{tt}", [128, I], F16) for tt in range(nt)]
        s_t = [cm.__enter__() for cm in s_cm]

        u_cm = nc.sbuf_tensor("u%d" % h, [128, I], F16, side="right")
        u_sb = u_cm.__enter__()

        ln1_tok = emit_ln1(h, xlT)
        gate_toks = emit_gateup(h, "gate", xlT, s_t, ln1_tok)
        thr_toks = emit_select(h, s_t, gate_toks, u_sb)
        up_toks = emit_gateup(h, "up", xlT, s_t, ln1_tok, thr_toks=thr_toks)

        u_cm.__exit__(None, None, None)
        xlT_cm.__exit__(None, None, None)
        attnx1_cm = nc.sbuf_tensor("ax1_%d" % h, [128, nt * D], F16, side="right")
        attnx1 = attnx1_cm.__enter__()

        x1_toks, dn_done = emit_down(h, s_t, up_toks, attnx1)

        if debug_x1:
            @blk.gpsimd
            def _(g, h=h, attnx1=attnx1, x1_toks=x1_toks):
                if h == 0:
                    ins = g.dma_start(out=dbg4_h[:, 72:72 + nt], in_=stat[:, 72:72 + nt])
                    msem_n[0] += 1
                    C.inc(ins, msem, 16)
                for tt in range(nt):
                    C.wait(g, "g", ch_v, x1_toks[tt])
                    rows = h * cfg.hb + tt * 128
                    ins = g.dma_start(out=dbg_h[rows:rows + 128, :],
                                      in_=attnx1[:, tt * D:(tt + 1) * D])
                    msem_n[0] += 1
                    C.inc(ins, msem, 16)

        for cm in reversed(s_cm):
            cm.__exit__(None, None, None)

        xlT2_cm = nc.sbuf_tensor("xlT2_%d" % h, [128, hc * nt * 128], F16)
        xlT2 = xlT2_cm.__enter__()
        ffn_cm = nc.sbuf_tensor("ffn_%d" % h, [128, nt * D], F16)
        ffn_sb = ffn_cm.__enter__()
        qk_cm = nc.sbuf_tensor("qk_%d" % h, [128, nt * 512], F16)
        qk_sb = qk_cm.__enter__()
        hg_cm = nc.sbuf_tensor("hg_%d" % h, [128, 3 * nt * 128], F16)
        hg_sb = hg_cm.__enter__()
        wbm_cm = nc.sbuf_tensor("wbm_%d" % h, [128, D], F16)
        wbm_sb = wbm_cm.__enter__()

        emit_ffn(h, attnx1, x1_toks, xlT2, ffn_sb, qk_sb, hg_sb, wbm_sb, dn_done)

        if debug_x1:
            @blk.gpsimd
            def _(g, h=h, ffn_sb=ffn_sb, qk_sb=qk_sb, hg_sb=hg_sb, xlT2=xlT2):
                C.wait(g, "g", ch_v, C.val(ch_v))
                C.wait(g, "g", ch_s, C.val(ch_s))
                for tt in range(nt):
                    rows = h * cfg.hb + tt * 128
                    ins = g.dma_start(out=dbg2_h[rows:rows + 128, :],
                                      in_=ffn_sb[:, tt * D:(tt + 1) * D])
                    msem_n[0] += 1
                    C.inc(ins, msem, 16)
                if h == nh - 1:
                    ins = g.dma_start(out=dbg3_h[0:128, 0:nt * 512], in_=qk_sb[:])
                    msem_n[0] += 1
                    C.inc(ins, msem, 16)
                    ins = g.dma_start(out=dbg3_h[128:256, 0:3 * nt * 128],
                                      in_=hg_sb[:])
                    msem_n[0] += 1
                    C.inc(ins, msem, 16)
                    ins = g.dma_start(out=dbg3_h[256:384, 0:hc * nt * 128],
                                      in_=xlT2[:])
                    msem_n[0] += 1
                    C.inc(ins, msem, 16)

        for cm in (wbm_cm, hg_cm, qk_cm, ffn_cm, xlT2_cm):
            cm.__exit__(None, None, None)
        attnx1_cm.__exit__(None, None, None)

    @blk.sync
    def _(sy):
        sy.wait_ge(osem, out_dmas[0] * 16)
        sy.wait_ge(msem, msem_n[0] * 16)

    for cm in reversed(ctxs):
        cm.__exit__(None, None, None)
    return nc


def kernel(L0_attn_x, L0_attn_w_gate, L0_attn_w_up, L0_attn_w_down,
           L0_ffn_W, L0_ffn_b, L0_ffn_Wq, L0_ffn_Wk, L0_ffn_A, L0_ffn_Bm):
    cfg = CFG
    x, lay = host_layouts(L0_attn_x, L0_attn_w_gate, L0_attn_w_up, L0_attn_w_down,
                          L0_ffn_W, L0_ffn_b, L0_ffn_Wq, L0_ffn_Wk,
                          L0_ffn_A, L0_ffn_Bm, cfg)
    nc = build_nc(cfg)
    in_maps = []
    for c in range(NCORES):
        m = {"x": np.ascontiguousarray(x[c * cfg.tokc:(c + 1) * cfg.tokc])}
        m.update(lay)
        in_maps.append(m)
    res = run_bass_kernel_spmd(nc, in_maps, core_ids=list(range(NCORES)))
    out = np.concatenate([np.asarray(res.results[c]["out"]) for c in range(NCORES)], 0)
    return np.ascontiguousarray(out.reshape(2, 4096, 4096).astype(np.float32))


# version 13 (final)
# speedup vs baseline: 1.0582x; 1.0225x over previous
import numpy as np

import concourse.bass as bass
import concourse.mybir as mybir
from concourse.bass_utils import run_bass_kernel_spmd

F32 = mybir.dt.float32
F16 = mybir.dt.float16
ALU = mybir.AluOpType
ACTF = mybir.ActivationFunctionType
AXX = mybir.AxisListType.X

NCORES = 8
E_, DK_, R_ = 8, 32, 16
EDK = E_ * DK_
ER = E_ * R_
NIT = 28


class Cfg:
    def __init__(self, D=4096, I=11008, tokc=1024, nt=4):
        self.D = D
        self.I = I
        self.tokc = tokc
        self.nt = nt
        self.hb = nt * 128
        self.nh = tokc // self.hb
        assert self.nh * self.hb == tokc
        self.hc = D // 128
        self.ich = I // 128
        self.gblk = []
        i0 = 0
        while i0 < I:
            w = min(512, I - i0)
            self.gblk.append((i0, w))
            i0 += w
        self.ksel = float(I // 2)
        self.ob = D // 512
        self.qsel = I // 4
        assert self.qsel * 4 == I and self.qsel <= 2752


CFG = Cfg()


def _gu_layout(w, cfg):
    hc = cfg.hc
    O = w.shape[0]
    t = np.ascontiguousarray(w.T).reshape(hc, 128, O).transpose(1, 0, 2)
    blocks = []
    i0 = 0
    while i0 < O:
        wd = min(512, O - i0)
        blocks.append(np.ascontiguousarray(t[:, :, i0:i0 + wd]).reshape(128, hc * wd))
        i0 += wd
    return np.ascontiguousarray(np.concatenate(blocks, axis=1))


def _wd_layout(w_down, cfg):
    D, ich, nob = cfg.D, cfg.ich, cfg.ob
    t = np.ascontiguousarray(w_down.T).reshape(ich, 128, D).transpose(1, 0, 2)
    parts = [np.ascontiguousarray(t[:, :, ob * 512:(ob + 1) * 512]).reshape(128, ich * 512)
             for ob in range(nob)]
    return np.ascontiguousarray(np.concatenate(parts, axis=1))


def host_layouts(x, w_gate, w_up, w_down, W, b, Wq, Wk, A, Bm, cfg):
    x = np.ascontiguousarray(np.asarray(x, np.float32).reshape(-1, cfg.D))
    lay = {
        "wg": _gu_layout(np.asarray(w_gate, np.float32), cfg),
        "wu": _gu_layout(np.asarray(w_up, np.float32), cfg),
        "wd": _wd_layout(np.asarray(w_down, np.float32), cfg),
        "wW": _gu_layout(np.asarray(W, np.float32), cfg),
        "wqk": _gu_layout(np.concatenate([np.asarray(Wq, np.float32),
                                          np.asarray(Wk, np.float32)], 0), cfg),
        "wa": _gu_layout(np.asarray(A, np.float32).reshape(ER, cfg.D), cfg),
        "wbm": np.ascontiguousarray(
            np.asarray(Bm, np.float32).transpose(0, 2, 1).reshape(ER, cfg.D)),
        "b": np.ascontiguousarray(np.asarray(b, np.float32).reshape(1, cfg.D)),
    }
    return x, lay


class SemCtr:
    def __init__(self):
        self.v = {}
        self.w = {}

    def inc(self, ins, sem, n=1):
        self.v[sem] = self.v.get(sem, 0) + n
        ins.then_inc(sem, n)
        return self.v[sem]

    def val(self, sem):
        return self.v.get(sem, 0)

    def wait(self, eng, engname, sem, val):
        if not val or val <= 0:
            return
        key = (engname, sem)
        if self.w.get(key, 0) >= val:
            return
        eng.wait_ge(sem, val)
        self.w[key] = val


def build_nc(cfg=CFG, debug_x1=False):
    D, I, hc, ich, nt, nh = cfg.D, cfg.I, cfg.hc, cfg.ich, cfg.nt, cfg.nh
    nob, gblk = cfg.ob, cfg.gblk
    Q = cfg.qsel
    nc = bass.Bass("TRN2", target_bir_lowering=False)

    x_h = nc.dram_tensor("x", [cfg.tokc, D], F32, kind="ExternalInput").ap()
    wg_h = nc.dram_tensor("wg", [128, hc * I], F32, kind="ExternalInput").ap()
    wu_h = nc.dram_tensor("wu", [128, hc * I], F32, kind="ExternalInput").ap()
    wd_h = nc.dram_tensor("wd", [128, nob * ich * 512], F32, kind="ExternalInput").ap()
    wW_h = nc.dram_tensor("wW", [128, hc * D], F32, kind="ExternalInput").ap()
    wqk_h = nc.dram_tensor("wqk", [128, hc * EDK * 2], F32, kind="ExternalInput").ap()
    wa_h = nc.dram_tensor("wa", [128, hc * ER], F32, kind="ExternalInput").ap()
    wbm_h = nc.dram_tensor("wbm", [128, D], F32, kind="ExternalInput").ap()
    b_h = nc.dram_tensor("b", [1, D], F32, kind="ExternalInput").ap()
    out_h = nc.dram_tensor("out", [cfg.tokc, D], F32, kind="ExternalOutput").ap()
    dbg_h = (nc.dram_tensor("dbg", [cfg.tokc, D], F32, kind="ExternalOutput").ap()
             if debug_x1 else None)
    dbg2_h = (nc.dram_tensor("dbg2", [cfg.tokc, D], F32, kind="ExternalOutput").ap()
              if debug_x1 else None)
    dbg3_h = (nc.dram_tensor("dbg3", [384, max(hc * nt * 128, nt * 512)], F32,
                             kind="ExternalOutput").ap()
              if debug_x1 else None)
    dbg4_h = (nc.dram_tensor("dbg4", [128, 256], F32, kind="ExternalOutput").ap()
              if debug_x1 else None)

    ctxs = []

    def alloc(maker):
        cm = maker
        h = cm.__enter__()
        ctxs.append(cm)
        return h

    C = SemCtr()

    wslot = [alloc(nc.sbuf_tensor(f"wslot{i}", [128, 4096], F16)) for i in range(3)]
    scr16 = alloc(nc.sbuf_tensor("scr16", [128, 8192], F16))
    xtile = alloc(nc.sbuf_tensor("xtile", [128, D], F32))
    stat = alloc(nc.sbuf_tensor("stat", [128, 256], F32))
    ident = alloc(nc.sbuf_tensor("ident", [128, 128], F16))
    idenf = alloc(nc.sbuf_tensor("idenf", [128, 128], F32))
    iota8 = alloc(nc.sbuf_tensor("iota8", [128, 8], F32))
    ones1 = alloc(nc.sbuf_tensor("ones1", [1, 128], F16))
    epsv = alloc(nc.sbuf_tensor("epsv", [128, 1], F32))
    b_sb = alloc(nc.sbuf_tensor("b_sb", [1, D], F16))
    msk_u = alloc(nc.sbuf_tensor("msk_u", [128, 1], mybir.dt.uint32))

    pA = alloc(nc.psum_tensor("pA", [128, 2048], F32))
    pT = alloc(nc.psum_tensor("pT", [128, 4096], F16))

    ch_t = alloc(nc.semaphore("ch_t"))
    ch_v = alloc(nc.semaphore("ch_v"))
    ch_s = alloc(nc.semaphore("ch_s"))
    ch_g = alloc(nc.semaphore("ch_g"))
    wsem = [alloc(nc.semaphore(f"wsem{i}")) for i in range(3)]
    xsem = alloc(nc.semaphore("xsem"))
    msem = alloc(nc.semaphore("msem"))
    osem = alloc(nc.semaphore("osem"))

    SC = {"sum": 0, "sq": 1, "lnm": 2, "lnr": 3, "t1": 4, "t2": 5, "t3": 6,
          "m": 8, "negm": 9, "cnt": 10, "upd": 11, "invu": 12, "msum": 13,
          "cidx": 14, "sqt": 15,
          "cq1": 16, "cq2": 20,
          "lo": 24, "hi": 26, "chi": 28, "clo": 30,
          "sg": 234, "frac": 238, "wbr": 239, "d1": 240, "d2": 241, "rr": 242,
          "t8q": 32, "t8g": 64, "thr": 72,
          "asum": 80, "asq": 112,
          "am2": 144, "ar2": 148,
          "neg8": 152, "sexp": 153, "winv": 154,
          "fsum": 160, "fsq": 192,
          "fm": 224, "fr": 228}

    def st(name, off=0):
        c = SC[name] + off
        return stat[:, c:c + 1]

    blk_cm = nc.Block()
    blk = blk_cm.__enter__()
    ctxs.append(blk_cm)

    NSLOT = 3
    slot_fill = [0] * NSLOT
    slot_freetok = [0] * NSLOT
    slot_rr = [0]

    def dma_w(g, src_ap, width):
        sl = slot_rr[0] % NSLOT
        slot_rr[0] += 1
        C.wait(g, "g", ch_t, slot_freetok[sl])
        ins = g.dma_start(out=wslot[sl][:, :width], in_=src_ap)
        slot_fill[sl] += 1
        C.inc(ins, wsem[sl], 16)
        return sl, slot_fill[sl] * 16

    pt_state = {"n": 0, "ev": {}}

    def emit_transposes(items):
        last = None
        for g0 in range(0, len(items), 4):
            grp = items[g0:g0 + 4]
            base = pt_state["n"]

            @blk.tensor
            def _(t, grp=grp, base=base):
                for k, (srcap, dstap, tw) in enumerate(grp):
                    n = base + k
                    j = n % 32
                    off = (j % 4) * 1024 + (j // 4) * 128
                    for sem, valv in tw:
                        C.wait(t, "t", sem, valv)
                    if n >= 4:
                        C.wait(t, "t", ch_s, pt_state["ev"][n - 4])
                    ins = t.transpose(pT[:, off:off + 128], srcap, ident[:])
                    pt_state["ev"][("tr", n)] = C.inc(ins, ch_t, 1)
                pt_state["n"] = base + len(grp)

            @blk.scalar
            def _(s, grp=grp, base=base):
                nonlocal last
                for k, (srcap, dstap, tw) in enumerate(grp):
                    n = base + k
                    j = n % 32
                    off = (j % 4) * 1024 + (j // 4) * 128
                    C.wait(s, "s", ch_t, pt_state["ev"][("tr", n)])
                    ins = s.copy(dstap, pT[:, off:off + 128])
                    tk = C.inc(ins, ch_s, 1)
                    pt_state["ev"][n] = tk
                    last = tk
        return last

    pa_free = [(None, 0)] * 4
    pb_free = [(None, 0)]
    out_dmas = [0]
    xtile_free = [0]
    msem_n = [0]
    init_tok = [0]
    iota_tok = [0]

    def init_phase():
        @blk.gpsimd
        def _(g):
            ins = g.iota(iota8[:], pattern=[[1, 8]], base=0, channel_multiplier=0,
                         allow_small_or_imprecise_dtypes=True)
            iota_tok[0] = C.inc(ins, ch_g, 1)
            ins = g.iota(idenf[:], pattern=[[1, 128]], base=0, channel_multiplier=-1,
                         allow_small_or_imprecise_dtypes=True)
            C.inc(ins, ch_g, 1)
            ins = g.dma_start(out=b_sb[:], in_=b_h[:])
            msem_n[0] += 1
            C.inc(ins, msem, 16)

        @blk.vector
        def _(v):
            v.memset(ones1[:], 1.0)
            v.memset(epsv[:], 1e-5)
            C.wait(v, "v", ch_g, C.val(ch_g))
            ins = v.tensor_scalar(ident[:], idenf[:], 0.0, None, op0=ALU.is_equal)
            init_tok[0] = C.inc(ins, ch_v, 1)

    def emit_ln_tile(tt, src, toks, xlT, key):
        @blk.scalar
        def _(s):
            w = toks.get((key, "srcwait", tt))
            if w:
                C.wait(s, "s", w[0], w[1])
            ins = s.activation(scr16[:, 4096:4096 + D], src, ACTF.Identity,
                               accum_out=st("sum"))
            t0 = C.inc(ins, ch_s, 1)
            C.wait(s, "s", ch_s, t0)
            ins = s.activation(scr16[:, 4096:4096 + D], src, ACTF.Square,
                               accum_out=st("sq"))
            toks[(key, "stats", tt)] = C.inc(ins, ch_s, 1)

        @blk.vector
        def _(v):
            C.wait(v, "v", ch_s, toks[(key, "stats", tt)])
            ins = v.tensor_scalar(st("lnm"), st("sum"), 1.0 / D, None, op0=ALU.mult)
            t0 = C.inc(ins, ch_v, 1)
            C.wait(v, "v", ch_v, t0)
            v.tensor_scalar(st("t1"), st("sq"), 1.0 / D, None, op0=ALU.mult)
            ins = v.tensor_mul(st("t2"), st("lnm"), st("lnm"))
            tv = C.inc(ins, ch_v, 1)
            C.wait(v, "v", ch_v, tv)
            ins = v.tensor_sub(st("t1"), st("t1"), st("t2"))
            toks[(key, "var", tt)] = C.inc(ins, ch_v, 1)

        @blk.scalar
        def _(s):
            C.wait(s, "s", ch_v, toks[(key, "var", tt)])
            C.wait(s, "s", ch_v, init_tok[0])
            ins = s.activation(st("sqt"), st("t1"), ACTF.Sqrt, bias=epsv[:])
            toks[(key, "sqt", tt)] = C.inc(ins, ch_s, 1)

        @blk.vector
        def _(v):
            C.wait(v, "v", ch_s, toks[(key, "sqt", tt)])
            ins = v.reciprocal(st("lnr"), st("sqt"))
            t0 = C.inc(ins, ch_v, 1)
            C.wait(v, "v", ch_v, t0)
            prev = toks.get((key, "tr_done", tt - 1))
            if prev:
                C.wait(v, "v", ch_t, prev)
            ins = v.tensor_scalar(scr16[:, 0:D], src, st("lnm"), st("lnr"),
                                  op0=ALU.subtract, op1=ALU.mult)
            toks[(key, "norm", tt)] = C.inc(ins, ch_v, 1)

        items = []
        for c in range(hc):
            items.append((scr16[:, c * 128:(c + 1) * 128],
                          xlT[:, (c * nt + tt) * 128:(c * nt + tt) * 128 + 128],
                          [(ch_v, toks[(key, "norm", tt)]), (ch_v, init_tok[0])]))
        toks[(key, "evac", tt)] = emit_transposes(items)
        toks[(key, "tr_done", tt)] = C.val(ch_t)

    def emit_ln1(h, xlT):
        toks = {}
        for tt in range(nt):
            @blk.sync
            def _(sy, tt=tt):
                C.wait(sy, "y", ch_v, toks.get(("ln1", "xfree", tt - 1),
                                               xtile_free[0]))
                C.wait(sy, "y", osem, out_dmas[0] * 16)
                rows = h * cfg.hb + tt * 128
                ins = sy.dma_start(out=xtile[:], in_=x_h[rows:rows + 128, :])
                toks[("ln1", "xload", tt)] = C.inc(ins, xsem, 16)

            toks[("ln1", "srcwait", tt)] = (xsem, toks[("ln1", "xload", tt)])
            emit_ln_tile(tt, xtile[:], toks, xlT, "ln1")
            toks[("ln1", "xfree", tt)] = toks[("ln1", "norm", tt)]
        xtile_free[0] = toks[("ln1", "norm", nt - 1)]
        return toks[("ln1", "evac", nt - 1)]

    def emit_ln2(h, x1, x1_toks, xlT2, dn_done):
        toks = {}
        for tt in range(nt):
            toks[("ln2", "srcwait", tt)] = (ch_v, x1_toks[tt])
            emit_ln_tile(tt, x1[:, tt * D:(tt + 1) * D], toks, xlT2, "ln2")
        return toks[("ln2", "evac", nt - 1)]

    def emit_gateup(h, which, xlT, s_t, xlT_tok, thr_toks=None):
        toks = {}
        wsrc = wg_h if which == "gate" else wu_h
        nq = (hc + 7) // 8

        for bi, (i0, wd) in enumerate(gblk):
            nchk = wd // 128

            for q in range(nq):
                @blk.gpsimd
                def _(g, bi=bi, i0=i0, wd=wd, q=q):
                    nch = min((q + 1) * 8, hc) - q * 8
                    src = wsrc[:, hc * i0 + q * 8 * wd: hc * i0 + (q * 8 + nch) * wd]
                    toks[("wdma", bi, q)] = dma_w(g, src, nch * wd)

                @blk.tensor
                def _(t, bi=bi, i0=i0, wd=wd, q=q):
                    C.wait(t, "t", ch_s, xlT_tok)
                    sl, wv = toks[("wdma", bi, q)]
                    C.wait(t, "t", wsem[sl], wv)
                    nch = min((q + 1) * 8, hc) - q * 8
                    for ci in range(nch):
                        c = q * 8 + ci
                        for tt in range(nt):
                            if c == 0:
                                sem, valv = pa_free[tt]
                                if sem is not None:
                                    C.wait(t, "t", sem, valv)
                            ins = t.matmul(
                                pA[:, tt * 512:tt * 512 + wd],
                                xlT[:, (c * nt + tt) * 128:(c * nt + tt) * 128 + 128],
                                wslot[sl][:, ci * wd:(ci + 1) * wd],
                                start=(c == 0), stop=(c == hc - 1),
                                skip_group_check=True)
                            if ci == nch - 1 and tt == nt - 1:
                                tok = C.inc(ins, ch_t, 1)
                                slot_freetok[sl] = tok
                                if q == nq - 1:
                                    toks[("mm", bi)] = tok

            if which == "gate":
                @blk.scalar
                def _(s, bi=bi, i0=i0, wd=wd):
                    C.wait(s, "s", ch_t, toks[("mm", bi)])
                    for tt in range(nt):
                        ins = s.activation(s_t[tt][:, i0:i0 + wd],
                                           pA[:, tt * 512:tt * 512 + wd], ACTF.Silu)
                        tok = C.inc(ins, ch_s, 1)
                        pa_free[tt] = (ch_s, tok)
                        toks[("ev", bi, tt)] = tok
                continue

            @blk.scalar
            def _(s, bi=bi, i0=i0, wd=wd):
                for tt in range(nt):
                    prev = toks.get(("mku", bi - 1, tt))
                    if prev:
                        C.wait(s, "s", ch_v, prev)
                    if thr_toks is not None:
                        C.wait(s, "s", ch_v, thr_toks[tt])
                    ins = s.activation(scr16[:, tt * 512:tt * 512 + wd],
                                       s_t[tt][:, i0:i0 + wd], ACTF.Abs)
                    toks[("abs", bi, tt)] = C.inc(ins, ch_s, 1)

            @blk.vector
            def _(v, bi=bi, i0=i0, wd=wd):
                C.wait(v, "v", ch_t, toks[("mm", bi)])
                for tt in range(nt):
                    C.wait(v, "v", ch_s, toks[("abs", bi, tt)])
                    if thr_toks is not None:
                        C.wait(v, "v", ch_v, thr_toks[tt])
                    prevtr = toks.get(("mtr_last", bi - 1, tt))
                    if prevtr:
                        C.wait(v, "v", ch_t, prevtr)
                    am = scr16[:, tt * 512:tt * 512 + wd]
                    mk = scr16[:, 2048 + tt * 512:2048 + tt * 512 + wd]
                    ins = v.scalar_tensor_tensor(mk, am, st("thr", tt),
                                                 s_t[tt][:, i0:i0 + wd],
                                                 op0=ALU.is_gt, op1=ALU.mult)
                    t0 = C.inc(ins, ch_v, 1)
                    C.wait(v, "v", ch_v, t0)
                    ins = v.tensor_mul(mk, mk, pA[:, tt * 512:tt * 512 + wd])
                    tok = C.inc(ins, ch_v, 1)
                    pa_free[tt] = (ch_v, tok)
                    toks[("mku", bi, tt)] = tok

            items = []
            for tt in range(nt):
                mk = scr16[:, 2048 + tt * 512:2048 + tt * 512 + wd]
                for cc in range(nchk):
                    items.append((mk[:, cc * 128:(cc + 1) * 128],
                                  s_t[tt][:, i0 + cc * 128:i0 + (cc + 1) * 128],
                                  [(ch_v, toks[("mku", bi, tt)])]))
            toks[("mev", bi, nt - 1)] = emit_transposes(items)
            for tt in range(nt):
                toks[("mtr_last", bi, tt)] = C.val(ch_t)
        return toks

    NIT_I = 20

    def emit_select(h, s_t, gate_toks, u_sb):
        thr_tok = [None] * nt
        Qh = Q
        for tt in range(nt):
            toksl = {}

            for qq in range(4):
                @blk.scalar
                def _(s, tt=tt, qq=qq):
                    C.wait(s, "s", ch_s, gate_toks[("ev", len(gblk) - 1, tt)])
                    if tt > 0:
                        C.wait(s, "s", ch_v, thr_tok[tt - 1])
                    ins = s.activation(u_sb[:, qq * Qh:(qq + 1) * Qh],
                                       s_t[tt][:, qq * Qh:(qq + 1) * Qh], ACTF.Abs)
                    toksl[("uabs", qq)] = C.inc(ins, ch_s, 1)

            @blk.vector
            def _(v, tt=tt):
                C.wait(v, "v", ch_s, toksl[("uabs", 3)])
                ins = v.reduce_max(st("hi", 0), u_sb[:], axis=AXX)
                C.inc(ins, ch_v, 1)
                v.memset(st("lo", 0), 0.0)
                v.memset(st("clo", 0), 0.0)
                ins = v.memset(st("chi", 0), float(I))
                t0 = C.inc(ins, ch_v, 1)
                C.wait(v, "v", ch_v, t0)

            par = 0
            for it in range(NIT_I):
                @blk.vector
                def _(v, tt=tt, it=it, par=par):
                    lo, hi = st("lo", par), st("hi", par)
                    clo, chi = st("clo", par), st("chi", par)
                    ins = v.tensor_sub(st("wbr"), hi, lo)
                    t0 = C.inc(ins, ch_v, 1)
                    if it == 0:
                        C.wait(v, "v", ch_v, t0)
                        ins = v.tensor_scalar(st("t1"), st("wbr"), 1.0 / 64, None,
                                              op0=ALU.mult)
                        t1 = C.inc(ins, ch_v, 1)
                        C.wait(v, "v", ch_v, t1)
                        ins = v.tensor_add(st("m"), lo, st("t1"))
                        tm = C.inc(ins, ch_v, 1)
                        C.wait(v, "v", ch_v, tm)
                    else:
                        v.tensor_sub(st("d1"), chi, clo)
                        ins = v.tensor_scalar(st("d2"), clo, -1.0, cfg.ksel,
                                              op0=ALU.mult, op1=ALU.add)
                        t1 = C.inc(ins, ch_v, 1)
                        C.wait(v, "v", ch_v, t1)
                        ins = v.reciprocal(st("rr"), st("d1"))
                        t2 = C.inc(ins, ch_v, 1)
                        C.wait(v, "v", ch_v, t2)
                        ins = v.tensor_mul(st("frac"), st("d2"), st("rr"))
                        t3 = C.inc(ins, ch_v, 1)
                        C.wait(v, "v", ch_v, t3)
                        ins = v.tensor_scalar_min(st("frac"), st("frac"), 63.0 / 64)
                        t3b = C.inc(ins, ch_v, 1)
                        C.wait(v, "v", ch_v, t3b)
                        ins = v.tensor_scalar_max(st("frac"), st("frac"), 1.0 / 64)
                        t4 = C.inc(ins, ch_v, 1)
                        C.wait(v, "v", ch_v, t4)
                        ins = v.tensor_mul(st("t1"), st("wbr"), st("frac"))
                        t5 = C.inc(ins, ch_v, 1)
                        C.wait(v, "v", ch_v, t5)
                        ins = v.tensor_add(st("m"), lo, st("t1"))
                        tm = C.inc(ins, ch_v, 1)
                        C.wait(v, "v", ch_v, tm)
                    toksl[("m", it)] = C.val(ch_v)
                    dmp = scr16[:, 4096:4096 + Qh]
                    for qq in range(2):
                        ins = v.tensor_scalar(dmp, u_sb[:, qq * Qh:(qq + 1) * Qh],
                                              st("m"), None,
                                              op0=ALU.is_le, op1=ALU.add,
                                              accum_out=st("cq1", qq))
                        t6 = C.inc(ins, ch_v, 1)
                        C.wait(v, "v", ch_v, t6)

                @blk.scalar
                def _(s, tt=tt, it=it):
                    C.wait(s, "s", ch_v, toksl[("m", it)])
                    dmp2 = scr16[:, 0:Qh]
                    for qq in (2, 3):
                        ins = s.activation(dmp2, u_sb[:, qq * Qh:(qq + 1) * Qh],
                                           ACTF.Sign, bias=st("m"), scale=-1.0,
                                           accum_out=st("sg", qq - 2))
                        tk = C.inc(ins, ch_s, 1)
                        C.wait(s, "s", ch_s, tk)
                    toksl[("actc", it)] = C.val(ch_s)

                @blk.vector
                def _(v, tt=tt, it=it, par=par):
                    lo, hi = st("lo", par), st("hi", par)
                    clo, chi = st("clo", par), st("chi", par)
                    nlo, nhi = st("lo", 1 - par), st("hi", 1 - par)
                    nclo, nchi = st("clo", 1 - par), st("chi", 1 - par)
                    C.wait(v, "v", ch_s, toksl[("actc", it)])
                    v.tensor_add(st("t1"), st("cq1", 0), st("cq1", 1))
                    ins = v.tensor_add(st("t2"), st("sg", 0), st("sg", 1))
                    t0 = C.inc(ins, ch_v, 1)
                    C.wait(v, "v", ch_v, t0)
                    ins = v.tensor_scalar(st("t2"), st("t2"), 0.5, float(Qh),
                                          op0=ALU.mult, op1=ALU.add)
                    t1 = C.inc(ins, ch_v, 1)
                    C.wait(v, "v", ch_v, t1)
                    ins = v.tensor_add(st("cnt"), st("t1"), st("t2"))
                    t2 = C.inc(ins, ch_v, 1)
                    C.wait(v, "v", ch_v, t2)
                    v.tensor_scalar(st("upd"), st("cnt"), cfg.ksel, None,
                                    op0=ALU.is_ge)
                    ins = v.tensor_scalar(st("invu"), st("cnt"), cfg.ksel, None,
                                          op0=ALU.is_lt)
                    t3 = C.inc(ins, ch_v, 1)
                    C.wait(v, "v", ch_v, t3)
                    v.tensor_mul(st("t1"), st("upd"), st("m"))
                    v.tensor_mul(st("t2"), st("invu"), hi)
                    v.tensor_mul(st("t3"), st("upd"), lo)
                    ins = v.tensor_mul(st("d1"), st("invu"), st("m"))
                    t4 = C.inc(ins, ch_v, 1)
                    C.wait(v, "v", ch_v, t4)
                    v.tensor_add(nhi, st("t1"), st("t2"))
                    ins = v.tensor_add(nlo, st("t3"), st("d1"))
                    t5 = C.inc(ins, ch_v, 1)
                    C.wait(v, "v", ch_v, t5)
                    v.tensor_mul(st("t1"), st("upd"), st("cnt"))
                    v.tensor_mul(st("t2"), st("invu"), chi)
                    v.tensor_mul(st("t3"), st("upd"), clo)
                    ins = v.tensor_mul(st("d1"), st("invu"), st("cnt"))
                    t6 = C.inc(ins, ch_v, 1)
                    C.wait(v, "v", ch_v, t6)
                    v.tensor_add(nchi, st("t1"), st("t2"))
                    ins = v.tensor_add(nclo, st("t3"), st("d1"))
                    t7 = C.inc(ins, ch_v, 1)
                    C.wait(v, "v", ch_v, t7)
                par = 1 - par

            @blk.vector
            def _(v, tt=tt, par=par):
                hi = st("hi", par)
                dmp = scr16[:, 4096:4096 + Qh]
                for qq in range(4):
                    ins = v.tensor_scalar(dmp, u_sb[:, qq * Qh:(qq + 1) * Qh],
                                          hi, None, op0=ALU.is_le, op1=ALU.add,
                                          accum_out=st("cq1", qq))
                    t0 = C.inc(ins, ch_v, 1)
                    C.wait(v, "v", ch_v, t0)
                ins = v.reduce_sum(st("chi", par), stat[:, SC["cq1"]:SC["cq1"] + 4],
                                   axis=AXX)
                t1 = C.inc(ins, ch_v, 1)
                C.wait(v, "v", ch_v, t1)

            for qq in range(4):
                @blk.vector
                def _(v, tt=tt, qq=qq, par=par):
                    if qq > 0:
                        C.wait(v, "v", ch_v, toksl[("m8", qq - 1)])
                    dmp = scr16[:, 4096:4096 + Qh]
                    uq = u_sb[:, qq * Qh:(qq + 1) * Qh]
                    ins = v.scalar_tensor_tensor(dmp, uq, st("hi", par), uq,
                                                 op0=ALU.is_le, op1=ALU.mult)
                    t0 = C.inc(ins, ch_v, 1)
                    C.wait(v, "v", ch_v, t0)
                    ins = v.max(out=stat[:, SC["t8q"] + qq * 8:SC["t8q"] + qq * 8 + 8],
                                in_=dmp)
                    toksl[("m8", qq)] = C.inc(ins, ch_v, 1)

            @blk.vector
            def _(v, tt=tt, par=par):
                C.wait(v, "v", ch_v, toksl[("m8", 3)])
                ins = v.max(out=stat[:, SC["t8g"]:SC["t8g"] + 8],
                            in_=stat[:, SC["t8q"]:SC["t8q"] + 32])
                t1 = C.inc(ins, ch_v, 1)
                C.wait(v, "v", ch_v, t1)
                ins = v.tensor_scalar(st("cidx"), st("chi", par), cfg.ksel, None,
                                      op0=ALU.subtract)
                tci = C.inc(ins, ch_v, 1)
                C.wait(v, "v", ch_v, tci)
                ins = v.tensor_scalar_min(st("cidx"), st("cidx"), 7.0)
                t2 = C.inc(ins, ch_v, 1)
                C.wait(v, "v", ch_v, t2)
                C.wait(v, "v", ch_g, iota_tok[0])
                ins = v.scalar_tensor_tensor(stat[:, SC["t8q"]:SC["t8q"] + 8],
                                             iota8[:], st("cidx"),
                                             stat[:, SC["t8g"]:SC["t8g"] + 8],
                                             op0=ALU.is_equal, op1=ALU.mult,
                                             accum_out=st("thr", tt))
                thr_tok[tt] = C.inc(ins, ch_v, 1)
        return thr_tok

    def emit_down(h, s_t, up_toks, attnx1):
        toks = {}
        ngrp = (ich + 7) // 8

        for ob in range(nob):
            for gi in range(ngrp):
                @blk.gpsimd
                def _(g, ob=ob, gi=gi):
                    ic0 = gi * 8
                    ncg = min(8, ich - ic0)
                    src = wd_h[:, (ob * ich + ic0) * 512:(ob * ich + ic0 + ncg) * 512]
                    toks[("wdma", ob, gi)] = dma_w(g, src, ncg * 512)

                @blk.tensor
                def _(t, ob=ob, gi=gi):
                    C.wait(t, "t", ch_s, up_toks[("mev", len(gblk) - 1, nt - 1)])
                    sl, wv = toks[("wdma", ob, gi)]
                    C.wait(t, "t", wsem[sl], wv)
                    ic0 = gi * 8
                    ncg = min(8, ich - ic0)
                    for ci in range(ncg):
                        ic = ic0 + ci
                        for tt in range(nt):
                            if ic == 0:
                                sem, valv = pa_free[tt]
                                if sem is not None:
                                    C.wait(t, "t", sem, valv)
                            ins = t.matmul(
                                pA[:, tt * 512:tt * 512 + 512],
                                s_t[tt][:, ic * 128:(ic + 1) * 128],
                                wslot[sl][:, ci * 512:(ci + 1) * 512],
                                start=(ic == 0), stop=(ic == ich - 1),
                                skip_group_check=True)
                            if ci == ncg - 1 and tt == nt - 1:
                                tok = C.inc(ins, ch_t, 1)
                                slot_freetok[sl] = tok
                                if gi == ngrp - 1:
                                    toks[("mm", ob)] = tok

            @blk.scalar
            def _(s, ob=ob):
                C.wait(s, "s", ch_t, toks[("mm", ob)])
                for tt in range(nt):
                    dst = attnx1[:, tt * D + ob * 512:tt * D + (ob + 1) * 512]
                    ins = s.activation(dst, pA[:, tt * 512:tt * 512 + 512],
                                       ACTF.Identity,
                                       accum_out=st("asum", tt * 8 + ob))
                    t0 = C.inc(ins, ch_s, 1)
                    C.wait(s, "s", ch_s, t0)
                    ins = s.activation(scr16[:, 4096:4096 + 512],
                                       pA[:, tt * 512:tt * 512 + 512],
                                       ACTF.Square, accum_out=st("asq", tt * 8 + ob))
                    tok = C.inc(ins, ch_s, 1)
                    pa_free[tt] = (ch_s, tok)
                    toks[("ev", ob, tt)] = tok

        x1_toks = [None] * nt
        for tt in range(nt):
            @blk.sync
            def _(sy, tt=tt):
                C.wait(sy, "y", ch_v, xtile_free[0])
                rows = h * cfg.hb + tt * 128
                ins = sy.dma_start(out=xtile[:], in_=x_h[rows:rows + 128, :])
                toks[("xre", tt)] = C.inc(ins, xsem, 16)

            @blk.vector
            def _(v, tt=tt):
                C.wait(v, "v", ch_s, toks[("ev", nob - 1, tt)])
                v.reduce_sum(st("t1"),
                             stat[:, SC["asum"] + tt * 8:SC["asum"] + tt * 8 + nob],
                             axis=AXX)
                ins = v.reduce_sum(st("t2"),
                                   stat[:, SC["asq"] + tt * 8:SC["asq"] + tt * 8 + nob],
                                   axis=AXX)
                t0 = C.inc(ins, ch_v, 1)
                C.wait(v, "v", ch_v, t0)
                v.tensor_scalar(st("am2", tt), st("t1"), 1.0 / D, None, op0=ALU.mult)
                ins = v.tensor_scalar(st("t2"), st("t2"), 1.0 / D, None, op0=ALU.mult)
                t1 = C.inc(ins, ch_v, 1)
                C.wait(v, "v", ch_v, t1)
                ins = v.tensor_mul(st("t3"), st("am2", tt), st("am2", tt))
                tv = C.inc(ins, ch_v, 1)
                C.wait(v, "v", ch_v, tv)
                ins = v.tensor_sub(st("t2"), st("t2"), st("t3"))
                toks[("var", tt)] = C.inc(ins, ch_v, 1)

            @blk.scalar
            def _(s, tt=tt):
                C.wait(s, "s", ch_v, toks[("var", tt)])
                ins = s.activation(st("t2"), st("t2"), ACTF.Sqrt, bias=epsv[:])
                toks[("sqt", tt)] = C.inc(ins, ch_s, 1)

            @blk.vector
            def _(v, tt=tt):
                C.wait(v, "v", ch_s, toks[("sqt", tt)])
                ins = v.reciprocal(st("ar2", tt), st("t2"))
                t0 = C.inc(ins, ch_v, 1)
                C.wait(v, "v", ch_v, t0)
                sl = attnx1[:, tt * D:(tt + 1) * D]
                ins = v.tensor_scalar(sl, sl, st("am2", tt), st("ar2", tt),
                                      op0=ALU.subtract, op1=ALU.mult)
                t1 = C.inc(ins, ch_v, 1)
                C.wait(v, "v", ch_v, t1)
                C.wait(v, "v", xsem, toks[("xre", tt)])
                ins = v.tensor_add(sl, sl, xtile[:])
                tok = C.inc(ins, ch_v, 1)
                xtile_free[0] = tok
                x1_toks[tt] = tok
        return x1_toks, C.val(ch_t)

    def emit_ffn(h, x1, x1_toks, xlT2, ffn_sb, qk_sb, hg_sb, wbm_sb, dn_done):
        toks = {}
        ln2_tok = emit_ln2(h, x1, x1_toks, xlT2, dn_done)
        nq = (hc + 7) // 8

        @blk.gpsimd
        def _(g):
            C.wait(g, "g", ch_t, dn_done)
            ins = g.dma_start(out=wbm_sb[:], in_=wbm_h[:])
            msem_n[0] += 1
            toks["wbm"] = C.inc(ins, msem, 16)

        nqk = max(1, (hc * EDK * 2) // 4096)
        cpq = 4096 // (EDK * 2)
        for q in range(nqk):
            @blk.gpsimd
            def _(g, q=q):
                w = min(4096, hc * EDK * 2 - q * 4096)
                toks[("qkdma", q)] = dma_w(g, wqk_h[:, q * 4096:q * 4096 + w], w)

            @blk.tensor
            def _(t, q=q):
                C.wait(t, "t", ch_s, ln2_tok)
                sl, wv = toks[("qkdma", q)]
                C.wait(t, "t", wsem[sl], wv)
                for ci in range(cpq):
                    c = q * cpq + ci
                    if c >= hc:
                        break
                    for tt in range(nt):
                        if c == 0:
                            sem, valv = pa_free[tt]
                            if sem is not None:
                                C.wait(t, "t", sem, valv)
                        ins = t.matmul(
                            pA[:, tt * 512:tt * 512 + EDK * 2],
                            xlT2[:, (c * nt + tt) * 128:(c * nt + tt) * 128 + 128],
                            wslot[sl][:, ci * EDK * 2:(ci + 1) * EDK * 2],
                            start=(c == 0), stop=(c == hc - 1),
                            skip_group_check=True)
                        if (ci == cpq - 1 or c == hc - 1) and tt == nt - 1:
                            tok = C.inc(ins, ch_t, 1)
                            slot_freetok[sl] = tok
                            if q == nqk - 1:
                                toks["qkmm"] = tok

        for tt in range(nt):
            @blk.scalar
            def _(s, tt=tt):
                C.wait(s, "s", ch_t, toks["qkmm"])
                ins = s.copy(qk_sb[:, tt * 512:(tt + 1) * 512],
                             pA[:, tt * 512:tt * 512 + 512])
                tok = C.inc(ins, ch_s, 1)
                pa_free[tt] = (ch_s, tok)
                toks[("qkev", tt)] = tok

        na = max(1, (hc * ER) // 4096)
        cpa = min(4096 // ER, hc)
        for q in range(na):
            @blk.gpsimd
            def _(g, q=q):
                w = min(4096, hc * ER - q * 4096)
                toks[("adma", q)] = dma_w(g, wa_h[:, q * 4096:q * 4096 + w], w)

            @blk.tensor
            def _(t, q=q):
                sl, wv = toks[("adma", q)]
                C.wait(t, "t", wsem[sl], wv)
                for ci in range(cpa):
                    c = q * cpa + ci
                    if c >= hc:
                        break
                    for tt in range(nt):
                        if c == 0:
                            sem, valv = pa_free[tt]
                            if sem is not None:
                                C.wait(t, "t", sem, valv)
                        ins = t.matmul(
                            pA[:, tt * 512:tt * 512 + ER],
                            xlT2[:, (c * nt + tt) * 128:(c * nt + tt) * 128 + 128],
                            wslot[sl][:, ci * ER:(ci + 1) * ER],
                            start=(c == 0), stop=(c == hc - 1),
                            skip_group_check=True)
                        if (ci == cpa - 1 or c == hc - 1) and tt == nt - 1:
                            tok = C.inc(ins, ch_t, 1)
                            slot_freetok[sl] = tok
                            if q == na - 1:
                                toks["hmm"] = tok

        gT_toks = [None] * nt
        for tt in range(nt):
            @blk.scalar
            def _(s, tt=tt):
                C.wait(s, "s", ch_t, toks["hmm"])
                ins = s.copy(hg_sb[:, tt * 128:(tt + 1) * 128],
                             pA[:, tt * 512:tt * 512 + ER])
                tok = C.inc(ins, ch_s, 1)
                pa_free[tt] = (ch_s, tok)
                toks[("hev", tt)] = tok

            @blk.vector
            def _(v, tt=tt):
                C.wait(v, "v", ch_s, toks[("qkev", tt)])
                prod = scr16[:, 6848:6848 + EDK]
                ins = v.tensor_mul(prod, qk_sb[:, tt * 512:tt * 512 + EDK],
                                   qk_sb[:, tt * 512 + EDK:tt * 512 + 2 * EDK])
                t0 = C.inc(ins, ch_v, 1)
                C.wait(v, "v", ch_v, t0)
                ins = v.reduce_sum(stat[:, SC["t8g"]:SC["t8g"] + 8],
                                   prod.rearrange("p (e d) -> p e d", e=8), axis=AXX)
                t1 = C.inc(ins, ch_v, 1)
                C.wait(v, "v", ch_v, t1)
                ins = v.tensor_scalar(stat[:, SC["t8g"]:SC["t8g"] + 8],
                                      stat[:, SC["t8g"]:SC["t8g"] + 8],
                                      1.0 / float(np.sqrt(DK_)), None, op0=ALU.mult)
                t2 = C.inc(ins, ch_v, 1)
                C.wait(v, "v", ch_v, t2)
                ins = v.reduce_max(st("neg8"), stat[:, SC["t8g"]:SC["t8g"] + 8],
                                   axis=AXX, negate=True)
                toks[("sneg", tt)] = C.inc(ins, ch_v, 1)

            @blk.scalar
            def _(s, tt=tt):
                C.wait(s, "s", ch_v, toks[("sneg", tt)])
                ins = s.activation(stat[:, SC["t8q"]:SC["t8q"] + 8],
                                   stat[:, SC["t8g"]:SC["t8g"] + 8], ACTF.Exp,
                                   bias=st("neg8"), accum_out=st("sexp"))
                toks[("exp", tt)] = C.inc(ins, ch_s, 1)

            @blk.vector
            def _(v, tt=tt):
                C.wait(v, "v", ch_s, toks[("exp", tt)])
                ins = v.reciprocal(st("winv"), st("sexp"))
                t0 = C.inc(ins, ch_v, 1)
                C.wait(v, "v", ch_v, t0)
                ins = v.tensor_scalar(stat[:, SC["t8q"]:SC["t8q"] + 8],
                                      stat[:, SC["t8q"]:SC["t8q"] + 8],
                                      st("winv"), None, op0=ALU.mult)
                t1 = C.inc(ins, ch_v, 1)
                C.wait(v, "v", ch_v, t1)
                C.wait(v, "v", ch_s, toks[("hev", tt)])
                gdst = hg_sb[:, nt * 128 + tt * 128:nt * 128 + (tt + 1) * 128]
                ins = v.tensor_mul(
                    gdst.rearrange("p (e r) -> p e r", e=8),
                    hg_sb[:, tt * 128:(tt + 1) * 128].rearrange(
                        "p (e r) -> p e r", e=8),
                    stat[:, SC["t8q"]:SC["t8q"] + 8]
                    .rearrange("p (e o) -> p e o", o=1).to_broadcast([128, 8, R_]))
                toks[("g", tt)] = C.inc(ins, ch_v, 1)

            gdst = hg_sb[:, nt * 128 + tt * 128:nt * 128 + (tt + 1) * 128]
            gt = hg_sb[:, 2 * nt * 128 + tt * 128:2 * nt * 128 + (tt + 1) * 128]
            gT_toks[tt] = emit_transposes(
                [(gdst, gt, [(ch_v, toks[("g", tt)])])])

        for ob in range(nob):
            for q in range(nq):
                @blk.gpsimd
                def _(g, ob=ob, q=q):
                    nch = min((q + 1) * 8, hc) - q * 8
                    src = wW_h[:, hc * ob * 512 + q * 8 * 512:
                               hc * ob * 512 + (q * 8 + nch) * 512]
                    toks[("Wdma", ob, q)] = dma_w(g, src, nch * 512)

                @blk.tensor
                def _(t, ob=ob, q=q):
                    C.wait(t, "t", msem, toks["wbm"])
                    C.wait(t, "t", msem, 16)
                    C.wait(t, "t", ch_v, init_tok[0])
                    for tt in range(nt):
                        C.wait(t, "t", ch_s, gT_toks[tt])
                    sl, wv = toks[("Wdma", ob, q)]
                    C.wait(t, "t", wsem[sl], wv)
                    nch = min((q + 1) * 8, hc) - q * 8
                    for ci in range(nch):
                        c = q * 8 + ci
                        for tt in range(nt):
                            if c == 0:
                                sem, valv = pa_free[tt]
                                if sem is not None:
                                    C.wait(t, "t", sem, valv)
                            ins = t.matmul(
                                pA[:, tt * 512:tt * 512 + 512],
                                xlT2[:, (c * nt + tt) * 128:(c * nt + tt) * 128 + 128],
                                wslot[sl][:, ci * 512:(ci + 1) * 512],
                                start=(c == 0), stop=False,
                                skip_group_check=True)
                            if ci == nch - 1 and tt == nt - 1:
                                tok = C.inc(ins, ch_t, 1)
                                slot_freetok[sl] = tok

            @blk.tensor
            def _(t, ob=ob):
                for tt in range(nt):
                    t.matmul(pA[:, tt * 512:tt * 512 + 512], ones1[:],
                             b_sb[:, ob * 512:(ob + 1) * 512],
                             start=False, stop=False, skip_group_check=True)
                    gt = hg_sb[:, 2 * nt * 128 + tt * 128:
                               2 * nt * 128 + (tt + 1) * 128]
                    ins = t.matmul(pA[:, tt * 512:tt * 512 + 512], gt,
                                   wbm_sb[:, ob * 512:(ob + 1) * 512],
                                   start=False, stop=True, skip_group_check=True)
                    if tt == nt - 1:
                        toks[("Wmm", ob)] = C.inc(ins, ch_t, 1)

            @blk.scalar
            def _(s, ob=ob):
                C.wait(s, "s", ch_t, toks[("Wmm", ob)])
                for tt in range(nt):
                    dst = ffn_sb[:, tt * D + ob * 512:tt * D + (ob + 1) * 512]
                    ins = s.activation(dst, pA[:, tt * 512:tt * 512 + 512],
                                       ACTF.Identity,
                                       accum_out=st("fsum", tt * 8 + ob))
                    t0 = C.inc(ins, ch_s, 1)
                    C.wait(s, "s", ch_s, t0)
                    ins = s.activation(scr16[:, 4096:4096 + 512],
                                       pA[:, tt * 512:tt * 512 + 512],
                                       ACTF.Square, accum_out=st("fsq", tt * 8 + ob))
                    tok = C.inc(ins, ch_s, 1)
                    pa_free[tt] = (ch_s, tok)
                    toks[("fev", ob, tt)] = tok

        for tt in range(nt):
            @blk.vector
            def _(v, tt=tt):
                C.wait(v, "v", ch_s, toks[("fev", nob - 1, tt)])
                v.reduce_sum(st("t1"),
                             stat[:, SC["fsum"] + tt * 8:SC["fsum"] + tt * 8 + nob],
                             axis=AXX)
                ins = v.reduce_sum(st("t2"),
                                   stat[:, SC["fsq"] + tt * 8:SC["fsq"] + tt * 8 + nob],
                                   axis=AXX)
                t0 = C.inc(ins, ch_v, 1)
                C.wait(v, "v", ch_v, t0)
                v.tensor_scalar(st("fm", tt), st("t1"), 1.0 / D, None, op0=ALU.mult)
                ins = v.tensor_scalar(st("t2"), st("t2"), 1.0 / D, None, op0=ALU.mult)
                t1 = C.inc(ins, ch_v, 1)
                C.wait(v, "v", ch_v, t1)
                ins = v.tensor_mul(st("t3"), st("fm", tt), st("fm", tt))
                tv = C.inc(ins, ch_v, 1)
                C.wait(v, "v", ch_v, tv)
                ins = v.tensor_sub(st("t2"), st("t2"), st("t3"))
                toks[("fvar", tt)] = C.inc(ins, ch_v, 1)

            @blk.scalar
            def _(s, tt=tt):
                C.wait(s, "s", ch_v, toks[("fvar", tt)])
                ins = s.activation(st("t2"), st("t2"), ACTF.Sqrt, bias=epsv[:])
                toks[("fsqt", tt)] = C.inc(ins, ch_s, 1)

            @blk.vector
            def _(v, tt=tt):
                C.wait(v, "v", ch_s, toks[("fsqt", tt)])
                ins = v.reciprocal(st("fr", tt), st("t2"))
                t0 = C.inc(ins, ch_v, 1)
                C.wait(v, "v", ch_v, t0)
                C.wait(v, "v", ch_v, xtile_free[0])
                if out_dmas[0] > 0:
                    C.wait(v, "v", osem, out_dmas[0] * 16)
                ins = v.tensor_scalar(xtile[:], ffn_sb[:, tt * D:(tt + 1) * D],
                                      st("fm", tt), st("fr", tt),
                                      op0=ALU.subtract, op1=ALU.mult)
                t1 = C.inc(ins, ch_v, 1)
                C.wait(v, "v", ch_v, t1)
                ins = v.tensor_add(xtile[:], xtile[:], x1[:, tt * D:(tt + 1) * D])
                tok = C.inc(ins, ch_v, 1)
                xtile_free[0] = tok
                toks[("out", tt)] = tok

            @blk.sync
            def _(sy, tt=tt):
                C.wait(sy, "y", ch_v, toks[("out", tt)])
                rows = h * cfg.hb + tt * 128
                ins = sy.dma_start(out=out_h[rows:rows + 128, :], in_=xtile[:])
                out_dmas[0] += 1
                C.inc(ins, osem, 16)

    init_phase()

    for h in range(nh):
        if debug_x1 and h > 0:
            for sec, eng_name in ((blk.scalar, "s"), (blk.vector, "v"),
                                  (blk.tensor, "t"), (blk.gpsimd, "g"),
                                  (blk.sync, "y")):
                def _bar(e, eng_name=eng_name):
                    C.wait(e, eng_name, msem, msem_n[0] * 16)
                sec(_bar)
        xlT_cm = nc.sbuf_tensor("xlT%d" % h, [128, hc * nt * 128], F16, side="right")
        xlT = xlT_cm.__enter__()
        s_cm = [nc.sbuf_tensor(f"s{h}_{tt}", [128, I], F16) for tt in range(nt)]
        s_t = [cm.__enter__() for cm in s_cm]

        u_cm = nc.sbuf_tensor("u%d" % h, [128, I], F16, side="right")
        u_sb = u_cm.__enter__()

        ln1_tok = emit_ln1(h, xlT)
        gate_toks = emit_gateup(h, "gate", xlT, s_t, ln1_tok)
        thr_toks = emit_select(h, s_t, gate_toks, u_sb)
        up_toks = emit_gateup(h, "up", xlT, s_t, ln1_tok, thr_toks=thr_toks)

        u_cm.__exit__(None, None, None)
        xlT_cm.__exit__(None, None, None)
        attnx1_cm = nc.sbuf_tensor("ax1_%d" % h, [128, nt * D], F16, side="right")
        attnx1 = attnx1_cm.__enter__()

        x1_toks, dn_done = emit_down(h, s_t, up_toks, attnx1)

        if debug_x1:
            @blk.gpsimd
            def _(g, h=h, attnx1=attnx1, x1_toks=x1_toks):
                if h == 0:
                    ins = g.dma_start(out=dbg4_h[:, 72:72 + nt], in_=stat[:, 72:72 + nt])
                    msem_n[0] += 1
                    C.inc(ins, msem, 16)
                for tt in range(nt):
                    C.wait(g, "g", ch_v, x1_toks[tt])
                    rows = h * cfg.hb + tt * 128
                    ins = g.dma_start(out=dbg_h[rows:rows + 128, :],
                                      in_=attnx1[:, tt * D:(tt + 1) * D])
                    msem_n[0] += 1
                    C.inc(ins, msem, 16)

        for cm in reversed(s_cm):
            cm.__exit__(None, None, None)

        xlT2_cm = nc.sbuf_tensor("xlT2_%d" % h, [128, hc * nt * 128], F16)
        xlT2 = xlT2_cm.__enter__()
        ffn_cm = nc.sbuf_tensor("ffn_%d" % h, [128, nt * D], F16)
        ffn_sb = ffn_cm.__enter__()
        qk_cm = nc.sbuf_tensor("qk_%d" % h, [128, nt * 512], F16)
        qk_sb = qk_cm.__enter__()
        hg_cm = nc.sbuf_tensor("hg_%d" % h, [128, 3 * nt * 128], F16)
        hg_sb = hg_cm.__enter__()
        wbm_cm = nc.sbuf_tensor("wbm_%d" % h, [128, D], F16)
        wbm_sb = wbm_cm.__enter__()

        emit_ffn(h, attnx1, x1_toks, xlT2, ffn_sb, qk_sb, hg_sb, wbm_sb, dn_done)

        if debug_x1:
            @blk.gpsimd
            def _(g, h=h, ffn_sb=ffn_sb, qk_sb=qk_sb, hg_sb=hg_sb, xlT2=xlT2):
                C.wait(g, "g", ch_v, C.val(ch_v))
                C.wait(g, "g", ch_s, C.val(ch_s))
                for tt in range(nt):
                    rows = h * cfg.hb + tt * 128
                    ins = g.dma_start(out=dbg2_h[rows:rows + 128, :],
                                      in_=ffn_sb[:, tt * D:(tt + 1) * D])
                    msem_n[0] += 1
                    C.inc(ins, msem, 16)
                if h == nh - 1:
                    ins = g.dma_start(out=dbg3_h[0:128, 0:nt * 512], in_=qk_sb[:])
                    msem_n[0] += 1
                    C.inc(ins, msem, 16)
                    ins = g.dma_start(out=dbg3_h[128:256, 0:3 * nt * 128],
                                      in_=hg_sb[:])
                    msem_n[0] += 1
                    C.inc(ins, msem, 16)
                    ins = g.dma_start(out=dbg3_h[256:384, 0:hc * nt * 128],
                                      in_=xlT2[:])
                    msem_n[0] += 1
                    C.inc(ins, msem, 16)

        for cm in (wbm_cm, hg_cm, qk_cm, ffn_cm, xlT2_cm):
            cm.__exit__(None, None, None)
        attnx1_cm.__exit__(None, None, None)

    @blk.sync
    def _(sy):
        sy.wait_ge(osem, out_dmas[0] * 16)
        sy.wait_ge(msem, msem_n[0] * 16)

    for cm in reversed(ctxs):
        cm.__exit__(None, None, None)
    return nc


def kernel(L0_attn_x, L0_attn_w_gate, L0_attn_w_up, L0_attn_w_down,
           L0_ffn_W, L0_ffn_b, L0_ffn_Wq, L0_ffn_Wk, L0_ffn_A, L0_ffn_Bm):
    cfg = CFG
    x, lay = host_layouts(L0_attn_x, L0_attn_w_gate, L0_attn_w_up, L0_attn_w_down,
                          L0_ffn_W, L0_ffn_b, L0_ffn_Wq, L0_ffn_Wk,
                          L0_ffn_A, L0_ffn_Bm, cfg)
    nc = build_nc(cfg)
    in_maps = []
    for c in range(NCORES):
        m = {"x": np.ascontiguousarray(x[c * cfg.tokc:(c + 1) * cfg.tokc])}
        m.update(lay)
        in_maps.append(m)
    res = run_bass_kernel_spmd(nc, in_maps, core_ids=list(range(NCORES)))
    out = np.concatenate([np.asarray(res.results[c]["out"]) for c in range(NCORES)], 0)
    return np.ascontiguousarray(out.reshape(2, 4096, 4096).astype(np.float32))


# version 14
# speedup vs baseline: 1.0801x; 1.0208x over previous
import numpy as np

import concourse.bass as bass
import concourse.mybir as mybir
from concourse.bass_utils import run_bass_kernel_spmd

F32 = mybir.dt.float32
F16 = mybir.dt.float16
ALU = mybir.AluOpType
ACTF = mybir.ActivationFunctionType
AXX = mybir.AxisListType.X

NCORES = 8
E_, DK_, R_ = 8, 32, 16
EDK = E_ * DK_
ER = E_ * R_
NIT = 28


class Cfg:
    def __init__(self, D=4096, I=11008, tokc=1024, nt=4):
        self.D = D
        self.I = I
        self.tokc = tokc
        self.nt = nt
        self.hb = nt * 128
        self.nh = tokc // self.hb
        assert self.nh * self.hb == tokc
        self.hc = D // 128
        self.ich = I // 128
        self.gblk = []
        i0 = 0
        while i0 < I:
            w = min(512, I - i0)
            self.gblk.append((i0, w))
            i0 += w
        self.ksel = float(I // 2)
        self.ob = D // 512
        self.qsel = I // 4
        assert self.qsel * 4 == I and self.qsel <= 2752


CFG = Cfg()


def _gu_layout(w, cfg):
    hc = cfg.hc
    O = w.shape[0]
    t = np.ascontiguousarray(w.T).reshape(hc, 128, O).transpose(1, 0, 2)
    blocks = []
    i0 = 0
    while i0 < O:
        wd = min(512, O - i0)
        blocks.append(np.ascontiguousarray(t[:, :, i0:i0 + wd]).reshape(128, hc * wd))
        i0 += wd
    return np.ascontiguousarray(np.concatenate(blocks, axis=1))


def _wd_layout(w_down, cfg):
    D, ich, nob = cfg.D, cfg.ich, cfg.ob
    t = np.ascontiguousarray(w_down.T).reshape(ich, 128, D).transpose(1, 0, 2)
    parts = [np.ascontiguousarray(t[:, :, ob * 512:(ob + 1) * 512]).reshape(128, ich * 512)
             for ob in range(nob)]
    return np.ascontiguousarray(np.concatenate(parts, axis=1))


def host_layouts(x, w_gate, w_up, w_down, W, b, Wq, Wk, A, Bm, cfg):
    x = np.ascontiguousarray(np.asarray(x, np.float32).reshape(-1, cfg.D))
    lay = {
        "wg": _gu_layout(np.asarray(w_gate, np.float32), cfg),
        "wu": _gu_layout(np.asarray(w_up, np.float32), cfg),
        "wd": _wd_layout(np.asarray(w_down, np.float32), cfg),
        "wW": _gu_layout(np.asarray(W, np.float32), cfg),
        "wqk": _gu_layout(np.concatenate([np.asarray(Wq, np.float32),
                                          np.asarray(Wk, np.float32)], 0), cfg),
        "wa": _gu_layout(np.asarray(A, np.float32).reshape(ER, cfg.D), cfg),
        "wbm": np.ascontiguousarray(
            np.asarray(Bm, np.float32).transpose(0, 2, 1).reshape(ER, cfg.D)),
        "b": np.ascontiguousarray(np.asarray(b, np.float32).reshape(1, cfg.D)),
    }
    return x, lay


class SemCtr:
    def __init__(self):
        self.v = {}
        self.w = {}

    def inc(self, ins, sem, n=1):
        self.v[sem] = self.v.get(sem, 0) + n
        ins.then_inc(sem, n)
        return self.v[sem]

    def val(self, sem):
        return self.v.get(sem, 0)

    def wait(self, eng, engname, sem, val):
        if not val or val <= 0:
            return
        key = (engname, sem)
        if self.w.get(key, 0) >= val:
            return
        eng.wait_ge(sem, val)
        self.w[key] = val


def build_nc(cfg=CFG, debug_x1=False):
    D, I, hc, ich, nt, nh = cfg.D, cfg.I, cfg.hc, cfg.ich, cfg.nt, cfg.nh
    nob, gblk = cfg.ob, cfg.gblk
    Q = cfg.qsel
    nc = bass.Bass("TRN2", target_bir_lowering=False)

    x_h = nc.dram_tensor("x", [cfg.tokc, D], F32, kind="ExternalInput").ap()
    wg_h = nc.dram_tensor("wg", [128, hc * I], F32, kind="ExternalInput").ap()
    wu_h = nc.dram_tensor("wu", [128, hc * I], F32, kind="ExternalInput").ap()
    wd_h = nc.dram_tensor("wd", [128, nob * ich * 512], F32, kind="ExternalInput").ap()
    wW_h = nc.dram_tensor("wW", [128, hc * D], F32, kind="ExternalInput").ap()
    wqk_h = nc.dram_tensor("wqk", [128, hc * EDK * 2], F32, kind="ExternalInput").ap()
    wa_h = nc.dram_tensor("wa", [128, hc * ER], F32, kind="ExternalInput").ap()
    wbm_h = nc.dram_tensor("wbm", [128, D], F32, kind="ExternalInput").ap()
    b_h = nc.dram_tensor("b", [1, D], F32, kind="ExternalInput").ap()
    out_h = nc.dram_tensor("out", [cfg.tokc, D], F32, kind="ExternalOutput").ap()
    dbg_h = (nc.dram_tensor("dbg", [cfg.tokc, D], F32, kind="ExternalOutput").ap()
             if debug_x1 else None)
    dbg2_h = (nc.dram_tensor("dbg2", [cfg.tokc, D], F32, kind="ExternalOutput").ap()
              if debug_x1 else None)
    dbg3_h = (nc.dram_tensor("dbg3", [384, max(hc * nt * 128, nt * 512)], F32,
                             kind="ExternalOutput").ap()
              if debug_x1 else None)
    dbg4_h = (nc.dram_tensor("dbg4", [128, 256], F32, kind="ExternalOutput").ap()
              if debug_x1 else None)

    ctxs = []

    def alloc(maker):
        cm = maker
        h = cm.__enter__()
        ctxs.append(cm)
        return h

    C = SemCtr()

    wslot = [alloc(nc.sbuf_tensor(f"wslot{i}", [128, 4096], F16)) for i in range(3)]
    scr16 = alloc(nc.sbuf_tensor("scr16", [128, 8192], F16))
    xtile = alloc(nc.sbuf_tensor("xtile", [128, D], F32))
    stat = alloc(nc.sbuf_tensor("stat", [128, 256], F32))
    ident = alloc(nc.sbuf_tensor("ident", [128, 128], F16))
    idenf = alloc(nc.sbuf_tensor("idenf", [128, 128], F32))
    iota8 = alloc(nc.sbuf_tensor("iota8", [128, 8], F32))
    ones1 = alloc(nc.sbuf_tensor("ones1", [1, 128], F16))
    epsv = alloc(nc.sbuf_tensor("epsv", [128, 1], F32))
    b_sb = alloc(nc.sbuf_tensor("b_sb", [1, D], F16))
    msk_u = alloc(nc.sbuf_tensor("msk_u", [128, 1], mybir.dt.uint32))

    pA = alloc(nc.psum_tensor("pA", [128, 2048], F32))
    pT = alloc(nc.psum_tensor("pT", [128, 4096], F16))

    ch_t = alloc(nc.semaphore("ch_t"))
    ch_v = alloc(nc.semaphore("ch_v"))
    ch_s = alloc(nc.semaphore("ch_s"))
    ch_g = alloc(nc.semaphore("ch_g"))
    wsem = [alloc(nc.semaphore(f"wsem{i}")) for i in range(3)]
    xsem = alloc(nc.semaphore("xsem"))
    msem = alloc(nc.semaphore("msem"))
    osem = alloc(nc.semaphore("osem"))

    SC = {"sum": 0, "sq": 1, "lnm": 2, "lnr": 3, "t1": 4, "t2": 5, "t3": 6,
          "m": 8, "negm": 9, "cnt": 10, "upd": 11, "invu": 12, "msum": 13,
          "cidx": 14, "sqt": 15,
          "cq1": 16, "cq2": 20,
          "lo": 24, "hi": 26, "chi": 28, "clo": 30,
          "sg": 234, "frac": 238, "wbr": 239, "d1": 240, "d2": 241, "rr": 242,
          "t8q": 32, "t8g": 64, "thr": 72,
          "asum": 80, "asq": 112,
          "am2": 144, "ar2": 148,
          "neg8": 152, "sexp": 153, "winv": 154,
          "fsum": 160, "fsq": 192,
          "fm": 224, "fr": 228}

    def st(name, off=0):
        c = SC[name] + off
        return stat[:, c:c + 1]

    blk_cm = nc.Block()
    blk = blk_cm.__enter__()
    ctxs.append(blk_cm)

    NSLOT = 3
    slot_fill = [0] * NSLOT
    slot_freetok = [0] * NSLOT
    slot_rr = [0]

    def dma_w(g, src_ap, width):
        sl = slot_rr[0] % NSLOT
        slot_rr[0] += 1
        C.wait(g, "g", ch_t, slot_freetok[sl])
        ins = g.dma_start(out=wslot[sl][:, :width], in_=src_ap)
        slot_fill[sl] += 1
        C.inc(ins, wsem[sl], 16)
        return sl, slot_fill[sl] * 16

    pt_state = {"n": 0, "ev": {}}

    def emit_transposes(items):
        last = None
        for g0 in range(0, len(items), 4):
            grp = items[g0:g0 + 4]
            base = pt_state["n"]

            @blk.tensor
            def _(t, grp=grp, base=base):
                for k, (srcap, dstap, tw) in enumerate(grp):
                    n = base + k
                    j = n % 32
                    off = (j % 4) * 1024 + (j // 4) * 128
                    for sem, valv in tw:
                        C.wait(t, "t", sem, valv)
                    if n >= 4:
                        C.wait(t, "t", ch_s, pt_state["ev"][n - 4])
                    ins = t.transpose(pT[:, off:off + 128], srcap, ident[:])
                    pt_state["ev"][("tr", n)] = C.inc(ins, ch_t, 1)
                pt_state["n"] = base + len(grp)

            @blk.scalar
            def _(s, grp=grp, base=base):
                nonlocal last
                for k, (srcap, dstap, tw) in enumerate(grp):
                    n = base + k
                    j = n % 32
                    off = (j % 4) * 1024 + (j // 4) * 128
                    C.wait(s, "s", ch_t, pt_state["ev"][("tr", n)])
                    ins = s.copy(dstap, pT[:, off:off + 128])
                    tk = C.inc(ins, ch_s, 1)
                    pt_state["ev"][n] = tk
                    last = tk
        return last

    pa_free = [(None, 0)] * 4
    pb_free = [(None, 0)]
    out_dmas = [0]
    xtile_free = [0]
    msem_n = [0]
    init_tok = [0]
    iota_tok = [0]

    def init_phase():
        @blk.gpsimd
        def _(g):
            ins = g.iota(iota8[:], pattern=[[1, 8]], base=0, channel_multiplier=0,
                         allow_small_or_imprecise_dtypes=True)
            iota_tok[0] = C.inc(ins, ch_g, 1)
            ins = g.iota(idenf[:], pattern=[[1, 128]], base=0, channel_multiplier=-1,
                         allow_small_or_imprecise_dtypes=True)
            C.inc(ins, ch_g, 1)
            ins = g.dma_start(out=b_sb[:], in_=b_h[:])
            msem_n[0] += 1
            C.inc(ins, msem, 16)

        @blk.vector
        def _(v):
            v.memset(ones1[:], 1.0)
            v.memset(epsv[:], 1e-5)
            C.wait(v, "v", ch_g, C.val(ch_g))
            ins = v.tensor_scalar(ident[:], idenf[:], 0.0, None, op0=ALU.is_equal)
            init_tok[0] = C.inc(ins, ch_v, 1)

    def emit_ln_tile(tt, src, toks, xlT, key):
        @blk.scalar
        def _(s):
            w = toks.get((key, "srcwait", tt))
            if w:
                C.wait(s, "s", w[0], w[1])
            ins = s.activation(scr16[:, 4096:4096 + D], src, ACTF.Identity,
                               accum_out=st("sum"))
            t0 = C.inc(ins, ch_s, 1)
            C.wait(s, "s", ch_s, t0)
            ins = s.activation(scr16[:, 4096:4096 + D], src, ACTF.Square,
                               accum_out=st("sq"))
            toks[(key, "stats", tt)] = C.inc(ins, ch_s, 1)

        @blk.vector
        def _(v):
            C.wait(v, "v", ch_s, toks[(key, "stats", tt)])
            ins = v.tensor_scalar(st("lnm"), st("sum"), 1.0 / D, None, op0=ALU.mult)
            t0 = C.inc(ins, ch_v, 1)
            C.wait(v, "v", ch_v, t0)
            v.tensor_scalar(st("t1"), st("sq"), 1.0 / D, None, op0=ALU.mult)
            ins = v.tensor_mul(st("t2"), st("lnm"), st("lnm"))
            tv = C.inc(ins, ch_v, 1)
            C.wait(v, "v", ch_v, tv)
            ins = v.tensor_sub(st("t1"), st("t1"), st("t2"))
            toks[(key, "var", tt)] = C.inc(ins, ch_v, 1)

        @blk.scalar
        def _(s):
            C.wait(s, "s", ch_v, toks[(key, "var", tt)])
            C.wait(s, "s", ch_v, init_tok[0])
            ins = s.activation(st("sqt"), st("t1"), ACTF.Sqrt, bias=epsv[:])
            toks[(key, "sqt", tt)] = C.inc(ins, ch_s, 1)

        @blk.vector
        def _(v):
            C.wait(v, "v", ch_s, toks[(key, "sqt", tt)])
            ins = v.reciprocal(st("lnr"), st("sqt"))
            t0 = C.inc(ins, ch_v, 1)
            C.wait(v, "v", ch_v, t0)
            prev = toks.get((key, "tr_done", tt - 1))
            if prev:
                C.wait(v, "v", ch_t, prev)
            ins = v.tensor_scalar(scr16[:, 0:D], src, st("lnm"), st("lnr"),
                                  op0=ALU.subtract, op1=ALU.mult)
            toks[(key, "norm", tt)] = C.inc(ins, ch_v, 1)

        items = []
        for c in range(hc):
            items.append((scr16[:, c * 128:(c + 1) * 128],
                          xlT[:, (c * nt + tt) * 128:(c * nt + tt) * 128 + 128],
                          [(ch_v, toks[(key, "norm", tt)]), (ch_v, init_tok[0])]))
        toks[(key, "evac", tt)] = emit_transposes(items)
        toks[(key, "tr_done", tt)] = C.val(ch_t)

    def emit_ln1(h, xlT):
        toks = {}
        for tt in range(nt):
            @blk.sync
            def _(sy, tt=tt):
                C.wait(sy, "y", ch_v, toks.get(("ln1", "xfree", tt - 1),
                                               xtile_free[0]))
                C.wait(sy, "y", osem, out_dmas[0] * 16)
                rows = h * cfg.hb + tt * 128
                ins = sy.dma_start(out=xtile[:], in_=x_h[rows:rows + 128, :])
                toks[("ln1", "xload", tt)] = C.inc(ins, xsem, 16)

            toks[("ln1", "srcwait", tt)] = (xsem, toks[("ln1", "xload", tt)])
            emit_ln_tile(tt, xtile[:], toks, xlT, "ln1")
            toks[("ln1", "xfree", tt)] = toks[("ln1", "norm", tt)]
        xtile_free[0] = toks[("ln1", "norm", nt - 1)]
        return [toks[("ln1", "evac", tt)] for tt in range(nt)]

    def emit_ln2(h, x1, x1_toks, xlT2, dn_done):
        toks = {}
        for tt in range(nt):
            toks[("ln2", "srcwait", tt)] = (ch_v, x1_toks[tt])
            emit_ln_tile(tt, x1[:, tt * D:(tt + 1) * D], toks, xlT2, "ln2")
        return [toks[("ln2", "evac", tt)] for tt in range(nt)]

    def emit_gateup(h, which, xlT, s_t, xlT_tok, thr_toks=None):
        toks = {}
        wsrc = wg_h if which == "gate" else wu_h
        nq = (hc + 7) // 8

        for bi, (i0, wd) in enumerate(gblk):
            nchk = wd // 128

            for q in range(nq):
                @blk.gpsimd
                def _(g, bi=bi, i0=i0, wd=wd, q=q):
                    nch = min((q + 1) * 8, hc) - q * 8
                    src = wsrc[:, hc * i0 + q * 8 * wd: hc * i0 + (q * 8 + nch) * wd]
                    toks[("wdma", bi, q)] = dma_w(g, src, nch * wd)

                @blk.tensor
                def _(t, bi=bi, i0=i0, wd=wd, q=q):
                    sl, wv = toks[("wdma", bi, q)]
                    C.wait(t, "t", wsem[sl], wv)
                    nch = min((q + 1) * 8, hc) - q * 8
                    for ci in range(nch):
                        c = q * 8 + ci
                        for tt in range(nt):
                            if c == 0:
                                C.wait(t, "t", ch_s, xlT_tok[tt])
                                sem, valv = pa_free[tt]
                                if sem is not None:
                                    C.wait(t, "t", sem, valv)
                            ins = t.matmul(
                                pA[:, tt * 512:tt * 512 + wd],
                                xlT[:, (c * nt + tt) * 128:(c * nt + tt) * 128 + 128],
                                wslot[sl][:, ci * wd:(ci + 1) * wd],
                                start=(c == 0), stop=(c == hc - 1),
                                skip_group_check=True)
                            if ci == nch - 1 and tt == nt - 1:
                                tok = C.inc(ins, ch_t, 1)
                                slot_freetok[sl] = tok
                                if q == nq - 1:
                                    toks[("mm", bi)] = tok

            if which == "gate":
                @blk.scalar
                def _(s, bi=bi, i0=i0, wd=wd):
                    C.wait(s, "s", ch_t, toks[("mm", bi)])
                    for tt in range(nt):
                        ins = s.activation(s_t[tt][:, i0:i0 + wd],
                                           pA[:, tt * 512:tt * 512 + wd], ACTF.Silu)
                        tok = C.inc(ins, ch_s, 1)
                        pa_free[tt] = (ch_s, tok)
                        toks[("ev", bi, tt)] = tok
                continue

            @blk.scalar
            def _(s, bi=bi, i0=i0, wd=wd):
                for tt in range(nt):
                    prev = toks.get(("mku", bi - 1, tt))
                    if prev:
                        C.wait(s, "s", ch_v, prev)
                    if thr_toks is not None:
                        C.wait(s, "s", ch_v, thr_toks[tt])
                    ins = s.activation(scr16[:, tt * 512:tt * 512 + wd],
                                       s_t[tt][:, i0:i0 + wd], ACTF.Abs)
                    toks[("abs", bi, tt)] = C.inc(ins, ch_s, 1)

            @blk.vector
            def _(v, bi=bi, i0=i0, wd=wd):
                C.wait(v, "v", ch_t, toks[("mm", bi)])
                for tt in range(nt):
                    C.wait(v, "v", ch_s, toks[("abs", bi, tt)])
                    if thr_toks is not None:
                        C.wait(v, "v", ch_v, thr_toks[tt])
                    prevtr = toks.get(("mtr_last", bi - 1, tt))
                    if prevtr:
                        C.wait(v, "v", ch_t, prevtr)
                    am = scr16[:, tt * 512:tt * 512 + wd]
                    mk = scr16[:, 2048 + tt * 512:2048 + tt * 512 + wd]
                    ins = v.scalar_tensor_tensor(mk, am, st("thr", tt),
                                                 s_t[tt][:, i0:i0 + wd],
                                                 op0=ALU.is_gt, op1=ALU.mult)
                    t0 = C.inc(ins, ch_v, 1)
                    C.wait(v, "v", ch_v, t0)
                    ins = v.tensor_mul(mk, mk, pA[:, tt * 512:tt * 512 + wd])
                    tok = C.inc(ins, ch_v, 1)
                    pa_free[tt] = (ch_v, tok)
                    toks[("mku", bi, tt)] = tok

            items = []
            for tt in range(nt):
                mk = scr16[:, 2048 + tt * 512:2048 + tt * 512 + wd]
                for cc in range(nchk):
                    items.append((mk[:, cc * 128:(cc + 1) * 128],
                                  s_t[tt][:, i0 + cc * 128:i0 + (cc + 1) * 128],
                                  [(ch_v, toks[("mku", bi, tt)])]))
            toks[("mev", bi, nt - 1)] = emit_transposes(items)
            for tt in range(nt):
                toks[("mtr_last", bi, tt)] = C.val(ch_t)
        return toks

    NIT_I = 20

    def emit_select(h, s_t, gate_toks, u_sb):
        thr_tok = [None] * nt
        Qh = Q
        for tt in range(nt):
            toksl = {}

            for qq in range(4):
                @blk.scalar
                def _(s, tt=tt, qq=qq):
                    C.wait(s, "s", ch_s, gate_toks[("ev", len(gblk) - 1, tt)])
                    if tt > 0:
                        C.wait(s, "s", ch_v, thr_tok[tt - 1])
                    ins = s.activation(u_sb[:, qq * Qh:(qq + 1) * Qh],
                                       s_t[tt][:, qq * Qh:(qq + 1) * Qh], ACTF.Abs)
                    toksl[("uabs", qq)] = C.inc(ins, ch_s, 1)

            @blk.vector
            def _(v, tt=tt):
                C.wait(v, "v", ch_s, toksl[("uabs", 3)])
                ins = v.reduce_max(st("hi", 0), u_sb[:], axis=AXX)
                C.inc(ins, ch_v, 1)
                v.memset(st("lo", 0), 0.0)
                v.memset(st("clo", 0), 0.0)
                ins = v.memset(st("chi", 0), float(I))
                t0 = C.inc(ins, ch_v, 1)
                C.wait(v, "v", ch_v, t0)

            par = 0
            for it in range(NIT_I):
                @blk.vector
                def _(v, tt=tt, it=it, par=par):
                    lo, hi = st("lo", par), st("hi", par)
                    clo, chi = st("clo", par), st("chi", par)
                    ins = v.tensor_sub(st("wbr"), hi, lo)
                    t0 = C.inc(ins, ch_v, 1)
                    if it == 0:
                        C.wait(v, "v", ch_v, t0)
                        ins = v.tensor_scalar(st("t1"), st("wbr"), 1.0 / 64, None,
                                              op0=ALU.mult)
                        t1 = C.inc(ins, ch_v, 1)
                        C.wait(v, "v", ch_v, t1)
                        ins = v.tensor_add(st("m"), lo, st("t1"))
                        tm = C.inc(ins, ch_v, 1)
                        C.wait(v, "v", ch_v, tm)
                    else:
                        v.tensor_sub(st("d1"), chi, clo)
                        ins = v.tensor_scalar(st("d2"), clo, -1.0, cfg.ksel,
                                              op0=ALU.mult, op1=ALU.add)
                        t1 = C.inc(ins, ch_v, 1)
                        C.wait(v, "v", ch_v, t1)
                        ins = v.reciprocal(st("rr"), st("d1"))
                        t2 = C.inc(ins, ch_v, 1)
                        C.wait(v, "v", ch_v, t2)
                        ins = v.tensor_mul(st("frac"), st("d2"), st("rr"))
                        t3 = C.inc(ins, ch_v, 1)
                        C.wait(v, "v", ch_v, t3)
                        ins = v.tensor_scalar_min(st("frac"), st("frac"), 63.0 / 64)
                        t3b = C.inc(ins, ch_v, 1)
                        C.wait(v, "v", ch_v, t3b)
                        ins = v.tensor_scalar_max(st("frac"), st("frac"), 1.0 / 64)
                        t4 = C.inc(ins, ch_v, 1)
                        C.wait(v, "v", ch_v, t4)
                        ins = v.tensor_mul(st("t1"), st("wbr"), st("frac"))
                        t5 = C.inc(ins, ch_v, 1)
                        C.wait(v, "v", ch_v, t5)
                        ins = v.tensor_add(st("m"), lo, st("t1"))
                        tm = C.inc(ins, ch_v, 1)
                        C.wait(v, "v", ch_v, tm)
                    toksl[("m", it)] = C.val(ch_v)
                    dmp = scr16[:, 4096:4096 + Qh]
                    for qq in range(2):
                        ins = v.tensor_scalar(dmp, u_sb[:, qq * Qh:(qq + 1) * Qh],
                                              st("m"), None,
                                              op0=ALU.is_le, op1=ALU.add,
                                              accum_out=st("cq1", qq))
                        t6 = C.inc(ins, ch_v, 1)
                        C.wait(v, "v", ch_v, t6)

                @blk.scalar
                def _(s, tt=tt, it=it):
                    C.wait(s, "s", ch_v, toksl[("m", it)])
                    dmp2 = scr16[:, 0:Qh]
                    for qq in (2, 3):
                        ins = s.activation(dmp2, u_sb[:, qq * Qh:(qq + 1) * Qh],
                                           ACTF.Sign, bias=st("m"), scale=-1.0,
                                           accum_out=st("sg", qq - 2))
                        tk = C.inc(ins, ch_s, 1)
                        C.wait(s, "s", ch_s, tk)
                    toksl[("actc", it)] = C.val(ch_s)

                @blk.vector
                def _(v, tt=tt, it=it, par=par):
                    lo, hi = st("lo", par), st("hi", par)
                    clo, chi = st("clo", par), st("chi", par)
                    nlo, nhi = st("lo", 1 - par), st("hi", 1 - par)
                    nclo, nchi = st("clo", 1 - par), st("chi", 1 - par)
                    C.wait(v, "v", ch_s, toksl[("actc", it)])
                    v.tensor_add(st("t1"), st("cq1", 0), st("cq1", 1))
                    ins = v.tensor_add(st("t2"), st("sg", 0), st("sg", 1))
                    t0 = C.inc(ins, ch_v, 1)
                    C.wait(v, "v", ch_v, t0)
                    ins = v.tensor_scalar(st("t2"), st("t2"), 0.5, float(Qh),
                                          op0=ALU.mult, op1=ALU.add)
                    t1 = C.inc(ins, ch_v, 1)
                    C.wait(v, "v", ch_v, t1)
                    ins = v.tensor_add(st("cnt"), st("t1"), st("t2"))
                    t2 = C.inc(ins, ch_v, 1)
                    C.wait(v, "v", ch_v, t2)
                    v.tensor_scalar(st("upd"), st("cnt"), cfg.ksel, None,
                                    op0=ALU.is_ge)
                    ins = v.tensor_scalar(st("invu"), st("cnt"), cfg.ksel, None,
                                          op0=ALU.is_lt)
                    t3 = C.inc(ins, ch_v, 1)
                    C.wait(v, "v", ch_v, t3)
                    v.tensor_mul(st("t1"), st("upd"), st("m"))
                    v.tensor_mul(st("t2"), st("invu"), hi)
                    v.tensor_mul(st("t3"), st("upd"), lo)
                    ins = v.tensor_mul(st("d1"), st("invu"), st("m"))
                    t4 = C.inc(ins, ch_v, 1)
                    C.wait(v, "v", ch_v, t4)
                    v.tensor_add(nhi, st("t1"), st("t2"))
                    ins = v.tensor_add(nlo, st("t3"), st("d1"))
                    t5 = C.inc(ins, ch_v, 1)
                    C.wait(v, "v", ch_v, t5)
                    v.tensor_mul(st("t1"), st("upd"), st("cnt"))
                    v.tensor_mul(st("t2"), st("invu"), chi)
                    v.tensor_mul(st("t3"), st("upd"), clo)
                    ins = v.tensor_mul(st("d1"), st("invu"), st("cnt"))
                    t6 = C.inc(ins, ch_v, 1)
                    C.wait(v, "v", ch_v, t6)
                    v.tensor_add(nchi, st("t1"), st("t2"))
                    ins = v.tensor_add(nclo, st("t3"), st("d1"))
                    t7 = C.inc(ins, ch_v, 1)
                    C.wait(v, "v", ch_v, t7)
                par = 1 - par

            @blk.vector
            def _(v, tt=tt, par=par):
                hi = st("hi", par)
                dmp = scr16[:, 4096:4096 + Qh]
                for qq in range(4):
                    ins = v.tensor_scalar(dmp, u_sb[:, qq * Qh:(qq + 1) * Qh],
                                          hi, None, op0=ALU.is_le, op1=ALU.add,
                                          accum_out=st("cq1", qq))
                    t0 = C.inc(ins, ch_v, 1)
                    C.wait(v, "v", ch_v, t0)
                ins = v.reduce_sum(st("chi", par), stat[:, SC["cq1"]:SC["cq1"] + 4],
                                   axis=AXX)
                t1 = C.inc(ins, ch_v, 1)
                C.wait(v, "v", ch_v, t1)

            for qq in range(4):
                @blk.vector
                def _(v, tt=tt, qq=qq, par=par):
                    if qq > 0:
                        C.wait(v, "v", ch_v, toksl[("m8", qq - 1)])
                    dmp = scr16[:, 4096:4096 + Qh]
                    uq = u_sb[:, qq * Qh:(qq + 1) * Qh]
                    ins = v.scalar_tensor_tensor(dmp, uq, st("hi", par), uq,
                                                 op0=ALU.is_le, op1=ALU.mult)
                    t0 = C.inc(ins, ch_v, 1)
                    C.wait(v, "v", ch_v, t0)
                    ins = v.max(out=stat[:, SC["t8q"] + qq * 8:SC["t8q"] + qq * 8 + 8],
                                in_=dmp)
                    toksl[("m8", qq)] = C.inc(ins, ch_v, 1)

            @blk.vector
            def _(v, tt=tt, par=par):
                C.wait(v, "v", ch_v, toksl[("m8", 3)])
                ins = v.max(out=stat[:, SC["t8g"]:SC["t8g"] + 8],
                            in_=stat[:, SC["t8q"]:SC["t8q"] + 32])
                t1 = C.inc(ins, ch_v, 1)
                C.wait(v, "v", ch_v, t1)
                ins = v.tensor_scalar(st("cidx"), st("chi", par), cfg.ksel, None,
                                      op0=ALU.subtract)
                tci = C.inc(ins, ch_v, 1)
                C.wait(v, "v", ch_v, tci)
                ins = v.tensor_scalar_min(st("cidx"), st("cidx"), 7.0)
                t2 = C.inc(ins, ch_v, 1)
                C.wait(v, "v", ch_v, t2)
                C.wait(v, "v", ch_g, iota_tok[0])
                ins = v.scalar_tensor_tensor(stat[:, SC["t8q"]:SC["t8q"] + 8],
                                             iota8[:], st("cidx"),
                                             stat[:, SC["t8g"]:SC["t8g"] + 8],
                                             op0=ALU.is_equal, op1=ALU.mult,
                                             accum_out=st("thr", tt))
                thr_tok[tt] = C.inc(ins, ch_v, 1)
        return thr_tok

    def emit_down(h, s_t, up_toks, attnx1):
        toks = {}
        ngrp = (ich + 7) // 8

        for ob in range(nob):
            for gi in range(ngrp):
                @blk.gpsimd
                def _(g, ob=ob, gi=gi):
                    ic0 = gi * 8
                    ncg = min(8, ich - ic0)
                    src = wd_h[:, (ob * ich + ic0) * 512:(ob * ich + ic0 + ncg) * 512]
                    toks[("wdma", ob, gi)] = dma_w(g, src, ncg * 512)

                @blk.tensor
                def _(t, ob=ob, gi=gi):
                    C.wait(t, "t", ch_s, up_toks[("mev", len(gblk) - 1, nt - 1)])
                    sl, wv = toks[("wdma", ob, gi)]
                    C.wait(t, "t", wsem[sl], wv)
                    ic0 = gi * 8
                    ncg = min(8, ich - ic0)
                    for ci in range(ncg):
                        ic = ic0 + ci
                        for tt in range(nt):
                            if ic == 0:
                                sem, valv = pa_free[tt]
                                if sem is not None:
                                    C.wait(t, "t", sem, valv)
                            ins = t.matmul(
                                pA[:, tt * 512:tt * 512 + 512],
                                s_t[tt][:, ic * 128:(ic + 1) * 128],
                                wslot[sl][:, ci * 512:(ci + 1) * 512],
                                start=(ic == 0), stop=(ic == ich - 1),
                                skip_group_check=True)
                            if ci == ncg - 1 and tt == nt - 1:
                                tok = C.inc(ins, ch_t, 1)
                                slot_freetok[sl] = tok
                                if gi == ngrp - 1:
                                    toks[("mm", ob)] = tok

            @blk.scalar
            def _(s, ob=ob):
                C.wait(s, "s", ch_t, toks[("mm", ob)])
                for tt in range(nt):
                    dst = attnx1[:, tt * D + ob * 512:tt * D + (ob + 1) * 512]
                    ins = s.activation(dst, pA[:, tt * 512:tt * 512 + 512],
                                       ACTF.Identity,
                                       accum_out=st("asum", tt * 8 + ob))
                    t0 = C.inc(ins, ch_s, 1)
                    C.wait(s, "s", ch_s, t0)
                    ins = s.activation(scr16[:, 4096:4096 + 512],
                                       pA[:, tt * 512:tt * 512 + 512],
                                       ACTF.Square, accum_out=st("asq", tt * 8 + ob))
                    tok = C.inc(ins, ch_s, 1)
                    pa_free[tt] = (ch_s, tok)
                    toks[("ev", ob, tt)] = tok

        x1_toks = [None] * nt
        for tt in range(nt):
            @blk.sync
            def _(sy, tt=tt):
                C.wait(sy, "y", ch_v, xtile_free[0])
                rows = h * cfg.hb + tt * 128
                ins = sy.dma_start(out=xtile[:], in_=x_h[rows:rows + 128, :])
                toks[("xre", tt)] = C.inc(ins, xsem, 16)

            @blk.vector
            def _(v, tt=tt):
                C.wait(v, "v", ch_s, toks[("ev", nob - 1, tt)])
                v.reduce_sum(st("t1"),
                             stat[:, SC["asum"] + tt * 8:SC["asum"] + tt * 8 + nob],
                             axis=AXX)
                ins = v.reduce_sum(st("t2"),
                                   stat[:, SC["asq"] + tt * 8:SC["asq"] + tt * 8 + nob],
                                   axis=AXX)
                t0 = C.inc(ins, ch_v, 1)
                C.wait(v, "v", ch_v, t0)
                v.tensor_scalar(st("am2", tt), st("t1"), 1.0 / D, None, op0=ALU.mult)
                ins = v.tensor_scalar(st("t2"), st("t2"), 1.0 / D, None, op0=ALU.mult)
                t1 = C.inc(ins, ch_v, 1)
                C.wait(v, "v", ch_v, t1)
                ins = v.tensor_mul(st("t3"), st("am2", tt), st("am2", tt))
                tv = C.inc(ins, ch_v, 1)
                C.wait(v, "v", ch_v, tv)
                ins = v.tensor_sub(st("t2"), st("t2"), st("t3"))
                toks[("var", tt)] = C.inc(ins, ch_v, 1)

            @blk.scalar
            def _(s, tt=tt):
                C.wait(s, "s", ch_v, toks[("var", tt)])
                ins = s.activation(st("t2"), st("t2"), ACTF.Sqrt, bias=epsv[:])
                toks[("sqt", tt)] = C.inc(ins, ch_s, 1)

            @blk.vector
            def _(v, tt=tt):
                C.wait(v, "v", ch_s, toks[("sqt", tt)])
                ins = v.reciprocal(st("ar2", tt), st("t2"))
                t0 = C.inc(ins, ch_v, 1)
                C.wait(v, "v", ch_v, t0)
                sl = attnx1[:, tt * D:(tt + 1) * D]
                ins = v.tensor_scalar(sl, sl, st("am2", tt), st("ar2", tt),
                                      op0=ALU.subtract, op1=ALU.mult)
                t1 = C.inc(ins, ch_v, 1)
                C.wait(v, "v", ch_v, t1)
                C.wait(v, "v", xsem, toks[("xre", tt)])
                ins = v.tensor_add(sl, sl, xtile[:])
                tok = C.inc(ins, ch_v, 1)
                xtile_free[0] = tok
                x1_toks[tt] = tok
        return x1_toks, C.val(ch_t)

    def emit_ffn(h, x1, x1_toks, xlT2, ffn_sb, qk_sb, hg_sb, wbm_sb, dn_done):
        toks = {}
        ln2_tok = emit_ln2(h, x1, x1_toks, xlT2, dn_done)
        nq = (hc + 7) // 8

        @blk.gpsimd
        def _(g):
            C.wait(g, "g", ch_t, dn_done)
            ins = g.dma_start(out=wbm_sb[:], in_=wbm_h[:])
            msem_n[0] += 1
            toks["wbm"] = C.inc(ins, msem, 16)

        nqk = max(1, (hc * EDK * 2) // 4096)
        cpq = 4096 // (EDK * 2)
        for q in range(nqk):
            @blk.gpsimd
            def _(g, q=q):
                w = min(4096, hc * EDK * 2 - q * 4096)
                toks[("qkdma", q)] = dma_w(g, wqk_h[:, q * 4096:q * 4096 + w], w)

            @blk.tensor
            def _(t, q=q):
                sl, wv = toks[("qkdma", q)]
                C.wait(t, "t", wsem[sl], wv)
                for ci in range(cpq):
                    c = q * cpq + ci
                    if c >= hc:
                        break
                    for tt in range(nt):
                        if c == 0:
                            C.wait(t, "t", ch_s, ln2_tok[tt])
                            sem, valv = pa_free[tt]
                            if sem is not None:
                                C.wait(t, "t", sem, valv)
                        ins = t.matmul(
                            pA[:, tt * 512:tt * 512 + EDK * 2],
                            xlT2[:, (c * nt + tt) * 128:(c * nt + tt) * 128 + 128],
                            wslot[sl][:, ci * EDK * 2:(ci + 1) * EDK * 2],
                            start=(c == 0), stop=(c == hc - 1),
                            skip_group_check=True)
                        if (ci == cpq - 1 or c == hc - 1) and tt == nt - 1:
                            tok = C.inc(ins, ch_t, 1)
                            slot_freetok[sl] = tok
                            if q == nqk - 1:
                                toks["qkmm"] = tok

        for tt in range(nt):
            @blk.scalar
            def _(s, tt=tt):
                C.wait(s, "s", ch_t, toks["qkmm"])
                ins = s.copy(qk_sb[:, tt * 512:(tt + 1) * 512],
                             pA[:, tt * 512:tt * 512 + 512])
                tok = C.inc(ins, ch_s, 1)
                pa_free[tt] = (ch_s, tok)
                toks[("qkev", tt)] = tok

        na = max(1, (hc * ER) // 4096)
        cpa = min(4096 // ER, hc)
        for q in range(na):
            @blk.gpsimd
            def _(g, q=q):
                w = min(4096, hc * ER - q * 4096)
                toks[("adma", q)] = dma_w(g, wa_h[:, q * 4096:q * 4096 + w], w)

            @blk.tensor
            def _(t, q=q):
                sl, wv = toks[("adma", q)]
                C.wait(t, "t", wsem[sl], wv)
                for ci in range(cpa):
                    c = q * cpa + ci
                    if c >= hc:
                        break
                    for tt in range(nt):
                        if c == 0:
                            sem, valv = pa_free[tt]
                            if sem is not None:
                                C.wait(t, "t", sem, valv)
                        ins = t.matmul(
                            pA[:, tt * 512:tt * 512 + ER],
                            xlT2[:, (c * nt + tt) * 128:(c * nt + tt) * 128 + 128],
                            wslot[sl][:, ci * ER:(ci + 1) * ER],
                            start=(c == 0), stop=(c == hc - 1),
                            skip_group_check=True)
                        if (ci == cpa - 1 or c == hc - 1) and tt == nt - 1:
                            tok = C.inc(ins, ch_t, 1)
                            slot_freetok[sl] = tok
                            if q == na - 1:
                                toks["hmm"] = tok

        gT_toks = [None] * nt
        for tt in range(nt):
            @blk.scalar
            def _(s, tt=tt):
                C.wait(s, "s", ch_t, toks["hmm"])
                ins = s.copy(hg_sb[:, tt * 128:(tt + 1) * 128],
                             pA[:, tt * 512:tt * 512 + ER])
                tok = C.inc(ins, ch_s, 1)
                pa_free[tt] = (ch_s, tok)
                toks[("hev", tt)] = tok

            @blk.vector
            def _(v, tt=tt):
                C.wait(v, "v", ch_s, toks[("qkev", tt)])
                prod = scr16[:, 6848:6848 + EDK]
                ins = v.tensor_mul(prod, qk_sb[:, tt * 512:tt * 512 + EDK],
                                   qk_sb[:, tt * 512 + EDK:tt * 512 + 2 * EDK])
                t0 = C.inc(ins, ch_v, 1)
                C.wait(v, "v", ch_v, t0)
                ins = v.reduce_sum(stat[:, SC["t8g"]:SC["t8g"] + 8],
                                   prod.rearrange("p (e d) -> p e d", e=8), axis=AXX)
                t1 = C.inc(ins, ch_v, 1)
                C.wait(v, "v", ch_v, t1)
                ins = v.tensor_scalar(stat[:, SC["t8g"]:SC["t8g"] + 8],
                                      stat[:, SC["t8g"]:SC["t8g"] + 8],
                                      1.0 / float(np.sqrt(DK_)), None, op0=ALU.mult)
                t2 = C.inc(ins, ch_v, 1)
                C.wait(v, "v", ch_v, t2)
                ins = v.reduce_max(st("neg8"), stat[:, SC["t8g"]:SC["t8g"] + 8],
                                   axis=AXX, negate=True)
                toks[("sneg", tt)] = C.inc(ins, ch_v, 1)

            @blk.scalar
            def _(s, tt=tt):
                C.wait(s, "s", ch_v, toks[("sneg", tt)])
                ins = s.activation(stat[:, SC["t8q"]:SC["t8q"] + 8],
                                   stat[:, SC["t8g"]:SC["t8g"] + 8], ACTF.Exp,
                                   bias=st("neg8"), accum_out=st("sexp"))
                toks[("exp", tt)] = C.inc(ins, ch_s, 1)

            @blk.vector
            def _(v, tt=tt):
                C.wait(v, "v", ch_s, toks[("exp", tt)])
                ins = v.reciprocal(st("winv"), st("sexp"))
                t0 = C.inc(ins, ch_v, 1)
                C.wait(v, "v", ch_v, t0)
                ins = v.tensor_scalar(stat[:, SC["t8q"]:SC["t8q"] + 8],
                                      stat[:, SC["t8q"]:SC["t8q"] + 8],
                                      st("winv"), None, op0=ALU.mult)
                t1 = C.inc(ins, ch_v, 1)
                C.wait(v, "v", ch_v, t1)
                C.wait(v, "v", ch_s, toks[("hev", tt)])
                gdst = hg_sb[:, nt * 128 + tt * 128:nt * 128 + (tt + 1) * 128]
                ins = v.tensor_mul(
                    gdst.rearrange("p (e r) -> p e r", e=8),
                    hg_sb[:, tt * 128:(tt + 1) * 128].rearrange(
                        "p (e r) -> p e r", e=8),
                    stat[:, SC["t8q"]:SC["t8q"] + 8]
                    .rearrange("p (e o) -> p e o", o=1).to_broadcast([128, 8, R_]))
                toks[("g", tt)] = C.inc(ins, ch_v, 1)

            gdst = hg_sb[:, nt * 128 + tt * 128:nt * 128 + (tt + 1) * 128]
            gt = hg_sb[:, 2 * nt * 128 + tt * 128:2 * nt * 128 + (tt + 1) * 128]
            gT_toks[tt] = emit_transposes(
                [(gdst, gt, [(ch_v, toks[("g", tt)])])])

        for ob in range(nob):
            for q in range(nq):
                @blk.gpsimd
                def _(g, ob=ob, q=q):
                    nch = min((q + 1) * 8, hc) - q * 8
                    src = wW_h[:, hc * ob * 512 + q * 8 * 512:
                               hc * ob * 512 + (q * 8 + nch) * 512]
                    toks[("Wdma", ob, q)] = dma_w(g, src, nch * 512)

                @blk.tensor
                def _(t, ob=ob, q=q):
                    C.wait(t, "t", msem, toks["wbm"])
                    C.wait(t, "t", msem, 16)
                    C.wait(t, "t", ch_v, init_tok[0])
                    for tt in range(nt):
                        C.wait(t, "t", ch_s, gT_toks[tt])
                    sl, wv = toks[("Wdma", ob, q)]
                    C.wait(t, "t", wsem[sl], wv)
                    nch = min((q + 1) * 8, hc) - q * 8
                    for ci in range(nch):
                        c = q * 8 + ci
                        for tt in range(nt):
                            if c == 0:
                                sem, valv = pa_free[tt]
                                if sem is not None:
                                    C.wait(t, "t", sem, valv)
                            ins = t.matmul(
                                pA[:, tt * 512:tt * 512 + 512],
                                xlT2[:, (c * nt + tt) * 128:(c * nt + tt) * 128 + 128],
                                wslot[sl][:, ci * 512:(ci + 1) * 512],
                                start=(c == 0), stop=False,
                                skip_group_check=True)
                            if ci == nch - 1 and tt == nt - 1:
                                tok = C.inc(ins, ch_t, 1)
                                slot_freetok[sl] = tok

            @blk.tensor
            def _(t, ob=ob):
                for tt in range(nt):
                    t.matmul(pA[:, tt * 512:tt * 512 + 512], ones1[:],
                             b_sb[:, ob * 512:(ob + 1) * 512],
                             start=False, stop=False, skip_group_check=True)
                    gt = hg_sb[:, 2 * nt * 128 + tt * 128:
                               2 * nt * 128 + (tt + 1) * 128]
                    ins = t.matmul(pA[:, tt * 512:tt * 512 + 512], gt,
                                   wbm_sb[:, ob * 512:(ob + 1) * 512],
                                   start=False, stop=True, skip_group_check=True)
                    if tt == nt - 1:
                        toks[("Wmm", ob)] = C.inc(ins, ch_t, 1)

            @blk.scalar
            def _(s, ob=ob):
                C.wait(s, "s", ch_t, toks[("Wmm", ob)])
                for tt in range(nt):
                    dst = ffn_sb[:, tt * D + ob * 512:tt * D + (ob + 1) * 512]
                    ins = s.activation(dst, pA[:, tt * 512:tt * 512 + 512],
                                       ACTF.Identity,
                                       accum_out=st("fsum", tt * 8 + ob))
                    t0 = C.inc(ins, ch_s, 1)
                    C.wait(s, "s", ch_s, t0)
                    ins = s.activation(scr16[:, 4096:4096 + 512],
                                       pA[:, tt * 512:tt * 512 + 512],
                                       ACTF.Square, accum_out=st("fsq", tt * 8 + ob))
                    tok = C.inc(ins, ch_s, 1)
                    pa_free[tt] = (ch_s, tok)
                    toks[("fev", ob, tt)] = tok

        for tt in range(nt):
            @blk.vector
            def _(v, tt=tt):
                C.wait(v, "v", ch_s, toks[("fev", nob - 1, tt)])
                v.reduce_sum(st("t1"),
                             stat[:, SC["fsum"] + tt * 8:SC["fsum"] + tt * 8 + nob],
                             axis=AXX)
                ins = v.reduce_sum(st("t2"),
                                   stat[:, SC["fsq"] + tt * 8:SC["fsq"] + tt * 8 + nob],
                                   axis=AXX)
                t0 = C.inc(ins, ch_v, 1)
                C.wait(v, "v", ch_v, t0)
                v.tensor_scalar(st("fm", tt), st("t1"), 1.0 / D, None, op0=ALU.mult)
                ins = v.tensor_scalar(st("t2"), st("t2"), 1.0 / D, None, op0=ALU.mult)
                t1 = C.inc(ins, ch_v, 1)
                C.wait(v, "v", ch_v, t1)
                ins = v.tensor_mul(st("t3"), st("fm", tt), st("fm", tt))
                tv = C.inc(ins, ch_v, 1)
                C.wait(v, "v", ch_v, tv)
                ins = v.tensor_sub(st("t2"), st("t2"), st("t3"))
                toks[("fvar", tt)] = C.inc(ins, ch_v, 1)

            @blk.scalar
            def _(s, tt=tt):
                C.wait(s, "s", ch_v, toks[("fvar", tt)])
                ins = s.activation(st("t2"), st("t2"), ACTF.Sqrt, bias=epsv[:])
                toks[("fsqt", tt)] = C.inc(ins, ch_s, 1)

            @blk.vector
            def _(v, tt=tt):
                C.wait(v, "v", ch_s, toks[("fsqt", tt)])
                ins = v.reciprocal(st("fr", tt), st("t2"))
                t0 = C.inc(ins, ch_v, 1)
                C.wait(v, "v", ch_v, t0)
                C.wait(v, "v", ch_v, xtile_free[0])
                if out_dmas[0] > 0:
                    C.wait(v, "v", osem, out_dmas[0] * 16)
                ins = v.tensor_scalar(xtile[:], ffn_sb[:, tt * D:(tt + 1) * D],
                                      st("fm", tt), st("fr", tt),
                                      op0=ALU.subtract, op1=ALU.mult)
                t1 = C.inc(ins, ch_v, 1)
                C.wait(v, "v", ch_v, t1)
                ins = v.tensor_add(xtile[:], xtile[:], x1[:, tt * D:(tt + 1) * D])
                tok = C.inc(ins, ch_v, 1)
                xtile_free[0] = tok
                toks[("out", tt)] = tok

            @blk.sync
            def _(sy, tt=tt):
                C.wait(sy, "y", ch_v, toks[("out", tt)])
                rows = h * cfg.hb + tt * 128
                ins = sy.dma_start(out=out_h[rows:rows + 128, :], in_=xtile[:])
                out_dmas[0] += 1
                C.inc(ins, osem, 16)

    init_phase()

    for h in range(nh):
        if debug_x1 and h > 0:
            for sec, eng_name in ((blk.scalar, "s"), (blk.vector, "v"),
                                  (blk.tensor, "t"), (blk.gpsimd, "g"),
                                  (blk.sync, "y")):
                def _bar(e, eng_name=eng_name):
                    C.wait(e, eng_name, msem, msem_n[0] * 16)
                sec(_bar)
        xlT_cm = nc.sbuf_tensor("xlT%d" % h, [128, hc * nt * 128], F16, side="right")
        xlT = xlT_cm.__enter__()
        s_cm = [nc.sbuf_tensor(f"s{h}_{tt}", [128, I], F16) for tt in range(nt)]
        s_t = [cm.__enter__() for cm in s_cm]

        u_cm = nc.sbuf_tensor("u%d" % h, [128, I], F16, side="right")
        u_sb = u_cm.__enter__()

        ln1_tok = emit_ln1(h, xlT)
        gate_toks = emit_gateup(h, "gate", xlT, s_t, ln1_tok)
        thr_toks = emit_select(h, s_t, gate_toks, u_sb)
        up_toks = emit_gateup(h, "up", xlT, s_t, ln1_tok, thr_toks=thr_toks)

        u_cm.__exit__(None, None, None)
        xlT_cm.__exit__(None, None, None)
        attnx1_cm = nc.sbuf_tensor("ax1_%d" % h, [128, nt * D], F16, side="right")
        attnx1 = attnx1_cm.__enter__()

        x1_toks, dn_done = emit_down(h, s_t, up_toks, attnx1)

        if debug_x1:
            @blk.gpsimd
            def _(g, h=h, attnx1=attnx1, x1_toks=x1_toks):
                if h == 0:
                    ins = g.dma_start(out=dbg4_h[:, 72:72 + nt], in_=stat[:, 72:72 + nt])
                    msem_n[0] += 1
                    C.inc(ins, msem, 16)
                for tt in range(nt):
                    C.wait(g, "g", ch_v, x1_toks[tt])
                    rows = h * cfg.hb + tt * 128
                    ins = g.dma_start(out=dbg_h[rows:rows + 128, :],
                                      in_=attnx1[:, tt * D:(tt + 1) * D])
                    msem_n[0] += 1
                    C.inc(ins, msem, 16)

        for cm in reversed(s_cm):
            cm.__exit__(None, None, None)

        xlT2_cm = nc.sbuf_tensor("xlT2_%d" % h, [128, hc * nt * 128], F16)
        xlT2 = xlT2_cm.__enter__()
        ffn_cm = nc.sbuf_tensor("ffn_%d" % h, [128, nt * D], F16)
        ffn_sb = ffn_cm.__enter__()
        qk_cm = nc.sbuf_tensor("qk_%d" % h, [128, nt * 512], F16)
        qk_sb = qk_cm.__enter__()
        hg_cm = nc.sbuf_tensor("hg_%d" % h, [128, 3 * nt * 128], F16)
        hg_sb = hg_cm.__enter__()
        wbm_cm = nc.sbuf_tensor("wbm_%d" % h, [128, D], F16)
        wbm_sb = wbm_cm.__enter__()

        emit_ffn(h, attnx1, x1_toks, xlT2, ffn_sb, qk_sb, hg_sb, wbm_sb, dn_done)

        if debug_x1:
            @blk.gpsimd
            def _(g, h=h, ffn_sb=ffn_sb, qk_sb=qk_sb, hg_sb=hg_sb, xlT2=xlT2):
                C.wait(g, "g", ch_v, C.val(ch_v))
                C.wait(g, "g", ch_s, C.val(ch_s))
                for tt in range(nt):
                    rows = h * cfg.hb + tt * 128
                    ins = g.dma_start(out=dbg2_h[rows:rows + 128, :],
                                      in_=ffn_sb[:, tt * D:(tt + 1) * D])
                    msem_n[0] += 1
                    C.inc(ins, msem, 16)
                if h == nh - 1:
                    ins = g.dma_start(out=dbg3_h[0:128, 0:nt * 512], in_=qk_sb[:])
                    msem_n[0] += 1
                    C.inc(ins, msem, 16)
                    ins = g.dma_start(out=dbg3_h[128:256, 0:3 * nt * 128],
                                      in_=hg_sb[:])
                    msem_n[0] += 1
                    C.inc(ins, msem, 16)
                    ins = g.dma_start(out=dbg3_h[256:384, 0:hc * nt * 128],
                                      in_=xlT2[:])
                    msem_n[0] += 1
                    C.inc(ins, msem, 16)

        for cm in (wbm_cm, hg_cm, qk_cm, ffn_cm, xlT2_cm):
            cm.__exit__(None, None, None)
        attnx1_cm.__exit__(None, None, None)

    @blk.sync
    def _(sy):
        sy.wait_ge(osem, out_dmas[0] * 16)
        sy.wait_ge(msem, msem_n[0] * 16)

    for cm in reversed(ctxs):
        cm.__exit__(None, None, None)
    return nc


def kernel(L0_attn_x, L0_attn_w_gate, L0_attn_w_up, L0_attn_w_down,
           L0_ffn_W, L0_ffn_b, L0_ffn_Wq, L0_ffn_Wk, L0_ffn_A, L0_ffn_Bm):
    cfg = CFG
    x, lay = host_layouts(L0_attn_x, L0_attn_w_gate, L0_attn_w_up, L0_attn_w_down,
                          L0_ffn_W, L0_ffn_b, L0_ffn_Wq, L0_ffn_Wk,
                          L0_ffn_A, L0_ffn_Bm, cfg)
    nc = build_nc(cfg)
    in_maps = []
    for c in range(NCORES):
        m = {"x": np.ascontiguousarray(x[c * cfg.tokc:(c + 1) * cfg.tokc])}
        m.update(lay)
        in_maps.append(m)
    res = run_bass_kernel_spmd(nc, in_maps, core_ids=list(range(NCORES)))
    out = np.concatenate([np.asarray(res.results[c]["out"]) for c in range(NCORES)], 0)
    return np.ascontiguousarray(out.reshape(2, 4096, 4096).astype(np.float32))
